# version 10
# baseline (speedup 1.0000x reference)
import contextlib
import numpy as np
import ml_dtypes
import concourse.bass as bass
import concourse.mybir as mybir
from concourse.bass_utils import run_bass_kernel_spmd

F32 = mybir.dt.float32
BF16 = mybir.dt.bfloat16
AF = mybir.ActivationFunctionType
ALU = mybir.AluOpType
AX = mybir.AxisListType.X
BF = ml_dtypes.bfloat16

NCORES = 8
SEQ = 2048
D = 1024
TPC = 2 * SEQ
TT = 512
NTILES = TPC // TT
DH = 4096
EPS = 1e-5
DEBUG_TAPS = False


def _reg_sem(nc, name, persist=False):
    attr = "_sem_persist" if persist else "_sem_registry"
    reg = getattr(nc, attr, None)
    if reg is None:
        reg = {}
        setattr(nc, attr, reg)
    if name not in reg:
        reg[name] = nc.alloc_semaphore(name)
    return reg[name]


def _emit_sem_reset(nc):
    with nc.Block() as rb:
        @rb.sync
        def _(sp):
            for h in nc._sem_registry.values():
                sp.sem_clear(h)


class S:
    def __init__(self, nc, name, persist=False):
        self.h = _reg_sem(nc, name, persist)
        self.n = 0

    def inc(self, instr, k=1):
        instr.then_inc(self.h, k)
        self.n += k
        return instr


def _sems(nc, names):
    return {nm: S(nc, nm) for nm in names}


class Chain:

    def __init__(self, eng, util_sem):
        self.e = eng
        self._u = util_sem
        self._last = None

    def wait_ge(self, h, v):
        self.e.wait_ge(h, v)

    def op(self, name, *a, inc=None, k=1, **kw):
        if self._last is not None:
            self.e.wait_ge(*self._last)
        instr = getattr(self.e, name)(*a, **kw)
        if inc is None:
            self._u.inc(instr, 1)
            self._last = (self._u.h, self._u.n)
        else:
            inc.inc(instr, k)
            self._last = (inc.h, inc.n)
        return instr



FFN0_KEYS = ["w1T", "w2T", "h1T", "h2T", "w3T", "s1c", "s2c", "sh1c", "sh2c",
             "s3c", "identb", "identf"]


def emit_ffn0(nc, pfx, x_in, x_out, wd, wscr=None, first=True):
    _st = contextlib.ExitStack()
    sb = lambda n, s, d: _st.enter_context(nc.sbuf_tensor(pfx + n, s, d))
    pst = lambda n, s, d: _st.enter_context(nc.psum_tensor(pfx + n, s, d))
    w1T = wd["w1T"]; w2T = wd["w2T"]; h1T = wd["h1T"]; h2T = wd["h2T"]
    w3T = wd["w3T"]; s1d = wd["s1c"]; s2d = wd["s2c"]; sh1d = wd["sh1c"]
    sh2d = wd["sh2c"]; s3d = wd["s3c"]; idb_d = wd["identb"]; idf_d = wd["identf"]

    x_in_t = x_in.rearrange("(t g p) f -> t p g f", g=4, p=128)
    x_out_t = x_out.rearrange("(t g p) f -> t p g f", g=4, p=128)
    r_w1 = w1T.rearrange("(a p) m -> p a m", p=128)
    r_w2 = w2T.rearrange("(a p) m -> p a m", p=128)
    r_h1 = h1T.rearrange("(a p) m -> p a m", p=128)
    r_h2 = h2T.rearrange("(a p) m -> p a m", p=128)
    r_w3 = w3T.rearrange("(a p) m -> p a m", p=128)

    xt = [sb(f"xt{q}", [128, 4 * D], F32) for q in range(2)]
    xo = sb("xo", [128, 4 * D], F32)
    zt = sb("zt", [128, 4 * D], BF16)
    zf = sb("zf", [128, 8 * TT], BF16)
    sq = sb("sq", [128, D], F32)
    ha = sb("ha", [128, 32 * TT], BF16)
    hb = sb("hb", [128, 32 * TT], BF16)
    yf = sb("yf", [128, 8 * TT], F32)
    wbuf = [sb(f"wbuf{q}", [128, 8 * 512], BF16) for q in range(4)]
    t1 = [sb(f"t1{q}", [128, TT], BF16) for q in range(2)]
    t2 = [sb(f"t2{q}", [128, TT], BF16) for q in range(2)]
    s1c = sb("s1cs", [128, 32], F32)
    s2c = sb("s2cs", [128, 32], F32)
    sh1c = sb("sh1cs", [128, 32], F32)
    sh2c = sb("sh2cs", [128, 32], F32)
    s3c = sb("s3cs", [128, 8], F32)
    idb = sb("idb", [128, 128], BF16)
    idf = sb("idf", [128, 128], F32)
    epsb = sb("epsb", [128, 1], F32)
    sums = sb("sums", [128, 4], F32)
    sumsq = sb("sumsq", [128, 4], F32)
    mu = sb("mu", [128, 4], F32)
    ex2 = sb("ex2", [128, 4], F32)
    musq = sb("musq", [128, 4], F32)
    var = sb("var", [128, 4], F32)
    stdv = sb("stdv", [128, 4], F32)
    istd = sb("istd", [128, 4], F32)

    psA = [pst(f"psA{q}", [128, TT], F32) for q in range(4)]
    psW = pst("psW", [128, D], F32)
    psT = [pst(f"psT{q}", [128, 1024], BF16) for q in range(2)]

    sm = _sems(nc, ["sst", "sKG", "sln", "stp", "szf", "sA1", "sA2",
                    "sE1", "sE2", "sPr", "sM1", "sH1", "sM2", "sH2", "sM3",
                    "sYf", "sYT", "sRes", "sVr", "sSd", "sC"])
    sw4 = [S(nc, pfx + f"sw{q}", persist=True) for q in range(4)]
    swb4 = [S(nc, f"swb4_{q}") for q in range(4)]
    sx2 = [S(nc, f"sx{q}") for q in range(2)]
    VC = Chain(nc.vector, S(nc, "chV"))
    SC = Chain(nc.scalar, S(nc, "chS"))

    if wscr is not None:
        rb_w1 = wscr["w1T"].rearrange("(a p) m -> p a m", p=128)
        rb_w2 = wscr["w2T"].rearrange("(a p) m -> p a m", p=128)
        rb_h1 = wscr["h1T"].rearrange("(a p) m -> p a m", p=128)
        rb_h2 = wscr["h2T"].rearrange("(a p) m -> p a m", p=128)
        rb_w3 = wscr["w3T"].rearrange("(a p) m -> p a m", p=128)

    stripes = []
    for mg in range(8):
        stripes.append(("w1", mg, 0))
        stripes.append(("w2", mg, 0))
    for mg in range(8):
        for kg in range(4):
            stripes.append(("h1", mg, kg))
    for mg in range(8):
        for kg in range(4):
            stripes.append(("h2", mg, kg))
    for mg in range(2):
        for kg in range(4):
            stripes.append(("w3", mg, kg))
    NS = len(stripes)
    stripe_free = {}

    def stripe_src(kind, mg, kg, f32src):
        if f32src:
            t = {"w1": r_w1, "w2": r_w2, "h1": r_h1, "h2": r_h2, "w3": r_w3}[kind]
        else:
            t = {"w1": rb_w1, "w2": rb_w2, "h1": rb_h1, "h2": rb_h2,
                 "w3": rb_w3}[kind]
        if kind in ("w1", "w2"):
            return t[:, :, mg * 512:(mg + 1) * 512]
        return t[:, kg * 8:(kg + 1) * 8, mg * 512:(mg + 1) * 512]

    with nc.Block() as b:
        @b.sync
        def _(sp):
            sp.dma_start(s1c[:], s1d).then_inc(sm["sC"].h, 16)
            sp.dma_start(s2c[:], s2d).then_inc(sm["sC"].h, 16)
            sp.dma_start(sh1c[:], sh1d).then_inc(sm["sC"].h, 16)
            sp.dma_start(sh2c[:], sh2d).then_inc(sm["sC"].h, 16)
            sp.dma_start(s3c[:], s3d).then_inc(sm["sC"].h, 16)
            sp.dma_start(idb[:], idb_d).then_inc(sm["sC"].h, 16)
            sp.dma_start(idf[:], idf_d).then_inc(sm["sC"].h, 16)
            sm["sC"].n = 7 * 16

        for i in range(NTILES):
            @b.sync
            def _(sp, i=i):
                if i >= 2:
                    sp.wait_ge(sm["sRes"].h, 4 * (i - 1))
                sx2[i % 2].inc(sp.dma_start(xt[i % 2][:].rearrange("p (g f) -> p g f", f=D),
                                            x_in_t[i]), 16)
                if i == 0 and wscr is not None and first:
                    for idx, (kind, mg, kg) in enumerate(stripes):
                        sp.wait_ge(sw4[idx % 4].h, 16 * (idx // 4 + 1))
                        dstw = stripe_src(kind, mg, kg, False)
                        swb4[idx % 4].inc(sp.dma_start(
                            dstw,
                            wbuf[idx % 4][:].rearrange("p (a m) -> p a m", m=512)), 16)
                sp.wait_ge(sm["sRes"].h, 4 * (i + 1))
                sm["sst"].inc(sp.dma_start(x_out_t[i],
                                           xo[:].rearrange("p (g f) -> p g f", f=D)), 16)

            @b.vector
            def _(v_, i=i):
                v = VC
                if i == 0:
                    v.wait_ge(sm["sC"].h, 7 * 16)
                    v.op("memset", epsb[:], EPS)
                v.wait_ge(sx2[i % 2].h, 16 * (i // 2 + 1))
                for g in range(4):
                    xg = xt[i % 2][:, g * D:(g + 1) * D]
                    v.op("tensor_tensor", sq[:], xg, xg, ALU.mult)
                    v.op("reduce_sum", sumsq[:, g:g + 1], sq[:], AX)
                    v.op("reduce_sum", sums[:, g:g + 1], xg, AX)
                v.op("tensor_scalar_mul", mu[:], sums[:], 1.0 / D)
                v.op("tensor_scalar_mul", ex2[:], sumsq[:], 1.0 / D)
                v.op("tensor_tensor", musq[:], mu[:], mu[:], ALU.mult)
                v.op("tensor_tensor", var[:], ex2[:], musq[:], ALU.subtract,
                     inc=sm["sVr"])
                v.wait_ge(sm["sSd"].h, i + 1)
                v.op("reciprocal", istd[:], stdv[:])
                v.wait_ge(sm["stp"].h, 8 * i)
                for g in range(4):
                    v.op("tensor_scalar",
                         zt[:, g * D:(g + 1) * D], xt[i % 2][:, g * D:(g + 1) * D],
                         mu[:, g:g + 1], istd[:, g:g + 1], ALU.subtract, ALU.mult,
                         inc=sm["sln"])
                v.wait_ge(sm["sA2"].h, 32 * i)
                for c in range(8):
                    v.wait_ge(sm["stp"].h, i * 8 + c + 1)
                    v.op("tensor_copy", zf[:, c * TT:(c + 1) * TT],
                         psT[c % 2][:, 0:TT], inc=sm["szf"])
                v.wait_ge(sm["sM3"].h, 8 * i)
                for mci in range(32):
                    G = i * 32 + mci
                    v.wait_ge(sm["sA2"].h, G + 1)
                    v.op("tensor_scalar_mul", t2[mci % 2][:],
                         psA[2 + mci % 2][:], s2c[:, mci:mci + 1], inc=sm["sE2"])
                    v.wait_ge(sm["sE1"].h, G + 1)
                    v.op("tensor_tensor", ha[:, mci * TT:(mci + 1) * TT],
                         t1[mci % 2][:], t2[mci % 2][:], ALU.mult, inc=sm["sPr"])
                if i >= 1:
                    v.wait_ge(sm["sst"].h, 16 * i)
                for g in range(4):
                    v.wait_ge(sm["sYT"].h, i * 4 + g + 1)
                    v.op("scalar_tensor_tensor",
                         xo[:, g * D:(g + 1) * D], psW[:, 0:D], 0.5,
                         xt[i % 2][:, g * D:(g + 1) * D], ALU.mult, ALU.add,
                         inc=sm["sRes"])

            @b.scalar
            def _(s_, i=i):
                s = SC
                if i == 0:
                    s.wait_ge(sm["sC"].h, 7 * 16)
                s.wait_ge(sm["sVr"].h, i + 1)
                s.op("activation", stdv[:], var[:], AF.Sqrt, bias=epsb[:, 0:1],
                     inc=sm["sSd"])
                s.wait_ge(sm["sYT"].h, 4 * i)
                for mci in range(32):
                    G = i * 32 + mci
                    s.wait_ge(sm["sA1"].h, G + 1)
                    if G >= 2:
                        s.wait_ge(sm["sPr"].h, G - 1)
                    s.op("activation", t1[mci % 2][:], psA[mci % 2][:],
                         AF.Silu, scale=s1c[:, mci:mci + 1], inc=sm["sE1"])
                s.wait_ge(sm["sM2"].h, 32 * i)
                for mci in range(32):
                    s.wait_ge(sm["sM1"].h, i * 32 + mci + 1)
                    s.op("activation", hb[:, mci * TT:(mci + 1) * TT],
                         psA[mci % 4][:], AF.Gelu, scale=sh1c[:, mci:mci + 1],
                         inc=sm["sH1"])
                for mci in range(32):
                    s.wait_ge(sm["sM2"].h, i * 32 + mci + 1)
                    s.op("activation", ha[:, mci * TT:(mci + 1) * TT],
                         psA[mci % 4][:], AF.Gelu, scale=sh2c[:, mci:mci + 1],
                         inc=sm["sH2"])
                for mci in range(8):
                    s.wait_ge(sm["sM3"].h, i * 8 + mci + 1)
                    s.op("activation", yf[:, mci * TT:(mci + 1) * TT],
                         psA[mci % 4][:], AF.Copy, scale=s3c[:, mci:mci + 1],
                         inc=sm["sYf"])

            @b.tensor
            def _(t, i=i):
                J0 = i * NS
                if i == 0:
                    t.wait_ge(sm["sC"].h, 7 * 16)
                t.wait_ge(sm["sln"].h, 4 * (i + 1))
                for c in range(8):
                    C = i * 8 + c
                    if C >= 2:
                        t.wait_ge(sm["szf"].h, C - 1)
                    for g in range(4):
                        tr = t.transpose(psT[c % 2][:, g * 128:(g + 1) * 128],
                                         zt[:, g * D + c * 128:g * D + c * 128 + 128],
                                         idb[:])
                    sm["stp"].inc(tr)
                t.wait_ge(sm["szf"].h, 8 * (i + 1))
                t.wait_ge(sm["sYf"].h, 8 * i)
                for mg in range(8):
                    Jw1 = J0 + 2 * mg
                    Jw2 = Jw1 + 1
                    t.wait_ge(sw4[Jw1 % 4].h, 16 * (Jw1 // 4 + 1))
                    t.wait_ge(sw4[Jw2 % 4].h, 16 * (Jw2 // 4 + 1))
                    for mc in range(4):
                        mci = mg * 4 + mc
                        G = i * 32 + mci
                        if G >= 2:
                            t.wait_ge(sm["sE1"].h, G - 1)
                        for k in range(8):
                            mm = t.matmul(psA[mci % 2][:],
                                          wbuf[Jw1 % 4][:, k * 512 + mc * 128:k * 512 + mc * 128 + 128],
                                          zf[:, k * TT:(k + 1) * TT],
                                          start=(k == 0), stop=(k == 7))
                        sm["sA1"].inc(mm)
                        if G >= 2:
                            t.wait_ge(sm["sE2"].h, G - 1)
                        for k in range(8):
                            mm = t.matmul(psA[2 + mci % 2][:],
                                          wbuf[Jw2 % 4][:, k * 512 + mc * 128:k * 512 + mc * 128 + 128],
                                          zf[:, k * TT:(k + 1) * TT],
                                          start=(k == 0), stop=(k == 7))
                        sm["sA2"].inc(mm)
                    stripe_free[Jw1] = (sm["sE1"], i * 32 + mg * 4 + 4)
                    stripe_free[Jw2] = (sm["sE2"], i * 32 + mg * 4 + 4)
                for kind, s_in, sMx, sHx, base in (("h1", ha, sm["sM1"], sm["sH1"], J0 + 16),
                                                   ("h2", hb, sm["sM2"], sm["sH2"], J0 + 48)):
                    if kind == "h1":
                        t.wait_ge(sm["sPr"].h, 32 * (i + 1))
                    else:
                        t.wait_ge(sm["sH1"].h, 32 * (i + 1))
                    for mg in range(8):
                        for kg in range(4):
                            J = base + mg * 4 + kg
                            t.wait_ge(sw4[J % 4].h, 16 * (J // 4 + 1))
                            for mc in range(4):
                                mci = mg * 4 + mc
                                if kg == 0 and mg >= 1:
                                    t.wait_ge(sHx.h, i * 32 + (mg - 1) * 4 + mc + 1)
                                for k in range(8):
                                    mm = t.matmul(
                                        psA[mc][:],
                                        wbuf[J % 4][:, k * 512 + mc * 128:k * 512 + mc * 128 + 128],
                                        s_in[:, (kg * 8 + k) * TT:(kg * 8 + k + 1) * TT],
                                        start=(kg == 0 and k == 0),
                                        stop=(kg == 3 and k == 7))
                                if kg == 3:
                                    sMx.inc(mm)
                            if kg < 3:
                                sm["sKG"].inc(mm)
                                stripe_free[J] = (sm["sKG"], sm["sKG"].n)
                            else:
                                stripe_free[J] = (sMx, i * 32 + mg * 4 + 4)
                t.wait_ge(sm["sH2"].h, 32 * (i + 1))
                for mg in range(2):
                    for kg in range(4):
                        J = J0 + 80 + mg * 4 + kg
                        t.wait_ge(sw4[J % 4].h, 16 * (J // 4 + 1))
                        for mc in range(4):
                            mci = mg * 4 + mc
                            if kg == 0 and mg >= 1:
                                t.wait_ge(sm["sYf"].h, i * 8 + (mg - 1) * 4 + mc + 1)
                            for k in range(8):
                                mm = t.matmul(
                                    psA[mc][:],
                                    wbuf[J % 4][:, k * 512 + mc * 128:k * 512 + mc * 128 + 128],
                                    ha[:, (kg * 8 + k) * TT:(kg * 8 + k + 1) * TT],
                                    start=(kg == 0 and k == 0),
                                    stop=(kg == 3 and k == 7))
                            if kg == 3:
                                sm["sM3"].inc(mm)
                        if kg < 3:
                            sm["sKG"].inc(mm)
                            stripe_free[J] = (sm["sKG"], sm["sKG"].n)
                        else:
                            stripe_free[J] = (sm["sM3"], i * 8 + mg * 4 + 4)
                t.wait_ge(sm["sYf"].h, 8 * (i + 1))
                for g in range(4):
                    t.wait_ge(sm["sRes"].h, 4 * i + g)
                    for c in range(8):
                        tr = t.transpose(psW[:, c * 128:(c + 1) * 128],
                                         yf[:, c * TT + g * 128:c * TT + g * 128 + 128],
                                         idf[:])
                    sm["sYT"].inc(tr)

            @b.gpsimd
            def _(g, i=i):
                wb_on = wscr is not None and first
                for idx, (kind, mg, kg) in enumerate(stripes):
                    J = i * NS + idx
                    if J >= 4:
                        fs, fv = stripe_free[J - 4]
                        g.wait_ge(fs.h, fv)
                        if wb_on and J - 4 < NS:
                            g.wait_ge(swb4[(J - 4) % 4].h,
                                      16 * ((J - 4) // 4 + 1))
                    f32src = (wscr is None) or (first and i == 0)
                    src = stripe_src(kind, mg, kg, f32src)
                    dst = wbuf[J % 4][:].rearrange("p (a m) -> p a m", m=512)
                    sw4[J % 4].inc(g.dma_start(dst, src), 16)

        if DEBUG_TAPS:
            @b.gpsimd
            def _(g):
                g.wait_ge(sm["sst"].h, 16 * NTILES)
                for nm, buf in (("d_zt", zt), ("d_zf", zf), ("d_ha", ha),
                                ("d_hb", hb), ("d_yf", yf), ("d_xt", xt[(NTILES - 1) % 2])):
                    dt_ = nc.dram_tensor(nm, list(buf[:].shape), buf.dtype,
                                         kind="ExternalOutput").ap()
                    g.dma_start(dt_, buf[:]).then_inc(sm["sC"].h, 16)
                g.wait_ge(sm["sC"].h, sm["sC"].n + 6 * 16)

        @b.sync
        def _(sp):
            sp.wait_ge(sm["sst"].h, 16 * NTILES)

    _st.close()
    _emit_sem_reset(nc)


def _declare_io(nc, keys, shapes):
    wd = {}
    for k in keys:
        shp, dt = shapes[k]
        wd[k] = nc.dram_tensor(k, shp, dt, kind="ExternalInput").ap()
    return wd


_SHAPES = {
    "w1T": ([D, DH], F32), "w2T": ([D, DH], F32), "h1T": ([DH, DH], F32),
    "h2T": ([DH, DH], F32), "w3T": ([DH, D], F32), "s1c": ([128, 32], F32),
    "s2c": ([128, 32], F32), "sh1c": ([128, 32], F32), "sh2c": ([128, 32], F32),
    "s3c": ([128, 8], F32), "identb": ([128, 128], BF16), "identf": ([128, 128], F32),
    "WthT": ([D, 256], F32), "WphT": ([D, 256], F32), "WgT": ([D, D], F32),
    "WrT": ([D, 8], F32), "ATall": ([D, 512], F32), "Bpk": ([512, D], F32),
    "Eall": ([8, 512], F32),
    "WQT": ([D, D], F32), "WKT": ([D, D], F32), "WVT": ([D, D], F32),
    "WOT": ([D, D], F32), "Ctab": ([128, SEQ], F32), "Stab": ([128, SEQ], F32),
    "Mtab": ([128, 2048], F32), "Pswap": ([128, 128], BF16),
}


def build_ffn0():
    nc = bass.Bass("TRN2", target_bir_lowering=False)
    x_in = nc.dram_tensor("x_in", [TPC, D], F32, kind="ExternalInput").ap()
    x_out = nc.dram_tensor("x_out", [TPC, D], F32, kind="ExternalOutput").ap()
    wd = _declare_io(nc, FFN0_KEYS, _SHAPES)
    emit_ffn0(nc, "f_", x_in, x_out, wd)
    return nc




L0_KEYS = ["WthT", "WphT", "WgT", "identb"]


def emit_l0attn(nc, pfx, x_in, x_out, wd):
    _st = contextlib.ExitStack()
    sb = lambda n, s, d: _st.enter_context(nc.sbuf_tensor(pfx + n, s, d))
    pst = lambda n, s, d: _st.enter_context(nc.psum_tensor(pfx + n, s, d))
    wthd = wd["WthT"]; wphd = wd["WphT"]; wgd = wd["WgT"]; idb_d = wd["identb"]

    x_in_t = x_in.rearrange("(t g p) f -> t p g f", g=4, p=128)
    x_in_q = x_in.rearrange("(q p) f -> q p f", p=128)
    x_out_q = x_out.rearrange("(q p) f -> q p f", p=128)
    r_wth = wthd.rearrange("(a p) m -> p a m", p=128)
    r_wph = wphd.rearrange("(a p) m -> p a m", p=128)
    r_wg = wgd.rearrange("(a p) m -> p a m", p=128)

    xt2 = [sb(f"xt2{q}", [128, 4 * D], F32) for q in range(2)]
    xq = [sb(f"xq{q}", [128, D], F32) for q in range(2)]
    xoq = [sb(f"xoq{q}", [128, D], F32) for q in range(2)]
    zt = sb("zt", [128, 4 * D], BF16)
    zf = sb("zf", [128, 8 * SEQ], BF16)
    sq = sb("sq", [128, D], F32)
    th = sb("th", [128, 2 * SEQ], BF16)
    ph = sb("ph", [128, 2 * SEQ], BF16)
    gf = sb("gf", [128, 8 * SEQ], BF16)
    phT = sb("phT", [128, 16 * 256], BF16)
    gT = sb("gT", [128, 16 * D], BF16)
    m_sb = sb("m_sb", [128, 2 * D], BF16)
    wpb = sb("wpb", [128, 8 * D], BF16)
    idb = sb("idb", [128, 128], BF16)
    msc = sb("msc", [128, 1], F32)
    sums = sb("sums", [128, 4], F32)
    sumsq = sb("sumsq", [128, 4], F32)
    mu = sb("mu", [128, 4], F32)
    ex2 = sb("ex2", [128, 4], F32)
    musq = sb("musq", [128, 4], F32)
    var = sb("var", [128, 4], F32)
    stdv = sb("stdv", [128, 4], F32)
    istd = sb("istd", [128, 4], F32)
    epsb = sb("epsb", [128, 1], F32)

    psA = [pst(f"psA{q}", [128, 512], F32) for q in range(4)]
    psW = pst("psW", [128, D], F32)
    psT = [pst(f"psT{q}", [128, 1024], BF16) for q in range(2)]

    sm = _sems(nc, ["sC", "sst", "sln", "stp", "szf", "sMM", "sEv", "sTB",
                    "sTE", "sMs", "sVr", "sSd", "sRes"])
    sxt = [S(nc, f"sxt{q}") for q in range(2)]
    sst2 = [S(nc, f"sst2{q}") for q in range(2)]
    sxq = [S(nc, f"sxq{q}") for q in range(2)]
    swb = S(nc, pfx + "swb", persist=True)
    VC = Chain(nc.vector, S(nc, "chV"))
    SC = Chain(nc.scalar, S(nc, "chS"))

    def wview(mwid):
        return wpb[:, 0:8 * mwid].rearrange("p (a m) -> p a m", m=mwid)

    with nc.Block() as b0:
        @b0.sync
        def _(sp):
            sm["sC"].inc(sp.dma_start(idb[:], idb_d), 16)

        @b0.vector
        def _(v_):
            VC.op("memset", epsb[:], EPS)
            VC.op("memset", msc[:], 1.0 / 32.0)

    for bq in range(2):
        with nc.Block() as b:
            @b.sync
            def _(sp, bq=bq):
                for tau in range(4):
                    gt = bq * 4 + tau
                    if gt >= 2:
                        sp.wait_ge(sm["sln"].h, 4 * (gt - 1))
                    sxt[tau % 2].inc(sp.dma_start(
                        xt2[tau % 2][:].rearrange("p (g f) -> p g f", f=D),
                        x_in_t[gt]), 16)
                for qt in range(16):
                    gq = bq * 16 + qt
                    if gq >= 2:
                        sp.wait_ge(sm["sRes"].h, 2 * (gq - 1))
                    sxq[qt % 2].inc(sp.dma_start(xq[qt % 2][:], x_in_q[gq]), 16)
                    sp.wait_ge(sm["sRes"].h, 2 * (gq + 1))
                    sst2[qt % 2].inc(sp.dma_start(x_out_q[gq], xoq[qt % 2][:]), 16)

            @b.gpsimd
            def _(g, bq=bq):
                if bq == 0:
                    g.wait_ge(sm["sC"].h, 16)
                swb.inc(g.dma_start(wview(256), r_wth), 16)
                g.wait_ge(sm["sMM"].h, bq * 52 + 8)
                swb.inc(g.dma_start(wview(256), r_wph), 16)
                g.wait_ge(sm["sMM"].h, bq * 52 + 16)
                swb.inc(g.dma_start(wview(D), r_wg), 16)

            @b.vector
            def _(v_, bq=bq):
                v = VC
                for tau in range(4):
                    gt = bq * 4 + tau
                    v.wait_ge(sxt[tau % 2].h, 16 * (gt // 2 + 1))
                    for g in range(4):
                        xg = xt2[tau % 2][:, g * D:(g + 1) * D]
                        v.op("tensor_tensor", sq[:], xg, xg, ALU.mult)
                        v.op("reduce_sum", sumsq[:, g:g + 1], sq[:], AX)
                        v.op("reduce_sum", sums[:, g:g + 1], xg, AX)
                    v.op("tensor_scalar_mul", mu[:], sums[:], 1.0 / D)
                    v.op("tensor_scalar_mul", ex2[:], sumsq[:], 1.0 / D)
                    v.op("tensor_tensor", musq[:], mu[:], mu[:], ALU.mult)
                    v.op("tensor_tensor", var[:], ex2[:], musq[:], ALU.subtract,
                         inc=sm["sVr"])
                    v.wait_ge(sm["sSd"].h, gt + 1)
                    v.op("reciprocal", istd[:], stdv[:])
                    v.wait_ge(sm["stp"].h, 8 * gt)
                    for g in range(4):
                        v.op("tensor_scalar", zt[:, g * D:(g + 1) * D],
                             xt2[tau % 2][:, g * D:(g + 1) * D],
                             mu[:, g:g + 1], istd[:, g:g + 1],
                             ALU.subtract, ALU.mult, inc=sm["sln"])
                    for c in range(8):
                        CB = gt * 8 + c
                        v.wait_ge(sm["stp"].h, CB + 1)
                        v.op("tensor_copy",
                             zf[:, c * SEQ + tau * 512:c * SEQ + tau * 512 + 512],
                             psT[c % 2][:, 0:512], inc=sm["szf"])
                for kt in range(16):
                    v.wait_ge(sm["sTB"].h, bq * 32 + 2 * kt + 1)
                    v.op("tensor_copy", phT[:, kt * 256:(kt + 1) * 256],
                         psT[0][:, 0:256], inc=sm["sTE"])
                    v.wait_ge(sm["sTB"].h, bq * 32 + 2 * kt + 2)
                    v.op("tensor_copy", gT[:, kt * D:(kt + 1) * D],
                         psT[1][:, 0:D], inc=sm["sTE"])
                for qt in range(16):
                    gq = bq * 16 + qt
                    if gq >= 2:
                        v.wait_ge(sst2[gq % 2].h, 16 * (gq // 2))
                    v.wait_ge(sxq[qt % 2].h, 16 * (gq // 2 + 1))
                    for oc in range(2):
                        v.wait_ge(sm["sMs"].h, 2 * gq + oc + 1)
                        v.op("tensor_tensor",
                             xoq[qt % 2][:, oc * 512:(oc + 1) * 512],
                             psA[2 * (qt % 2) + oc][:],
                             xq[qt % 2][:, oc * 512:oc * 512 + 512],
                             ALU.add, inc=sm["sRes"])

            @b.scalar
            def _(s_, bq=bq):
                s = SC
                for tau in range(4):
                    s.wait_ge(sm["sVr"].h, bq * 4 + tau + 1)
                    s.op("activation", stdv[:], var[:], AF.Sqrt,
                         bias=epsb[:, 0:1], inc=sm["sSd"])
                for pi, (nmc, obuf, owid) in enumerate(
                        ((2, th, SEQ), (2, ph, SEQ), (8, gf, SEQ))):
                    base = [0, 8, 16][pi] + bq * 52
                    for mc in range(nmc):
                        for nt in range(4):
                            gidx = base + mc * 4 + nt
                            s.wait_ge(sm["sMM"].h, gidx + 1)
                            s.op("activation",
                                 obuf[:, mc * owid + nt * 512:mc * owid + nt * 512 + 512],
                                 psA[gidx % 4][:], AF.Copy, inc=sm["sEv"])
                for idx in range(4):
                    s.wait_ge(sm["sMM"].h, bq * 52 + 48 + idx + 1)
                    lc, oc = idx // 2, idx % 2
                    s.op("activation",
                         m_sb[:, lc * D + oc * 512:lc * D + oc * 512 + 512],
                         psA[idx][:], AF.Copy, scale=msc[:, 0:1], inc=sm["sEv"])

            @b.tensor
            def _(t, bq=bq):
                if bq == 0:
                    t.wait_ge(sm["sC"].h, 16)
                for tau in range(4):
                    gt = bq * 4 + tau
                    t.wait_ge(sm["sln"].h, 4 * (gt + 1))
                    for c in range(8):
                        CB = gt * 8 + c
                        if CB >= 2:
                            t.wait_ge(sm["szf"].h, CB - 1)
                        for g in range(4):
                            tr = t.transpose(psT[c % 2][:, g * 128:(g + 1) * 128],
                                             zt[:, g * D + c * 128:g * D + c * 128 + 128],
                                             idb[:])
                        sm["stp"].inc(tr)
                t.wait_ge(sm["szf"].h, 32 * (bq + 1))
                gidx = bq * 52
                for wi, (nmc, mwid) in enumerate(((2, 256), (2, 256), (8, D))):
                    t.wait_ge(swb.h, 16 * (bq * 3 + wi + 1))
                    for mc in range(nmc):
                        for nt in range(4):
                            if gidx >= 4 + bq * 52:
                                t.wait_ge(sm["sEv"].h, gidx - 3)
                            for k in range(8):
                                mm = t.matmul(psA[gidx % 4][:],
                                              wpb[:, k * mwid + mc * 128:k * mwid + mc * 128 + 128],
                                              zf[:, k * SEQ + nt * 512:k * SEQ + nt * 512 + 512],
                                              start=(k == 0), stop=(k == 7))
                            sm["sMM"].inc(mm)
                            gidx += 1
                t.wait_ge(sm["sEv"].h, bq * 52 + 16)
                for kt in range(16):
                    TB = bq * 32 + 2 * kt
                    if TB >= 2:
                        t.wait_ge(sm["sTE"].h, TB - 1)
                    for lc in range(2):
                        tr = t.transpose(psT[0][:, lc * 128:(lc + 1) * 128],
                                         ph[:, lc * SEQ + kt * 128:lc * SEQ + kt * 128 + 128],
                                         idb[:])
                    sm["sTB"].inc(tr)
                    if TB + 1 >= 2:
                        t.wait_ge(sm["sTE"].h, TB)
                    if kt == 0:
                        t.wait_ge(sm["sEv"].h, bq * 52 + 48)
                    for mc in range(8):
                        tr = t.transpose(psT[1][:, mc * 128:(mc + 1) * 128],
                                         gf[:, mc * SEQ + kt * 128:mc * SEQ + kt * 128 + 128],
                                         idb[:])
                    sm["sTB"].inc(tr)
                t.wait_ge(sm["sTE"].h, 32 * (bq + 1))
                for idx in range(4):
                    lc, oc = idx // 2, idx % 2
                    for kt in range(16):
                        mm = t.matmul(psA[idx][:],
                                      phT[:, kt * 256 + lc * 128:kt * 256 + lc * 128 + 128],
                                      gT[:, kt * D + oc * 512:kt * D + oc * 512 + 512],
                                      start=(kt == 0), stop=(kt == 15))
                    sm["sMM"].inc(mm)
                t.wait_ge(sm["sEv"].h, bq * 52 + 52)
                for qt in range(16):
                    gq = bq * 16 + qt
                    if gq >= 2:
                        t.wait_ge(sm["sRes"].h, 2 * (gq - 1))
                    for oc in range(2):
                        for lc in range(2):
                            mm = t.matmul(psA[2 * (qt % 2) + oc][:],
                                          th[:, lc * SEQ + qt * 128:lc * SEQ + qt * 128 + 128],
                                          m_sb[:, lc * D + oc * 512:lc * D + oc * 512 + 512],
                                          start=(lc == 0), stop=(lc == 1))
                        sm["sMs"].inc(mm)

    with nc.Block() as bf:
        @bf.sync
        def _(sp):
            sp.wait_ge(sst2[0].h, 16 * 16)
            sp.wait_ge(sst2[1].h, 16 * 16)

    _st.close()
    _emit_sem_reset(nc)


def build_l0attn():
    nc = bass.Bass("TRN2", target_bir_lowering=False)
    x_in = nc.dram_tensor("x_in", [TPC, D], F32, kind="ExternalInput").ap()
    x_out = nc.dram_tensor("x_out", [TPC, D], F32, kind="ExternalOutput").ap()
    wd = _declare_io(nc, L0_KEYS, _SHAPES)
    emit_l0attn(nc, "a_", x_in, x_out, wd)
    return nc




L1F_KEYS = ["WrT", "ATall", "Bpk", "Eall", "identb", "identf"]


def emit_l1ffn(nc, pfx, x_in, x_out, wd):
    _st = contextlib.ExitStack()
    sb = lambda n, s, d: _st.enter_context(nc.sbuf_tensor(pfx + n, s, d))
    pst = lambda n, s, d: _st.enter_context(nc.psum_tensor(pfx + n, s, d))
    wrd = wd["WrT"]; atd = wd["ATall"]; bpd = wd["Bpk"]; ed = wd["Eall"]
    idb_d = wd["identb"]; idf_d = wd["identf"]

    x_in_t = x_in.rearrange("(t g p) f -> t p g f", g=4, p=128)
    x_out_t = x_out.rearrange("(t g p) f -> t p g f", g=4, p=128)
    x_cls = x_in.rearrange("(b s) f -> b s f", s=SEQ)[:, 0, :]

    xt = [sb(f"xt{q}", [128, 4 * D], F32) for q in range(2)]
    xo = sb("xo", [128, 4 * D], F32)
    zt = sb("zt", [128, 4 * D], BF16)
    zf = sb("zf", [128, 8 * TT], BF16)
    sq = sb("sq", [128, D], F32)
    xzb = sb("xzb", [128, D], F32)
    mid_sb = sb("mid_sb", [128, 4 * TT], BF16)
    wf = sb("wf", [128, 8 * TT], F32)
    ats = sb("ats", [128, 8 * 512], BF16)
    bps = sb("bps", [128, 4 * D], BF16)
    wrs = sb("wrs", [128, 8 * 8], BF16)
    ebs = sb("ebs", [8, 512], BF16)
    cls_x = sb("cls_x", [2, D], F32)
    clsz = sb("clsz", [2, D], BF16)
    cls_fm = sb("cls_fm", [128, 16], BF16)
    lg = sb("lg", [2, 8], F32)
    gsc = sb("gsc", [2, 16], F32)
    gs2 = sb("gs2", [2, 40], F32)
    gb = sb("gb", [2, 8], BF16)
    gts = sb("gts", [8, 2], BF16)
    g_all = sb("g_all", [128, 8], F32)
    idb = sb("idb", [128, 128], BF16)
    idf = sb("idf", [128, 128], F32)
    epsb = sb("epsb", [128, 1], F32)
    sums = sb("sums", [128, 4], F32)
    sumsq = sb("sumsq", [128, 4], F32)
    mu = sb("mu", [128, 4], F32)
    ex2 = sb("ex2", [128, 4], F32)
    musq = sb("musq", [128, 4], F32)
    var = sb("var", [128, 4], F32)
    stdv = sb("stdv", [128, 4], F32)
    istd = sb("istd", [128, 4], F32)
    half = sb("half", [128, 4], F32)
    aZ = sb("aZ", [128, 4], F32)
    bZmu = sb("bZmu", [128, 4], F32)

    psA = [pst(f"psA{q}", [128, TT], F32) for q in range(4)]
    psW = pst("psW", [128, D], F32)
    psT = [pst(f"psT{q}", [128, 1024], BF16) for q in range(2)]

    sm = _sems(nc, ["sC", "sst", "sln", "stp", "szf", "sMD", "sME", "sEO",
                    "sWf", "sYT", "sRes", "sVr", "sSd", "sLNz", "sTz", "sFM",
                    "sLG", "sGpre", "sLGe", "sTg", "sGT", "sGV", "sGA"])
    sm["sCW"] = S(nc, pfx + "sCW", persist=True)
    sx2 = [S(nc, f"sx{q}") for q in range(2)]
    VC = Chain(nc.vector, S(nc, "chV"))
    SC = Chain(nc.scalar, S(nc, "chS"))

    with nc.Block() as b0:
        @b0.sync
        def _(sp):
            sm["sC"].inc(sp.dma_start(idb[:], idb_d), 16)
            sm["sC"].inc(sp.dma_start(idf[:], idf_d), 16)
            sm["sC"].inc(sp.dma_start(cls_x[:], x_cls), 16)

        @b0.gpsimd
        def _(g):
            sm["sCW"].inc(g.dma_start(
                wrs[:].rearrange("p (a e) -> p a e", e=8),
                wrd.rearrange("(a p) e -> p a e", p=128)), 16)
            sm["sCW"].inc(g.dma_start(
                ats[:].rearrange("p (a m) -> p a m", m=512),
                atd.rearrange("(a p) m -> p a m", p=128)), 16)
            sm["sCW"].inc(g.dma_start(
                bps[:].rearrange("p (a m) -> p a m", m=D),
                bpd.rearrange("(a p) m -> p a m", p=128)), 16)
            sm["sCW"].inc(g.dma_start(ebs[:], ed), 16)

        @b0.vector
        def _(v_):
            v = VC
            v.op("memset", epsb[:], EPS)
            v.wait_ge(sm["sC"].h, 3 * 16)
            v.wait_ge(sm["sCW"].h, 4 * 16)
            v.op("tensor_tensor", sq[0:2, :], cls_x[:], cls_x[:], ALU.mult)
            v.op("reduce_sum", gsc[:, 0:1], sq[0:2, :], AX)
            v.op("reduce_sum", gsc[:, 1:2], cls_x[:], AX)
            v.op("tensor_scalar_mul", gsc[:, 2:3], gsc[:, 1:2], 1.0 / D)
            v.op("tensor_scalar_mul", gsc[:, 3:4], gsc[:, 0:1], 1.0 / D)
            v.op("tensor_tensor", gsc[:, 4:5], gsc[:, 2:3], gsc[:, 2:3], ALU.mult)
            v.op("tensor_tensor", gsc[:, 5:6], gsc[:, 3:4], gsc[:, 4:5],
                 ALU.subtract, inc=sm["sVr"])
            v.wait_ge(sm["sSd"].h, 1)
            v.op("reciprocal", gsc[:, 7:8], gsc[:, 6:7])
            v.op("tensor_scalar", clsz[:], cls_x[:], gsc[:, 2:3], gsc[:, 7:8],
                 ALU.subtract, ALU.mult, inc=sm["sLNz"])
            v.wait_ge(sm["sTz"].h, 1)
            v.op("tensor_copy", cls_fm[:], psT[0][:, 0:16], inc=sm["sFM"])
            v.wait_ge(sm["sLG"].h, 1)
            v.op("tensor_copy", lg[:], psA[0][0:2, 0:8])
            v.op("reduce_max", gsc[:, 8:9], lg[:], AX, negate=True)
            v.op("tensor_scalar", gs2[:, 0:8], lg[:], gsc[:, 8:9], None,
                 ALU.add)
            v.op("tensor_single_scalar", gs2[:, 8:16], gs2[:, 0:8], 0.0, ALU.is_ge)
            v.op("scalar_tensor_tensor", gs2[:, 16:24], gs2[:, 8:16], -1e30,
                 gs2[:, 0:8], ALU.mult, ALU.add)
            v.op("reduce_max", gsc[:, 9:10], gs2[:, 16:24], AX, negate=True)
            v.op("tensor_scalar_mul", gsc[:, 10:11], gsc[:, 9:10], -1.0)
            v.op("tensor_single_scalar", gs2[:, 24:32], gs2[:, 0:8],
                 gsc[:, 10:11], ALU.is_ge, inc=sm["sGpre"])
            v.wait_ge(sm["sSd"].h, 2)
            v.op("tensor_tensor", gs2[:, 32:40], gs2[:, 32:40], gs2[:, 24:32],
                 ALU.mult)
            v.op("reduce_sum", gsc[:, 11:12], gs2[:, 32:40], AX)
            v.op("reciprocal", gsc[:, 12:13], gsc[:, 11:12])
            v.op("tensor_scalar_mul", gb[:], gs2[:, 32:40], gsc[:, 12:13],
                 inc=sm["sLGe"])
            v.wait_ge(sm["sTg"].h, 1)
            v.op("tensor_copy", gts[:], psT[0][0:8, 16:18], inc=sm["sGT"])
            v.wait_ge(sm["sGV"].h, 4)
            v.op("tensor_copy", g_all[:], psA[1][:, 0:8], inc=sm["sGA"])

        @b0.scalar
        def _(s_):
            s = SC
            s.wait_ge(sm["sVr"].h, 1)
            s.op("activation", gsc[:, 6:7], gsc[:, 5:6], AF.Sqrt,
                 bias=epsb[0:2, 0:1], inc=sm["sSd"])
            s.wait_ge(sm["sGpre"].h, 1)
            s.op("activation", gs2[:, 32:40], gs2[:, 0:8], AF.Exp, inc=sm["sSd"])

        @b0.tensor
        def _(t):
            t.wait_ge(sm["sC"].h, 3 * 16)
            t.wait_ge(sm["sCW"].h, 4 * 16)
            t.wait_ge(sm["sLNz"].h, 1)
            for c in range(8):
                tr = t.transpose(psT[0][:, c * 2:(c + 1) * 2],
                                 clsz[:, c * 128:(c + 1) * 128], idb[0:2, 0:2])
            sm["sTz"].inc(tr)
            t.wait_ge(sm["sFM"].h, 1)
            for kc in range(8):
                mm = t.matmul(psA[0][0:2, 0:8], cls_fm[:, kc * 2:(kc + 1) * 2],
                              wrs[:, kc * 8:(kc + 1) * 8],
                              start=(kc == 0), stop=(kc == 7))
            sm["sLG"].inc(mm)
            t.wait_ge(sm["sLGe"].h, 1)
            tr = t.transpose(psT[0][0:8, 16:18], gb[:], idb[0:2, 0:2])
            sm["sTg"].inc(tr)
            t.wait_ge(sm["sGT"].h, 1)
            for p in range(4):
                mm = t.matmul(psA[1][:, p * 2:(p + 1) * 2],
                              ebs[:, p * 128:(p + 1) * 128], gts[:],
                              start=True, stop=True)
                sm["sGV"].inc(mm)

    with nc.Block() as b:
        for i in range(NTILES):
            bqi = i // 4

            @b.sync
            def _(sp, i=i):
                if i >= 2:
                    sp.wait_ge(sm["sRes"].h, 4 * (i - 1))
                sx2[i % 2].inc(sp.dma_start(
                    xt[i % 2][:].rearrange("p (g f) -> p g f", f=D), x_in_t[i]), 16)
                sp.wait_ge(sm["sRes"].h, 4 * (i + 1))
                sm["sst"].inc(sp.dma_start(
                    x_out_t[i], xo[:].rearrange("p (g f) -> p g f", f=D)), 16)

            @b.vector
            def _(v_, i=i):
                v = VC
                v.wait_ge(sx2[i % 2].h, 16 * (i // 2 + 1))
                for g in range(4):
                    xg = xt[i % 2][:, g * D:(g + 1) * D]
                    v.op("tensor_tensor", sq[:], xg, xg, ALU.mult)
                    v.op("reduce_sum", sumsq[:, g:g + 1], sq[:], AX)
                    v.op("reduce_sum", sums[:, g:g + 1], xg, AX)
                v.op("tensor_scalar_mul", mu[:], sums[:], 1.0 / D)
                v.op("tensor_scalar_mul", ex2[:], sumsq[:], 1.0 / D)
                v.op("tensor_tensor", musq[:], mu[:], mu[:], ALU.mult)
                v.op("tensor_tensor", var[:], ex2[:], musq[:], ALU.subtract,
                     inc=sm["sVr"])
                v.wait_ge(sm["sSd"].h, i + 3)
                v.op("reciprocal", istd[:], stdv[:])
                v.op("tensor_scalar_mul", half[:], istd[:], 0.5)
                v.op("tensor_scalar_add", aZ[:], half[:], 1.0)
                v.op("tensor_tensor", bZmu[:], mu[:], half[:], ALU.mult)
                v.wait_ge(sm["stp"].h, 8 * i)
                for g in range(4):
                    v.op("tensor_scalar", zt[:, g * D:(g + 1) * D],
                         xt[i % 2][:, g * D:(g + 1) * D],
                         mu[:, g:g + 1], istd[:, g:g + 1],
                         ALU.subtract, ALU.mult, inc=sm["sln"])
                v.wait_ge(sm["sMD"].h, 4 * i)
                for c in range(8):
                    v.wait_ge(sm["stp"].h, i * 8 + c + 1)
                    v.op("tensor_copy", zf[:, c * TT:(c + 1) * TT],
                         psT[c % 2][:, 0:TT], inc=sm["szf"])
                if i >= 1:
                    v.wait_ge(sm["sst"].h, 16 * i)
                for g in range(4):
                    v.wait_ge(sm["sYT"].h, i * 4 + g + 1)
                    v.op("tensor_scalar", xzb[:], xt[i % 2][:, g * D:(g + 1) * D],
                         aZ[:, g:g + 1], bZmu[:, g:g + 1], ALU.mult, ALU.subtract)
                    v.op("scalar_tensor_tensor", xo[:, g * D:(g + 1) * D],
                         psW[:, 0:D], 0.5, xzb[:], ALU.mult, ALU.add,
                         inc=sm["sRes"])

            @b.scalar
            def _(s_, i=i):
                s = SC
                s.wait_ge(sm["sVr"].h, i + 2)
                s.op("activation", stdv[:], var[:], AF.Sqrt, bias=epsb[:, 0:1],
                     inc=sm["sSd"])
                s.wait_ge(sm["sGA"].h, 1)
                for p in range(4):
                    s.wait_ge(sm["sMD"].h, i * 4 + p + 1)
                    s.op("activation", mid_sb[:, p * TT:(p + 1) * TT],
                         psA[p][:], AF.Copy,
                         scale=g_all[:, (p * 2 + (i // 4)):(p * 2 + (i // 4)) + 1],
                         inc=sm["sME"])
                if i >= 1:
                    s.wait_ge(sm["sYT"].h, 4 * i)
                for m in range(8):
                    s.wait_ge(sm["sEO"].h, i * 8 + m + 1)
                    s.op("activation", wf[:, m * TT:(m + 1) * TT],
                         psA[m % 2][:], AF.Copy, inc=sm["sWf"])

            @b.tensor
            def _(t, i=i):
                if i == 0:
                    t.wait_ge(sm["sGA"].h, 1)
                t.wait_ge(sm["sln"].h, 4 * (i + 1))
                for c in range(8):
                    CB = i * 8 + c
                    if CB >= 2:
                        t.wait_ge(sm["szf"].h, CB - 1)
                    for g in range(4):
                        tr = t.transpose(psT[c % 2][:, g * 128:(g + 1) * 128],
                                         zt[:, g * D + c * 128:g * D + c * 128 + 128],
                                         idb[:])
                    sm["stp"].inc(tr)
                t.wait_ge(sm["szf"].h, 8 * (i + 1))
                if i >= 1:
                    t.wait_ge(sm["sME"].h, 4 * i)
                for p in range(4):
                    for k in range(8):
                        mm = t.matmul(psA[p][:],
                                      ats[:, k * 512 + p * 128:k * 512 + p * 128 + 128],
                                      zf[:, k * TT:(k + 1) * TT],
                                      start=(k == 0), stop=(k == 7))
                    sm["sMD"].inc(mm)
                t.wait_ge(sm["sME"].h, 4 * (i + 1))
                for m in range(8):
                    EOg = i * 8 + m
                    if EOg >= 2:
                        t.wait_ge(sm["sWf"].h, EOg - 1)
                    for p in range(4):
                        mm = t.matmul(psA[m % 2][:],
                                      bps[:, p * D + m * 128:p * D + m * 128 + 128],
                                      mid_sb[:, p * TT:(p + 1) * TT],
                                      start=(p == 0), stop=(p == 3))
                    sm["sEO"].inc(mm)
                t.wait_ge(sm["sWf"].h, 8 * (i + 1))
                for g in range(4):
                    t.wait_ge(sm["sRes"].h, 4 * i + g)
                    for c in range(8):
                        tr = t.transpose(psW[:, c * 128:(c + 1) * 128],
                                         wf[:, c * TT + g * 128:c * TT + g * 128 + 128],
                                         idf[:])
                    sm["sYT"].inc(tr)

    with nc.Block() as bf:
        @bf.sync
        def _(sp):
            sp.wait_ge(sm["sst"].h, 16 * NTILES)

    _st.close()
    _emit_sem_reset(nc)


def build_l1ffn():
    nc = bass.Bass("TRN2", target_bir_lowering=False)
    x_in = nc.dram_tensor("x_in", [TPC, D], F32, kind="ExternalInput").ap()
    x_out = nc.dram_tensor("x_out", [TPC, D], F32, kind="ExternalOutput").ap()
    wd = _declare_io(nc, L1F_KEYS, _SHAPES)
    emit_l1ffn(nc, "m_", x_in, x_out, wd)
    return nc




L1A_KEYS = ["WQT", "WKT", "WVT", "WOT", "Ctab", "Stab", "Mtab",
            "identb", "identf", "Pswap"]


def emit_l1attn(nc, pfx, x_in, x_out, wd):
    _st = contextlib.ExitStack()
    sb = lambda n, s, d: _st.enter_context(nc.sbuf_tensor(pfx + n, s, d))
    pst = lambda n, s, d: _st.enter_context(nc.psum_tensor(pfx + n, s, d))
    wqd = wd["WQT"]; wkd = wd["WKT"]; wvd = wd["WVT"]; wod = wd["WOT"]
    cthd = wd["Ctab"]; sthd = wd["Stab"]; mskd = wd["Mtab"]
    idb_d = wd["identb"]; idf_d = wd["identf"]; psw_d = wd["Pswap"]

    x_in_q = x_in.rearrange("(q p) f -> q p f", p=128)
    x_out_q = x_out.rearrange("(q p) f -> q p f", p=128)
    r_w = {"K": wkd.rearrange("(a p) m -> p a m", p=128),
           "Q": wqd.rearrange("(a p) m -> p a m", p=128),
           "V": wvd.rearrange("(a p) m -> p a m", p=128)}
    r_wo = wod.rearrange("(a p) m -> p a m", p=128)

    Kf = sb("Kf", [128, 8 * SEQ], BF16)
    Qf = sb("Qf", [128, 8 * SEQ], BF16)
    VT = sb("VT", [128, 16 * D], BF16)
    wo_sb = sb("wo_sb", [128, 8 * D], BF16)
    idb = sb("idb", [128, 128], BF16)
    idf = sb("idf", [128, 128], F32)
    psw = sb("psw", [128, 128], BF16)
    Ct = sb("Ct", [128, SEQ], F32)
    St = sb("St", [128, SEQ], F32)
    epsb = sb("epsb", [128, 1], F32)
    q8b = sb("q8b", [128, 1], F32)
    stt1 = sb("stt1", [128, 8], F32)

    sm = _sems(nc, ["sC", "sln", "stp", "szf", "sPj", "sStg", "sSw",
                    "sDn", "sVr", "sSd", "sSmm", "sSe", "sNg", "sEx", "sPn",
                    "sPTb", "sPTe", "sAVa", "sAVe", "sWOc", "sYe", "sYTb",
                    "sResB", "sMtl", "sRH"])
    sm["sCW"] = S(nc, pfx + "sCW", persist=True)
    sxg = [S(nc, f"sxg{q}") for q in range(2)]
    sxq = [S(nc, f"sxq{q}") for q in range(2)]
    sstB = [S(nc, f"sstB{q}") for q in range(2)]
    swv = [S(nc, pfx + f"swv{q}", persist=True) for q in range(2)]
    VCH = Chain(nc.vector, S(nc, "chV"))
    SCH = Chain(nc.scalar, S(nc, "chS"))

    with nc.Block() as b0:
        @b0.sync
        def _(sp):
            sm["sC"].inc(sp.dma_start(idb[:], idb_d), 16)
            sm["sC"].inc(sp.dma_start(idf[:], idf_d), 16)
            sm["sC"].inc(sp.dma_start(psw[:], psw_d), 16)
            sm["sC"].inc(sp.dma_start(Ct[:], cthd), 16)
            sm["sC"].inc(sp.dma_start(St[:], sthd), 16)

        @b0.gpsimd
        def _(g):
            sm["sCW"].inc(g.dma_start(
                wo_sb[:].rearrange("p (a m) -> p a m", m=D), r_wo), 16)

        @b0.vector
        def _(v_):
            VCH.op("memset", epsb[:], EPS)
            VCH.op("memset", q8b[:], 0.125)

    for bq in range(2):
        with (
            nc.sbuf_tensor(f"{pfx}ztA{bq}", [128, 2 * D], BF16) as ztA,
            nc.sbuf_tensor(f"{pfx}zfA{bq}", [128, 8 * SEQ], BF16) as zf,
            nc.sbuf_tensor(f"{pfx}xtA{bq}", [128, 2 * D], F32) as xtA,
            nc.sbuf_tensor(f"{pfx}wbA{bq}", [128, 2 * 8 * 512], BF16) as wbA,
            nc.sbuf_tensor(f"{pfx}stg{bq}", [128, 2 * 512], BF16) as stg,
            nc.sbuf_tensor(f"{pfx}v1b{bq}", [128, 512], F32) as v1b,
            nc.sbuf_tensor(f"{pfx}v2b{bq}", [128, 512], F32) as v2b,
            nc.sbuf_tensor(f"{pfx}sqA{bq}", [128, D], F32) as sqA,
            nc.psum_tensor(f"{pfx}psA0{bq}", [128, 512], F32) as psA0,
            nc.psum_tensor(f"{pfx}psA1{bq}", [128, 512], F32) as psA1,
            nc.psum_tensor(f"{pfx}psB0{bq}", [128, 512], F32) as psB0,
            nc.psum_tensor(f"{pfx}psB1{bq}", [128, 512], F32) as psB1,
            nc.psum_tensor(f"{pfx}psTa{bq}", [128, 1024], BF16) as psTa,
            nc.psum_tensor(f"{pfx}psTb{bq}", [128, 1024], BF16) as psTb,
        ):
            psA2 = [psA0, psA1]
            psB2 = [psB0, psB1]
            psT2 = [psTa, psTb]

            def pg_info(pg):
                proj = pg // 32
                mc = (pg % 32) // 4
                nt = pg % 4
                mg = mc // 4
                return proj, mc, nt, mg

            with nc.Block() as b:
                @b.sync
                def _(sp, bq=bq):
                    for g in range(16):
                        gg = bq * 16 + g
                        if gg >= 2:
                            sp.wait_ge(sm["sln"].h, gg - 1)
                        sxg[g % 2].inc(sp.dma_start(
                            xtA[:, (g % 2) * D:(g % 2) * D + D], x_in_q[gg]), 16)

                @b.gpsimd
                def _(g, bq=bq):
                    for si in range(6):
                        SI = bq * 6 + si
                        proj = "KQV"[si // 2]
                        mg = si % 2
                        if SI >= 2:
                            g.wait_ge(sm["sPj"].h, (SI - 1) * 16)
                        swv[SI % 2].inc(g.dma_start(
                            wbA[:, (SI % 2) * 4096:(SI % 2) * 4096 + 4096]
                            .rearrange("p (a m) -> p a m", m=512),
                            r_w[proj][:, :, mg * 512:(mg + 1) * 512]), 16)

                @b.vector
                def _(v_, bq=bq):
                    v = VCH
                    for g in range(16):
                        gg = bq * 16 + g
                        v.wait_ge(sxg[g % 2].h, 16 * (gg // 2 + 1))
                        xg = xtA[:, (g % 2) * D:(g % 2) * D + D]
                        v.op("tensor_tensor", sqA[:], xg, xg, ALU.mult)
                        v.op("reduce_sum", stt1[:, 0:1], sqA[:], AX)
                        v.op("reduce_sum", stt1[:, 1:2], xg, AX)
                        v.op("tensor_scalar_mul", stt1[:, 2:3], stt1[:, 1:2], 1.0 / D)
                        v.op("tensor_scalar_mul", stt1[:, 3:4], stt1[:, 0:1], 1.0 / D)
                        v.op("tensor_tensor", stt1[:, 4:5], stt1[:, 2:3],
                             stt1[:, 2:3], ALU.mult)
                        v.op("tensor_tensor", stt1[:, 5:6], stt1[:, 3:4],
                             stt1[:, 4:5], ALU.subtract, inc=sm["sVr"])
                        v.wait_ge(sm["sSd"].h, gg + 1)
                        v.op("reciprocal", stt1[:, 7:8], stt1[:, 6:7])
                        v.op("tensor_scalar", ztA[:, (g % 2) * D:(g % 2) * D + D],
                             xg, stt1[:, 2:3], stt1[:, 7:8],
                             ALU.subtract, ALU.mult, inc=sm["sln"])
                        v.wait_ge(sm["stp"].h, gg + 1)
                        v.op("tensor_copy",
                             zf[:].rearrange("p (c t) -> p c t", t=SEQ)[:, :, g * 128:(g + 1) * 128],
                             psT2[g % 2][:, 0:1024].rearrange("p (c t) -> p c t", t=128),
                             inc=sm["szf"])
                    for pg in range(96):
                        G = bq * 96 + pg
                        proj, mc, nt, mg = pg_info(pg)
                        v.wait_ge(sm["sSw"].h, G + 1)
                        if proj < 2:
                            v.op("tensor_tensor", v1b[:],
                                 stg[:, (pg % 2) * 512:(pg % 2) * 512 + 512],
                                 Ct[:, nt * 512:(nt + 1) * 512], ALU.mult)
                            v.op("tensor_tensor", v2b[:], psB2[pg % 2][:],
                                 St[:, nt * 512:(nt + 1) * 512], ALU.mult)
                            obuf = Kf if proj == 0 else Qf
                            v.op("tensor_tensor",
                                 obuf[:, mc * SEQ + nt * 512:mc * SEQ + nt * 512 + 512],
                                 v1b[:], v2b[:], ALU.add, inc=sm["sDn"])
                        else:
                            v.op("tensor_copy",
                                 VT[:].rearrange("p (kt d) -> p kt d", d=D)
                                 [:, nt * 4:(nt + 1) * 4, mc * 128:(mc + 1) * 128],
                                 psT2[pg % 2][:, 0:512].rearrange("p (s t) -> p s t", t=128),
                                 inc=sm["sDn"])

                @b.scalar
                def _(s_, bq=bq):
                    s = SCH
                    for g in range(16):
                        gg = bq * 16 + g
                        s.wait_ge(sm["sVr"].h, gg + 1)
                        s.op("activation", stt1[:, 6:7], stt1[:, 5:6], AF.Sqrt,
                             bias=epsb[:, 0:1], inc=sm["sSd"])
                    for pg in range(96):
                        G = bq * 96 + pg
                        proj = pg // 32
                        s.wait_ge(sm["sPj"].h, G + 1)
                        if G >= 2:
                            s.wait_ge(sm["sDn"].h, G - 1)
                            s.wait_ge(sm["sSw"].h, G - 1)
                        if proj == 1:
                            s.op("activation", stg[:, (pg % 2) * 512:(pg % 2) * 512 + 512],
                                 psA2[pg % 2][:], AF.Copy, scale=q8b[:, 0:1],
                                 inc=sm["sStg"])
                        else:
                            s.op("activation", stg[:, (pg % 2) * 512:(pg % 2) * 512 + 512],
                                 psA2[pg % 2][:], AF.Copy, inc=sm["sStg"])

                @b.tensor
                def _(t, bq=bq):
                    if bq == 0:
                        t.wait_ge(sm["sC"].h, 5 * 16)
                        t.wait_ge(sm["sCW"].h, 16)

                    def emit_post(item):
                        G, pg = item
                        proj = pg // 32
                        t.wait_ge(sm["sStg"].h, G + 1)
                        if G >= 2:
                            t.wait_ge(sm["sDn"].h, G - 1)
                        if proj < 2:
                            mm = t.matmul(psB2[pg % 2][:], psw[:],
                                          stg[:, (pg % 2) * 512:(pg % 2) * 512 + 512],
                                          start=True, stop=True)
                            sm["sSw"].inc(mm)
                        else:
                            for sblk in range(4):
                                tr = t.transpose(
                                    psT2[pg % 2][:, sblk * 128:(sblk + 1) * 128],
                                    stg[:, (pg % 2) * 512 + sblk * 128:(pg % 2) * 512 + sblk * 128 + 128],
                                    idb[:])
                            sm["sSw"].inc(tr)

                    for g in range(16):
                        gg = bq * 16 + g
                        t.wait_ge(sm["sln"].h, gg + 1)
                        if gg >= 2:
                            t.wait_ge(sm["szf"].h, gg - 1)
                        for c in range(8):
                            tr = t.transpose(psT2[g % 2][:, c * 128:(c + 1) * 128],
                                             ztA[:, (g % 2) * D + c * 128:(g % 2) * D + c * 128 + 128],
                                             idb[:])
                        sm["stp"].inc(tr)
                    t.wait_ge(sm["szf"].h, 16 * (bq + 1))
                    pend = None
                    for pg in range(96):
                        G = bq * 96 + pg
                        proj, mc, nt, mg = pg_info(pg)
                        SI = bq * 6 + proj * 2 + mg
                        if pg % 16 == 0:
                            t.wait_ge(swv[SI % 2].h, 16 * (SI // 2 + 1))
                        if G >= 2:
                            t.wait_ge(sm["sStg"].h, G - 1)
                        mcl = mc % 4
                        wofs = (SI % 2) * 4096
                        for k in range(8):
                            mm = t.matmul(psA2[pg % 2][:],
                                          wbA[:, wofs + k * 512 + mcl * 128:wofs + k * 512 + mcl * 128 + 128],
                                          zf[:, k * SEQ + nt * 512:k * SEQ + nt * 512 + 512],
                                          start=(k == 0), stop=(k == 7))
                        sm["sPj"].inc(mm)
                        if pend is not None:
                            emit_post(pend)
                        pend = (G, pg)
                    emit_post(pend)

        with (
            nc.sbuf_tensor(f"{pfx}mskt{bq}", [128, 2048], F32) as mskt,
            nc.sbuf_tensor(f"{pfx}srow{bq}", [128, 2 * SEQ], F32) as srow,
            nc.sbuf_tensor(f"{pfx}Psb{bq}", [128, 2 * SEQ], BF16) as Psb,
            nc.sbuf_tensor(f"{pfx}PTsb{bq}", [128, 2 * SEQ], BF16) as PTsb,
            nc.sbuf_tensor(f"{pfx}xqb{bq}", [128, 2 * D], F32) as xqb,
            nc.sbuf_tensor(f"{pfx}xob{bq}", [128, 2 * D], F32) as xob,
            nc.sbuf_tensor(f"{pfx}avsb{bq}", [128, 2 * 8 * 128], BF16) as avsb,
            nc.sbuf_tensor(f"{pfx}zhb{bq}", [128, 4], F32) as zhb,
            nc.sbuf_tensor(f"{pfx}ngb{bq}", [128, 4], F32) as ngb,
            nc.psum_tensor(f"{pfx}psS0{bq}", [128, 512], F32) as psS0,
            nc.psum_tensor(f"{pfx}psS1{bq}", [128, 512], F32) as psS1,
            nc.psum_tensor(f"{pfx}psPTa{bq}", [128, 1024], BF16) as psPTa,
            nc.psum_tensor(f"{pfx}psPTb{bq}", [128, 1024], BF16) as psPTb,
            nc.psum_tensor(f"{pfx}psAVa{bq}", [128, 128], F32) as psAVa,
            nc.psum_tensor(f"{pfx}psAVb{bq}", [128, 128], F32) as psAVb,
            nc.psum_tensor(f"{pfx}psWO{bq}", [128, 1024], F32) as psWO,
        ):
            psS2 = [psS0, psS1]
            psPT2 = [psPTa, psPTb]
            psAV2 = [psAVa, psAVb]
            c0 = {"ch": sm["sSmm"].n, "h": sm["sPn"].n, "bat": sm["sPTb"].n,
                  "hp": sm["sAVe"].n, "j": sm["sYe"].n, "av": sm["sAVa"].n,
                  "res": sm["sResB"].n, "ytb": sm["sYTb"].n, "woc": sm["sWOc"].n,
                  "rh": sm["sRH"].n}
            jorder = []
            for k in range(8):
                jorder += [k, 15 - k]
            rank = {j: r for r, j in enumerate(jorder)}
            HL = []
            for k in range(8):
                for h in range(16):
                    HL.append((k, h))
                    HL.append((15 - k, h))

            def nch_of(j):
                return j // 4 + 1

            def nb8(j):
                return (j + 8) // 8

            chA = []; batA = []; oddA = []
            c_ch = c0["ch"]; c_bat = c0["bat"]; c_odd = 0
            for (j, h) in HL:
                chA.append(c_ch); batA.append(c_bat); oddA.append(c_odd)
                c_ch += nch_of(j); c_bat += nb8(j)
                if h % 2 == 1:
                    c_odd += 1
            av_done = {}
            c_odd = 0
            for idx, (j, h) in enumerate(HL):
                if h % 2 == 1:
                    c_odd += 1
                    if h == 15:
                        av_done[j] = c_odd

            with nc.Block() as b:
                @b.sync
                def _(sp, bq=bq):
                    sm["sMtl"].inc(sp.dma_start(mskt[:], mskd), 16)
                    for r, j in enumerate(jorder):
                        gj = bq * 16 + j
                        if r >= 2:
                            sp.wait_ge(sm["sResB"].h, c0["res"] + r - 1)
                        sxq[r % 2].inc(sp.dma_start(
                            xqb[:, (r % 2) * D:(r % 2) * D + D], x_in_q[gj]), 16)
                        sp.wait_ge(sm["sResB"].h, c0["res"] + r + 1)
                        sstB[r % 2].inc(sp.dma_start(
                            x_out_q[gj], xob[:, (r % 2) * D:(r % 2) * D + D]), 16)
                    sp.wait_ge(sstB[0].h, 16 * 8 * (bq + 1))
                    sp.wait_ge(sstB[1].h, 16 * 8 * (bq + 1))

                @b.vector
                def _(v_, bq=bq, c0=c0):
                    v = VCH

                    def sm_front(idx):
                        j, h = HL[idx]
                        par = idx % 2
                        nch = nch_of(j)
                        ch = chA[idx]
                        if idx >= 2:
                            v.wait_ge(sm["sEx"].h, c0["h"] + idx - 1)
                        for kk in range(nch):
                            bank = ch % 2
                            ch += 1
                            v.wait_ge(sm["sSmm"].h, ch)
                            if idx == 0:
                                v.wait_ge(sm["sMtl"].h, 16 * (bq + 1))
                            dst = srow[:, par * SEQ + kk * 512:par * SEQ + kk * 512 + 512]
                            if kk == j // 4:
                                v.op("scalar_tensor_tensor", dst, psS2[bank][:],
                                     1.0, mskt[:, (j % 4) * 512:(j % 4) * 512 + 512],
                                     ALU.mult, ALU.add, inc=sm["sSe"])
                            else:
                                v.op("tensor_copy", dst, psS2[bank][:], inc=sm["sSe"])
                        wS = 512 * nch
                        v.op("reduce_max", ngb[:, par:par + 1],
                             srow[:, par * SEQ:par * SEQ + wS], AX,
                             negate=True, inc=sm["sNg"])
                        v.wait_ge(sm["sEx"].h, c0["h"] + idx + 1)
                        v.op("reciprocal", ngb[:, 2 + par:3 + par], zhb[:, par:par + 1])
                        if idx >= 2:
                            v.wait_ge(sm["sPTb"].h, batA[idx - 1])
                        v.op("tensor_scalar_mul", Psb[:, par * SEQ:par * SEQ + wS],
                             srow[:, par * SEQ:par * SEQ + wS],
                             ngb[:, 2 + par:3 + par], inc=sm["sPn"])

                    def resid(j2):
                        r = rank[j2]
                        v.wait_ge(sm["sWOc"].h, c0["woc"] + 8 * (r + 1))
                        if r >= 2:
                            v.wait_ge(sstB[r % 2].h, 16 * 8 * bq + 16 * (r // 2))
                        v.wait_ge(sxq[r % 2].h, 16 * 8 * bq + 16 * (r // 2 + 1))
                        v.op("tensor_tensor", xob[:, (r % 2) * D:(r % 2) * D + D],
                             psWO[:, 0:D], xqb[:, (r % 2) * D:(r % 2) * D + D],
                             ALU.add, inc=sm["sResB"])

                    for idx in range(len(HL)):
                        sm_front(idx)
                        if idx >= 3 and HL[idx - 3][1] == 15:
                            resid(HL[idx - 3][0])
                    for idx in (len(HL) - 3, len(HL) - 2, len(HL) - 1):
                        if HL[idx][1] == 15:
                            resid(HL[idx][0])

                @b.scalar
                def _(s_, bq=bq, c0=c0):
                    s = SCH

                    def pt_evac(idx):
                        j, h = HL[idx]
                        par = idx % 2
                        if idx >= 2:
                            s.wait_ge(sm["sAVa"].h, c0["av"] + idx - 1)
                        bat = batA[idx]
                        for bi in range(nb8(j)):
                            bat += 1
                            wl = 128 * min(8, j + 1 - bi * 8)
                            s.wait_ge(sm["sPTb"].h, bat)
                            s.op("copy",
                                 PTsb[:, par * SEQ + bi * 1024:par * SEQ + bi * 1024 + wl],
                                 psPT2[par][:, 0:wl], inc=sm["sPTe"])

                    def exp_h(idx):
                        j, h = HL[idx]
                        par = idx % 2
                        wS = 512 * nch_of(j)
                        s.wait_ge(sm["sNg"].h, c0["h"] + idx + 1)
                        if idx >= 2:
                            s.wait_ge(sm["sPn"].h, c0["h"] + idx - 1)
                        s.op("activation", srow[:, par * SEQ:par * SEQ + wS],
                             srow[:, par * SEQ:par * SEQ + wS], AF.Exp,
                             bias=ngb[:, par:par + 1],
                             accum_out=zhb[:, par:par + 1], inc=sm["sEx"])

                    def post(idx):
                        j, h = HL[idx]
                        r = rank[j]
                        if h % 2 == 1:
                            hp = h // 2
                            s.wait_ge(sm["sAVa"].h, c0["av"] + idx + 1)
                            if r >= 2 and hp == 0:
                                s.wait_ge(sm["sWOc"].h, c0["woc"] + 8 * (r - 1))
                            ofs = (r % 2) * 1024 + hp * 128
                            s.op("activation", avsb[:, ofs:ofs + 128],
                                 psAV2[idx % 2][:], AF.Copy, inc=sm["sAVe"])


                    for idx in range(len(HL)):
                        if idx >= 2:
                            pt_evac(idx - 2)
                        exp_h(idx)
                        if idx >= 3:
                            post(idx - 3)
                    n = len(HL)
                    pt_evac(n - 2)
                    post(n - 3)
                    pt_evac(n - 1)
                    post(n - 2)
                    post(n - 1)

                @b.tensor
                def _(t, bq=bq, c0=c0):
                    def s_mms(idx):
                        j, h = HL[idx]
                        qsl = Qf[64 * (h % 2):64 * (h % 2) + 64,
                                 (h // 2) * SEQ + j * 128:(h // 2) * SEQ + j * 128 + 128]
                        ch = chA[idx]
                        for kk in range(nch_of(j)):
                            bank = ch % 2
                            ch += 1
                            if ch > 2:
                                t.wait_ge(sm["sSe"].h, ch - 2)
                            mm = t.matmul(psS2[bank][:], qsl,
                                          Kf[64 * (h % 2):64 * (h % 2) + 64,
                                             (h // 2) * SEQ + kk * 512:(h // 2) * SEQ + kk * 512 + 512],
                                          start=True, stop=True)
                            sm["sSmm"].inc(mm)

                    def pt_part(idx):
                        j, h = HL[idx]
                        par = idx % 2
                        t.wait_ge(sm["sPn"].h, c0["h"] + idx + 1)
                        bat = batA[idx]
                        for bi in range(nb8(j)):
                            bat += 1
                            if bi > 0:
                                t.wait_ge(sm["sPTe"].h, bat - 1)
                            elif idx >= 2:
                                t.wait_ge(sm["sPTe"].h, batA[idx - 1])
                            for s8 in range(8):
                                kt = bi * 8 + s8
                                if kt > j:
                                    break
                                tr = t.transpose(psPT2[par][:, s8 * 128:(s8 + 1) * 128],
                                                 Psb[:, par * SEQ + kt * 128:par * SEQ + kt * 128 + 128],
                                                 idb[:])
                            sm["sPTb"].inc(tr)

                    def av_part(idx):
                        j, h = HL[idx]
                        par = idx % 2
                        hp, hs = h // 2, h % 2
                        t.wait_ge(sm["sPTe"].h, batA[idx] + nb8(j))
                        if hs == 0:
                            t.wait_ge(sm["sAVe"].h, c0["hp"] + oddA[idx])
                        for kt in range(j + 1):
                            mm = t.matmul(psAV2[idx % 2][64 * hs:64 * hs + 64, :],
                                          VT[:, kt * D + hp * 128 + hs * 64:kt * D + hp * 128 + hs * 64 + 64],
                                          PTsb[:, par * SEQ + kt * 128:par * SEQ + kt * 128 + 128],
                                          start=(kt == 0), stop=(kt == j),
                                          tile_position=(0, 64 * hs))
                        sm["sAVa"].inc(mm)

                    def wo_yt(j):
                        r = rank[j]
                        t.wait_ge(sm["sAVe"].h, c0["hp"] + av_done[j])
                        if r >= 1:
                            t.wait_ge(sm["sResB"].h, c0["res"] + r)
                        for m in range(8):
                            for hp in range(8):
                                mm = t.matmul(psWO[:, m * 128:(m + 1) * 128],
                                              avsb[:, (r % 2) * 1024 + hp * 128:(r % 2) * 1024 + hp * 128 + 128],
                                              wo_sb[:, hp * D + m * 128:hp * D + m * 128 + 128],
                                              start=(hp == 0), stop=(hp == 7))
                            sm["sWOc"].inc(mm)

                    for idx in range(len(HL)):
                        if idx >= 2:
                            pt_part(idx - 2)
                        if idx >= 3:
                            av_part(idx - 3)
                        s_mms(idx)
                        if idx >= 3 and HL[idx - 3][1] == 15:
                            wo_yt(HL[idx - 3][0])
                    n = len(HL)
                    pt_part(n - 2)
                    av_part(n - 3)
                    pt_part(n - 1)
                    av_part(n - 2)
                    wo_yt(HL[n - 2][0])
                    av_part(n - 1)
                    wo_yt(HL[n - 1][0])

    with nc.Block() as bf:
        @bf.sync
        def _(sp):
            sp.wait_ge(sstB[0].h, 16 * 16)
            sp.wait_ge(sstB[1].h, 16 * 16)

    _st.close()
    _emit_sem_reset(nc)


def build_l1attn():
    nc = bass.Bass("TRN2", target_bir_lowering=False)
    x_in = nc.dram_tensor("x_in", [TPC, D], F32, kind="ExternalInput").ap()
    x_out = nc.dram_tensor("x_out", [TPC, D], F32, kind="ExternalOutput").ap()
    wd = _declare_io(nc, L1A_KEYS, _SHAPES)
    emit_l1attn(nc, "t_", x_in, x_out, wd)
    return nc


FUSE_PARTS = [("ffn0", "f1_"), ("l0attn", "a1_"), ("ffn0", "f2_"),
              ("l1ffn", "m1_"), ("l1attn", "t1_"), ("l1ffn", "m2_")]
_EMITTERS = {"ffn0": None, "l0attn": None, "l1ffn": None, "l1attn": None}


def build_fused(parts=None):
    if parts is None:
        parts = FUSE_PARTS
    emitters = {"ffn0": emit_ffn0, "l0attn": emit_l0attn,
                "l1ffn": emit_l1ffn, "l1attn": emit_l1attn}
    nc = bass.Bass("TRN2", target_bir_lowering=False)
    x_in = nc.dram_tensor("x_in", [TPC, D], F32, kind="ExternalInput").ap()
    x_out = nc.dram_tensor("x_out", [TPC, D], F32, kind="ExternalOutput").ap()
    wd = _declare_io(nc, sorted(set(FFN0_KEYS + L0_KEYS + L1F_KEYS + L1A_KEYS)),
                     _SHAPES)
    wscr = {k: nc.dram_tensor(k + "_b16", _SHAPES[k][0], BF16).ap()
            for k in ("w1T", "w2T", "h1T", "h2T", "w3T")}
    cur = x_in
    seen_ffn = False
    for i, (name, pfx) in enumerate(parts):
        dst = (x_out if i == len(parts) - 1
               else nc.dram_tensor(f"xs{i}", [TPC, D], F32).ap())
        if name == "ffn0":
            emitters[name](nc, pfx, cur, dst, wd, wscr=wscr, first=not seen_ffn)
            seen_ffn = True
        else:
            emitters[name](nc, pfx, cur, dst, wd)
        cur = dst
    return nc



def _stage_common(inp):
    st = {}
    f32 = np.float32

    def tr(a):
        return np.ascontiguousarray(np.asarray(a, f32).T)

    st["w1T"] = tr(inp["L0_ffn_w1_w"])
    st["w2T"] = tr(inp["L0_ffn_w2_w"])
    st["h1T"] = tr(inp["L0_ffn_h1_w"])
    st["h2T"] = tr(inp["L0_ffn_h2_w"])
    st["w3T"] = tr(inp["L0_ffn_w3_w"])
    for nm, key, nchunk in (("s1c", "L0_ffn_w1_s", 32), ("s2c", "L0_ffn_w2_s", 32),
                            ("sh1c", "L0_ffn_h1_s", 32), ("sh2c", "L0_ffn_h2_s", 32),
                            ("s3c", "L0_ffn_w3_s", 8)):
        st[nm] = np.ascontiguousarray(
            np.asarray(inp[key], f32).reshape(nchunk, 128).T)
    st["identb"] = np.eye(128, dtype=BF)
    st["identf"] = np.eye(128, dtype=f32)
    st["WthT"] = tr(inp["L0_attn_W_theta"])
    st["WphT"] = tr(inp["L0_attn_W_phi"])
    st["WgT"] = tr(inp["L0_attn_W_g"])
    A = np.asarray(inp["L1_ffn_A"], f32)
    Bw = np.asarray(inp["L1_ffn_Bw"], f32)
    st["WrT"] = tr(inp["L1_ffn_Wr"])
    st["ATall"] = np.ascontiguousarray(
        np.concatenate([np.concatenate([A[2 * p].T, A[2 * p + 1].T], axis=1)
                        for p in range(4)], axis=1))
    st["Bpk"] = np.ascontiguousarray(
        np.concatenate([np.concatenate([Bw[2 * p].T, Bw[2 * p + 1].T], axis=0)
                        for p in range(4)], axis=0))
    E = np.zeros((8, 512), np.float32)
    for p in range(4):
        for r in range(128):
            E[2 * p + r // 64, p * 128 + r] = 1.0
    st["Eall"] = E
    st["WQT"] = tr(inp["L1_attn_W_Q"])
    st["WKT"] = tr(inp["L1_attn_W_K"])
    st["WVT"] = tr(inp["L1_attn_W_V"])
    st["WOT"] = tr(inp["L1_attn_W_O"])
    kk = np.arange(32, dtype=f32)
    invf = np.power(np.float32(10000.0), (np.float32(-2.0) * kk / np.float32(64.0)),
                    dtype=f32)
    pos = np.arange(SEQ, dtype=f32)
    ang = pos[None, :] * invf[:, None]
    cosv = np.cos(ang).astype(f32)
    sinv = np.sin(ang).astype(f32)
    C = np.empty((128, SEQ), f32)
    Sg = np.empty((128, SEQ), f32)
    for r in range(128):
        d = r % 64
        k = d // 2
        C[r] = cosv[k]
        Sg[r] = -sinv[k] if d % 2 == 0 else sinv[k]
    st["Ctab"] = C
    st["Stab"] = Sg
    M = np.zeros((128, 2048), f32)
    r_ = np.arange(128)[:, None]
    for jj in range(4):
        u = np.arange(512)[None, :]
        M[:, jj * 512:(jj + 1) * 512] = np.where(u <= 128 * jj + r_, 0.0, -1e30)
    st["Mtab"] = M
    P = np.zeros((128, 128), f32)
    for i in range(128):
        P[i, i ^ 1] = 1.0
    st["Pswap"] = P.astype(BF)
    return st


def _run(nc, in_maps, out_key):
    res = run_bass_kernel_spmd(nc, in_maps, core_ids=list(range(NCORES)))
    return [np.asarray(res.results[c][out_key]) for c in range(NCORES)]


_CACHE = {}


def _get(name, builder):
    if name not in _CACHE:
        _CACHE[name] = builder()
    return _CACHE[name]


_KEYS = {
    "ffn0": ["w1T", "w2T", "h1T", "h2T", "w3T", "s1c", "s2c", "sh1c", "sh2c",
             "s3c", "identb", "identf"],
    "l0attn": ["WthT", "WphT", "WgT", "identb"],
    "l1ffn": ["WrT", "ATall", "Bpk", "Eall", "identb", "identf"],
    "l1attn": ["WQT", "WKT", "WVT", "WOT", "Ctab", "Stab", "Mtab",
               "identb", "identf", "Pswap"],
}
_BUILDERS = {"ffn0": build_ffn0, "l0attn": build_l0attn,
             "l1ffn": build_l1ffn, "l1attn": build_l1attn}


def kernel(**inp):
    st = _stage_common(inp)
    x = np.asarray(inp["L0_attn_x"], np.float32).reshape(NCORES, TPC, D)
    shards = [np.ascontiguousarray(x[c]) for c in range(NCORES)]

    nc = _get("fused", build_fused)
    allk = sorted(set(_KEYS["ffn0"] + _KEYS["l0attn"] + _KEYS["l1ffn"] +
                      _KEYS["l1attn"]))
    maps = [dict({k: st[k] for k in allk}, x_in=shards[c])
            for c in range(NCORES)]
    shards = _run(nc, maps, "x_out")
    out = np.stack(shards).reshape(16, SEQ, D).astype(np.float32)
    return out


# version 11
# speedup vs baseline: 1.2011x; 1.2011x over previous
import contextlib
import numpy as np
import ml_dtypes
import concourse.bass as bass
import concourse.mybir as mybir
from concourse.bass_utils import run_bass_kernel_spmd

F32 = mybir.dt.float32
BF16 = mybir.dt.bfloat16
AF = mybir.ActivationFunctionType
ALU = mybir.AluOpType
AX = mybir.AxisListType.X
BF = ml_dtypes.bfloat16

NCORES = 8
SEQ = 2048
D = 1024
TPC = 2 * SEQ
TT = 512
NTILES = TPC // TT
DH = 4096
EPS = 1e-5
DEBUG_TAPS = False


def _reg_sem(nc, name, persist=False):
    attr = "_sem_persist" if persist else "_sem_registry"
    reg = getattr(nc, attr, None)
    if reg is None:
        reg = {}
        setattr(nc, attr, reg)
    if name not in reg:
        reg[name] = nc.alloc_semaphore(name)
    return reg[name]


def _emit_sem_reset(nc):
    with nc.Block() as rb:
        @rb.sync
        def _(sp):
            for h in nc._sem_registry.values():
                sp.sem_clear(h)


class S:
    def __init__(self, nc, name, persist=False):
        self.h = _reg_sem(nc, name, persist)
        self.n = 0

    def inc(self, instr, k=1):
        instr.then_inc(self.h, k)
        self.n += k
        return instr


def _sems(nc, names):
    return {nm: S(nc, nm) for nm in names}


class Chain:

    def __init__(self, eng, util_sem):
        self.e = eng
        self._u = util_sem
        self._last = None

    def wait_ge(self, h, v):
        self.e.wait_ge(h, v)

    def op(self, name, *a, inc=None, k=1, **kw):
        if self._last is not None:
            self.e.wait_ge(*self._last)
        instr = getattr(self.e, name)(*a, **kw)
        if inc is None:
            self._u.inc(instr, 1)
            self._last = (self._u.h, self._u.n)
        else:
            inc.inc(instr, k)
            self._last = (inc.h, inc.n)
        return instr



FFN0_KEYS = ["w1T", "w2T", "h1T", "h2T", "w3T", "s1c", "s2c", "sh1c", "sh2c",
             "s3c", "identb", "identf"]


def emit_ffn0(nc, pfx, x_in, x_out, wd, wscr=None, first=True):
    _st = contextlib.ExitStack()
    sb = lambda n, s, d: _st.enter_context(nc.sbuf_tensor(pfx + n, s, d))
    pst = lambda n, s, d: _st.enter_context(nc.psum_tensor(pfx + n, s, d))
    w1T = wd["w1T"]; w2T = wd["w2T"]; h1T = wd["h1T"]; h2T = wd["h2T"]
    w3T = wd["w3T"]; s1d = wd["s1c"]; s2d = wd["s2c"]; sh1d = wd["sh1c"]
    sh2d = wd["sh2c"]; s3d = wd["s3c"]; idb_d = wd["identb"]; idf_d = wd["identf"]

    x_in_t = x_in.rearrange("(t g p) f -> t p g f", g=4, p=128)
    x_out_t = x_out.rearrange("(t g p) f -> t p g f", g=4, p=128)
    r_w1 = w1T.rearrange("(a p) m -> p a m", p=128)
    r_w2 = w2T.rearrange("(a p) m -> p a m", p=128)
    r_h1 = h1T.rearrange("(a p) m -> p a m", p=128)
    r_h2 = h2T.rearrange("(a p) m -> p a m", p=128)
    r_w3 = w3T.rearrange("(a p) m -> p a m", p=128)

    xt = [sb(f"xt{q}", [128, 4 * D], F32) for q in range(2)]
    xo = sb("xo", [128, 4 * D], F32)
    zt = sb("zt", [128, 4 * D], BF16)
    zf = sb("zf", [128, 8 * TT], BF16)
    sq = sb("sq", [128, D], F32)
    ha = sb("ha", [128, 32 * TT], BF16)
    hb = sb("hb", [128, 32 * TT], BF16)
    yf = sb("yf", [128, 8 * TT], F32)
    wbuf = [sb(f"wbuf{q}", [128, 8 * 512], BF16) for q in range(4)]
    t1 = [sb(f"t1{q}", [128, TT], BF16) for q in range(2)]
    t2 = [sb(f"t2{q}", [128, TT], BF16) for q in range(2)]
    s1c = sb("s1cs", [128, 32], F32)
    s2c = sb("s2cs", [128, 32], F32)
    sh1c = sb("sh1cs", [128, 32], F32)
    sh2c = sb("sh2cs", [128, 32], F32)
    s3c = sb("s3cs", [128, 8], F32)
    idb = sb("idb", [128, 128], BF16)
    idf = sb("idf", [128, 128], F32)
    epsb = sb("epsb", [128, 1], F32)
    sums = sb("sums", [128, 4], F32)
    sumsq = sb("sumsq", [128, 4], F32)
    mu = sb("mu", [128, 4], F32)
    ex2 = sb("ex2", [128, 4], F32)
    musq = sb("musq", [128, 4], F32)
    var = sb("var", [128, 4], F32)
    stdv = sb("stdv", [128, 4], F32)
    istd = sb("istd", [128, 4], F32)

    psA = [pst(f"psA{q}", [128, TT], F32) for q in range(4)]
    psW = pst("psW", [128, D], F32)
    psT = [pst(f"psT{q}", [128, 1024], BF16) for q in range(2)]

    sm = _sems(nc, ["sst", "sKG", "sln", "stp", "szf", "sA1", "sA2",
                    "sE1", "sE2", "sPr", "sM1", "sH1", "sM2", "sH2", "sM3",
                    "sYf", "sYT", "sRes", "sVr", "sSd", "sC"])
    sw4 = [S(nc, pfx + f"sw{q}", persist=True) for q in range(4)]
    swb4 = [S(nc, f"swb4_{q}") for q in range(4)]
    sx2 = [S(nc, f"sx{q}") for q in range(2)]
    VC = Chain(nc.vector, S(nc, "chV"))
    SC = Chain(nc.scalar, S(nc, "chS"))

    if wscr is not None:
        rb_w1 = wscr["w1T"].rearrange("(a p) m -> p a m", p=128)
        rb_w2 = wscr["w2T"].rearrange("(a p) m -> p a m", p=128)
        rb_h1 = wscr["h1T"].rearrange("(a p) m -> p a m", p=128)
        rb_h2 = wscr["h2T"].rearrange("(a p) m -> p a m", p=128)
        rb_w3 = wscr["w3T"].rearrange("(a p) m -> p a m", p=128)

    stripes = []
    for mg in range(8):
        stripes.append(("w1", mg, 0))
        stripes.append(("w2", mg, 0))
    for mg in range(8):
        for kg in range(4):
            stripes.append(("h1", mg, kg))
    for mg in range(8):
        for kg in range(4):
            stripes.append(("h2", mg, kg))
    for mg in range(2):
        for kg in range(4):
            stripes.append(("w3", mg, kg))
    NS = len(stripes)
    stripe_free = {}

    def stripe_src(kind, mg, kg, f32src):
        if f32src:
            t = {"w1": r_w1, "w2": r_w2, "h1": r_h1, "h2": r_h2, "w3": r_w3}[kind]
        else:
            t = {"w1": rb_w1, "w2": rb_w2, "h1": rb_h1, "h2": rb_h2,
                 "w3": rb_w3}[kind]
        if kind in ("w1", "w2"):
            return t[:, :, mg * 512:(mg + 1) * 512]
        return t[:, kg * 8:(kg + 1) * 8, mg * 512:(mg + 1) * 512]

    with nc.Block() as b:
        @b.sync
        def _(sp):
            sp.dma_start(s1c[:], s1d).then_inc(sm["sC"].h, 16)
            sp.dma_start(s2c[:], s2d).then_inc(sm["sC"].h, 16)
            sp.dma_start(sh1c[:], sh1d).then_inc(sm["sC"].h, 16)
            sp.dma_start(sh2c[:], sh2d).then_inc(sm["sC"].h, 16)
            sp.dma_start(s3c[:], s3d).then_inc(sm["sC"].h, 16)
            sp.dma_start(idb[:], idb_d).then_inc(sm["sC"].h, 16)
            sp.dma_start(idf[:], idf_d).then_inc(sm["sC"].h, 16)
            sm["sC"].n = 7 * 16

        for i in range(NTILES):
            @b.sync
            def _(sp, i=i):
                if i >= 2:
                    sp.wait_ge(sm["sRes"].h, 4 * (i - 1))
                sx2[i % 2].inc(sp.dma_start(xt[i % 2][:].rearrange("p (g f) -> p g f", f=D),
                                            x_in_t[i]), 16)
                if i == 0 and wscr is not None and first:
                    for idx, (kind, mg, kg) in enumerate(stripes):
                        sp.wait_ge(sw4[idx % 4].h, 16 * (idx // 4 + 1))
                        dstw = stripe_src(kind, mg, kg, False)
                        swb4[idx % 4].inc(sp.dma_start(
                            dstw,
                            wbuf[idx % 4][:].rearrange("p (a m) -> p a m", m=512)), 16)
                sp.wait_ge(sm["sRes"].h, 4 * (i + 1))
                sm["sst"].inc(sp.dma_start(x_out_t[i],
                                           xo[:].rearrange("p (g f) -> p g f", f=D)), 16)

            @b.vector
            def _(v_, i=i):
                v = VC
                if i == 0:
                    v.wait_ge(sm["sC"].h, 7 * 16)
                    v.op("memset", epsb[:], EPS)
                v.wait_ge(sx2[i % 2].h, 16 * (i // 2 + 1))
                for g in range(4):
                    xg = xt[i % 2][:, g * D:(g + 1) * D]
                    v.op("tensor_tensor", sq[:], xg, xg, ALU.mult)
                    v.op("reduce_sum", sumsq[:, g:g + 1], sq[:], AX)
                    v.op("reduce_sum", sums[:, g:g + 1], xg, AX)
                v.op("tensor_scalar_mul", mu[:], sums[:], 1.0 / D)
                v.op("tensor_scalar_mul", ex2[:], sumsq[:], 1.0 / D)
                v.op("tensor_tensor", musq[:], mu[:], mu[:], ALU.mult)
                v.op("tensor_tensor", var[:], ex2[:], musq[:], ALU.subtract,
                     inc=sm["sVr"])
                v.wait_ge(sm["sSd"].h, i + 1)
                v.op("reciprocal", istd[:], stdv[:])
                v.wait_ge(sm["stp"].h, 8 * i)
                for g in range(4):
                    v.op("tensor_scalar",
                         zt[:, g * D:(g + 1) * D], xt[i % 2][:, g * D:(g + 1) * D],
                         mu[:, g:g + 1], istd[:, g:g + 1], ALU.subtract, ALU.mult,
                         inc=sm["sln"])
                v.wait_ge(sm["sA2"].h, 32 * i)
                for c in range(8):
                    v.wait_ge(sm["stp"].h, i * 8 + c + 1)
                    v.op("tensor_copy", zf[:, c * TT:(c + 1) * TT],
                         psT[c % 2][:, 0:TT], inc=sm["szf"])
                v.wait_ge(sm["sM3"].h, 8 * i)
                for mci in range(32):
                    G = i * 32 + mci
                    v.wait_ge(sm["sA2"].h, G + 1)
                    v.op("tensor_scalar_mul", t2[mci % 2][:],
                         psA[2 + mci % 2][:], s2c[:, mci:mci + 1], inc=sm["sE2"])
                    v.wait_ge(sm["sE1"].h, G + 1)
                    v.op("tensor_tensor", ha[:, mci * TT:(mci + 1) * TT],
                         t1[mci % 2][:], t2[mci % 2][:], ALU.mult, inc=sm["sPr"])
                if i >= 1:
                    v.wait_ge(sm["sst"].h, 16 * i)
                for g in range(4):
                    v.wait_ge(sm["sYT"].h, i * 4 + g + 1)
                    v.op("scalar_tensor_tensor",
                         xo[:, g * D:(g + 1) * D], psW[:, 0:D], 0.5,
                         xt[i % 2][:, g * D:(g + 1) * D], ALU.mult, ALU.add,
                         inc=sm["sRes"])

            @b.scalar
            def _(s_, i=i):
                s = SC
                if i == 0:
                    s.wait_ge(sm["sC"].h, 7 * 16)
                s.wait_ge(sm["sVr"].h, i + 1)
                s.op("activation", stdv[:], var[:], AF.Sqrt, bias=epsb[:, 0:1],
                     inc=sm["sSd"])
                s.wait_ge(sm["sYT"].h, 4 * i)
                for mci in range(32):
                    G = i * 32 + mci
                    s.wait_ge(sm["sA1"].h, G + 1)
                    if G >= 2:
                        s.wait_ge(sm["sPr"].h, G - 1)
                    s.op("activation", t1[mci % 2][:], psA[mci % 2][:],
                         AF.Silu, scale=s1c[:, mci:mci + 1], inc=sm["sE1"])
                s.wait_ge(sm["sM2"].h, 32 * i)
                for mci in range(32):
                    s.wait_ge(sm["sM1"].h, i * 32 + mci + 1)
                    s.op("activation", hb[:, mci * TT:(mci + 1) * TT],
                         psA[mci % 4][:], AF.Gelu, scale=sh1c[:, mci:mci + 1],
                         inc=sm["sH1"])
                for mci in range(32):
                    s.wait_ge(sm["sM2"].h, i * 32 + mci + 1)
                    s.op("activation", ha[:, mci * TT:(mci + 1) * TT],
                         psA[mci % 4][:], AF.Gelu, scale=sh2c[:, mci:mci + 1],
                         inc=sm["sH2"])
                for mci in range(8):
                    s.wait_ge(sm["sM3"].h, i * 8 + mci + 1)
                    s.op("activation", yf[:, mci * TT:(mci + 1) * TT],
                         psA[mci % 4][:], AF.Copy, scale=s3c[:, mci:mci + 1],
                         inc=sm["sYf"])

            @b.tensor
            def _(t, i=i):
                J0 = i * NS
                if i == 0:
                    t.wait_ge(sm["sC"].h, 7 * 16)
                t.wait_ge(sm["sln"].h, 4 * (i + 1))
                for c in range(8):
                    C = i * 8 + c
                    if C >= 2:
                        t.wait_ge(sm["szf"].h, C - 1)
                    for g in range(4):
                        tr = t.transpose(psT[c % 2][:, g * 128:(g + 1) * 128],
                                         zt[:, g * D + c * 128:g * D + c * 128 + 128],
                                         idb[:])
                    sm["stp"].inc(tr)
                t.wait_ge(sm["szf"].h, 8 * (i + 1))
                t.wait_ge(sm["sYf"].h, 8 * i)
                for mg in range(8):
                    Jw1 = J0 + 2 * mg
                    Jw2 = Jw1 + 1
                    t.wait_ge(sw4[Jw1 % 4].h, 16 * (Jw1 // 4 + 1))
                    t.wait_ge(sw4[Jw2 % 4].h, 16 * (Jw2 // 4 + 1))
                    for mc in range(4):
                        mci = mg * 4 + mc
                        G = i * 32 + mci
                        if G >= 2:
                            t.wait_ge(sm["sE1"].h, G - 1)
                        for k in range(8):
                            mm = t.matmul(psA[mci % 2][:],
                                          wbuf[Jw1 % 4][:, k * 512 + mc * 128:k * 512 + mc * 128 + 128],
                                          zf[:, k * TT:(k + 1) * TT],
                                          start=(k == 0), stop=(k == 7))
                        sm["sA1"].inc(mm)
                        if G >= 2:
                            t.wait_ge(sm["sE2"].h, G - 1)
                        for k in range(8):
                            mm = t.matmul(psA[2 + mci % 2][:],
                                          wbuf[Jw2 % 4][:, k * 512 + mc * 128:k * 512 + mc * 128 + 128],
                                          zf[:, k * TT:(k + 1) * TT],
                                          start=(k == 0), stop=(k == 7))
                        sm["sA2"].inc(mm)
                    stripe_free[Jw1] = (sm["sE1"], i * 32 + mg * 4 + 4)
                    stripe_free[Jw2] = (sm["sE2"], i * 32 + mg * 4 + 4)
                for kind, s_in, sMx, sHx, base in (("h1", ha, sm["sM1"], sm["sH1"], J0 + 16),
                                                   ("h2", hb, sm["sM2"], sm["sH2"], J0 + 48)):
                    if kind == "h1":
                        t.wait_ge(sm["sPr"].h, 32 * (i + 1))
                    else:
                        t.wait_ge(sm["sH1"].h, 32 * (i + 1))
                    for mg in range(8):
                        for kg in range(4):
                            J = base + mg * 4 + kg
                            t.wait_ge(sw4[J % 4].h, 16 * (J // 4 + 1))
                            for mc in range(4):
                                mci = mg * 4 + mc
                                if kg == 0 and mg >= 1:
                                    t.wait_ge(sHx.h, i * 32 + (mg - 1) * 4 + mc + 1)
                                for k in range(8):
                                    mm = t.matmul(
                                        psA[mc][:],
                                        wbuf[J % 4][:, k * 512 + mc * 128:k * 512 + mc * 128 + 128],
                                        s_in[:, (kg * 8 + k) * TT:(kg * 8 + k + 1) * TT],
                                        start=(kg == 0 and k == 0),
                                        stop=(kg == 3 and k == 7))
                                if kg == 3:
                                    sMx.inc(mm)
                            if kg < 3:
                                sm["sKG"].inc(mm)
                                stripe_free[J] = (sm["sKG"], sm["sKG"].n)
                            else:
                                stripe_free[J] = (sMx, i * 32 + mg * 4 + 4)
                t.wait_ge(sm["sH2"].h, 32 * (i + 1))
                for mg in range(2):
                    for kg in range(4):
                        J = J0 + 80 + mg * 4 + kg
                        t.wait_ge(sw4[J % 4].h, 16 * (J // 4 + 1))
                        for mc in range(4):
                            mci = mg * 4 + mc
                            if kg == 0 and mg >= 1:
                                t.wait_ge(sm["sYf"].h, i * 8 + (mg - 1) * 4 + mc + 1)
                            for k in range(8):
                                mm = t.matmul(
                                    psA[mc][:],
                                    wbuf[J % 4][:, k * 512 + mc * 128:k * 512 + mc * 128 + 128],
                                    ha[:, (kg * 8 + k) * TT:(kg * 8 + k + 1) * TT],
                                    start=(kg == 0 and k == 0),
                                    stop=(kg == 3 and k == 7))
                            if kg == 3:
                                sm["sM3"].inc(mm)
                        if kg < 3:
                            sm["sKG"].inc(mm)
                            stripe_free[J] = (sm["sKG"], sm["sKG"].n)
                        else:
                            stripe_free[J] = (sm["sM3"], i * 8 + mg * 4 + 4)
                t.wait_ge(sm["sYf"].h, 8 * (i + 1))
                for g in range(4):
                    t.wait_ge(sm["sRes"].h, 4 * i + g)
                    for c in range(8):
                        tr = t.transpose(psW[:, c * 128:(c + 1) * 128],
                                         yf[:, c * TT + g * 128:c * TT + g * 128 + 128],
                                         idf[:])
                    sm["sYT"].inc(tr)

            @b.gpsimd
            def _(g, i=i):
                wb_on = wscr is not None and first
                for idx, (kind, mg, kg) in enumerate(stripes):
                    J = i * NS + idx
                    if J >= 4:
                        fs, fv = stripe_free[J - 4]
                        g.wait_ge(fs.h, fv)
                        if wb_on and J - 4 < NS:
                            g.wait_ge(swb4[(J - 4) % 4].h,
                                      16 * ((J - 4) // 4 + 1))
                    f32src = (wscr is None) or (first and i == 0)
                    src = stripe_src(kind, mg, kg, f32src)
                    dst = wbuf[J % 4][:].rearrange("p (a m) -> p a m", m=512)
                    sw4[J % 4].inc(g.dma_start(dst, src), 16)

        if DEBUG_TAPS:
            @b.gpsimd
            def _(g):
                g.wait_ge(sm["sst"].h, 16 * NTILES)
                for nm, buf in (("d_zt", zt), ("d_zf", zf), ("d_ha", ha),
                                ("d_hb", hb), ("d_yf", yf), ("d_xt", xt[(NTILES - 1) % 2])):
                    dt_ = nc.dram_tensor(nm, list(buf[:].shape), buf.dtype,
                                         kind="ExternalOutput").ap()
                    g.dma_start(dt_, buf[:]).then_inc(sm["sC"].h, 16)
                g.wait_ge(sm["sC"].h, sm["sC"].n + 6 * 16)

        @b.sync
        def _(sp):
            sp.wait_ge(sm["sst"].h, 16 * NTILES)

    _st.close()
    _emit_sem_reset(nc)


def _declare_io(nc, keys, shapes):
    wd = {}
    for k in keys:
        shp, dt = shapes[k]
        wd[k] = nc.dram_tensor(k, shp, dt, kind="ExternalInput").ap()
    return wd


_SHAPES = {
    "w1T": ([D, DH], F32), "w2T": ([D, DH], F32), "h1T": ([DH, DH], F32),
    "h2T": ([DH, DH], F32), "w3T": ([DH, D], F32), "s1c": ([128, 32], F32),
    "s2c": ([128, 32], F32), "sh1c": ([128, 32], F32), "sh2c": ([128, 32], F32),
    "s3c": ([128, 8], F32), "identb": ([128, 128], BF16), "identf": ([128, 128], F32),
    "WthT": ([D, 256], F32), "WphT": ([D, 256], F32), "WgT": ([D, D], F32),
    "WrT": ([D, 8], F32), "ATall": ([D, 512], F32), "Bpk": ([512, D], F32),
    "Eall": ([8, 512], F32),
    "WQT": ([D, D], F32), "WKT": ([D, D], F32), "WVT": ([D, D], F32),
    "WOT": ([D, D], F32), "Ctab": ([128, SEQ], F32), "Stab": ([128, SEQ], F32),
    "Mtab": ([128, 2048], F32), "Mtabb": ([128, 2048], BF16),
    "Pswap": ([128, 128], BF16),
}


def build_ffn0():
    nc = bass.Bass("TRN2", target_bir_lowering=False)
    x_in = nc.dram_tensor("x_in", [TPC, D], F32, kind="ExternalInput").ap()
    x_out = nc.dram_tensor("x_out", [TPC, D], F32, kind="ExternalOutput").ap()
    wd = _declare_io(nc, FFN0_KEYS, _SHAPES)
    emit_ffn0(nc, "f_", x_in, x_out, wd)
    return nc




L0_KEYS = ["WthT", "WphT", "WgT", "identb"]


def emit_l0attn(nc, pfx, x_in, x_out, wd):
    _st = contextlib.ExitStack()
    sb = lambda n, s, d: _st.enter_context(nc.sbuf_tensor(pfx + n, s, d))
    pst = lambda n, s, d: _st.enter_context(nc.psum_tensor(pfx + n, s, d))
    wthd = wd["WthT"]; wphd = wd["WphT"]; wgd = wd["WgT"]; idb_d = wd["identb"]

    x_in_t = x_in.rearrange("(t g p) f -> t p g f", g=4, p=128)
    x_in_q = x_in.rearrange("(q p) f -> q p f", p=128)
    x_out_q = x_out.rearrange("(q p) f -> q p f", p=128)
    r_wth = wthd.rearrange("(a p) m -> p a m", p=128)
    r_wph = wphd.rearrange("(a p) m -> p a m", p=128)
    r_wg = wgd.rearrange("(a p) m -> p a m", p=128)

    xt2 = [sb(f"xt2{q}", [128, 4 * D], F32) for q in range(2)]
    xq = [sb(f"xq{q}", [128, D], F32) for q in range(2)]
    xoq = [sb(f"xoq{q}", [128, D], F32) for q in range(2)]
    zt = sb("zt", [128, 4 * D], BF16)
    zf = sb("zf", [128, 8 * SEQ], BF16)
    sq = sb("sq", [128, D], F32)
    th = sb("th", [128, 2 * SEQ], BF16)
    ph = sb("ph", [128, 2 * SEQ], BF16)
    gf = sb("gf", [128, 8 * SEQ], BF16)
    phT = sb("phT", [128, 16 * 256], BF16)
    gT = sb("gT", [128, 16 * D], BF16)
    m_sb = sb("m_sb", [128, 2 * D], BF16)
    wpb = sb("wpb", [128, 8 * D], BF16)
    idb = sb("idb", [128, 128], BF16)
    msc = sb("msc", [128, 1], F32)
    sums = sb("sums", [128, 4], F32)
    sumsq = sb("sumsq", [128, 4], F32)
    mu = sb("mu", [128, 4], F32)
    ex2 = sb("ex2", [128, 4], F32)
    musq = sb("musq", [128, 4], F32)
    var = sb("var", [128, 4], F32)
    stdv = sb("stdv", [128, 4], F32)
    istd = sb("istd", [128, 4], F32)
    epsb = sb("epsb", [128, 1], F32)

    psA = [pst(f"psA{q}", [128, 512], F32) for q in range(4)]
    psW = pst("psW", [128, D], F32)
    psT = [pst(f"psT{q}", [128, 1024], BF16) for q in range(2)]

    sm = _sems(nc, ["sC", "sst", "sln", "stp", "szf", "sMM", "sEv", "sTB",
                    "sTE", "sMs", "sVr", "sSd", "sRes"])
    sxt = [S(nc, f"sxt{q}") for q in range(2)]
    sst2 = [S(nc, f"sst2{q}") for q in range(2)]
    sxq = [S(nc, f"sxq{q}") for q in range(2)]
    swb = S(nc, pfx + "swb", persist=True)
    VC = Chain(nc.vector, S(nc, "chV"))
    SC = Chain(nc.scalar, S(nc, "chS"))

    def wview(mwid):
        return wpb[:, 0:8 * mwid].rearrange("p (a m) -> p a m", m=mwid)

    with nc.Block() as b0:
        @b0.sync
        def _(sp):
            sm["sC"].inc(sp.dma_start(idb[:], idb_d), 16)

        @b0.vector
        def _(v_):
            VC.op("memset", epsb[:], EPS)
            VC.op("memset", msc[:], 1.0 / 32.0)

    for bq in range(2):
        with nc.Block() as b:
            @b.sync
            def _(sp, bq=bq):
                for tau in range(4):
                    gt = bq * 4 + tau
                    if gt >= 2:
                        sp.wait_ge(sm["sln"].h, 4 * (gt - 1))
                    sxt[tau % 2].inc(sp.dma_start(
                        xt2[tau % 2][:].rearrange("p (g f) -> p g f", f=D),
                        x_in_t[gt]), 16)
                for qt in range(16):
                    gq = bq * 16 + qt
                    if gq >= 2:
                        sp.wait_ge(sm["sRes"].h, 2 * (gq - 1))
                    sxq[qt % 2].inc(sp.dma_start(xq[qt % 2][:], x_in_q[gq]), 16)
                    sp.wait_ge(sm["sRes"].h, 2 * (gq + 1))
                    sst2[qt % 2].inc(sp.dma_start(x_out_q[gq], xoq[qt % 2][:]), 16)

            @b.gpsimd
            def _(g, bq=bq):
                if bq == 0:
                    g.wait_ge(sm["sC"].h, 16)
                swb.inc(g.dma_start(wview(256), r_wth), 16)
                g.wait_ge(sm["sMM"].h, bq * 52 + 8)
                swb.inc(g.dma_start(wview(256), r_wph), 16)
                g.wait_ge(sm["sMM"].h, bq * 52 + 16)
                swb.inc(g.dma_start(wview(D), r_wg), 16)

            @b.vector
            def _(v_, bq=bq):
                v = VC
                for tau in range(4):
                    gt = bq * 4 + tau
                    v.wait_ge(sxt[tau % 2].h, 16 * (gt // 2 + 1))
                    for g in range(4):
                        xg = xt2[tau % 2][:, g * D:(g + 1) * D]
                        v.op("tensor_tensor", sq[:], xg, xg, ALU.mult)
                        v.op("reduce_sum", sumsq[:, g:g + 1], sq[:], AX)
                        v.op("reduce_sum", sums[:, g:g + 1], xg, AX)
                    v.op("tensor_scalar_mul", mu[:], sums[:], 1.0 / D)
                    v.op("tensor_scalar_mul", ex2[:], sumsq[:], 1.0 / D)
                    v.op("tensor_tensor", musq[:], mu[:], mu[:], ALU.mult)
                    v.op("tensor_tensor", var[:], ex2[:], musq[:], ALU.subtract,
                         inc=sm["sVr"])
                    v.wait_ge(sm["sSd"].h, gt + 1)
                    v.op("reciprocal", istd[:], stdv[:])
                    v.wait_ge(sm["stp"].h, 8 * gt)
                    for g in range(4):
                        v.op("tensor_scalar", zt[:, g * D:(g + 1) * D],
                             xt2[tau % 2][:, g * D:(g + 1) * D],
                             mu[:, g:g + 1], istd[:, g:g + 1],
                             ALU.subtract, ALU.mult, inc=sm["sln"])
                    for c in range(8):
                        CB = gt * 8 + c
                        v.wait_ge(sm["stp"].h, CB + 1)
                        v.op("tensor_copy",
                             zf[:, c * SEQ + tau * 512:c * SEQ + tau * 512 + 512],
                             psT[c % 2][:, 0:512], inc=sm["szf"])
                for kt in range(16):
                    v.wait_ge(sm["sTB"].h, bq * 32 + 2 * kt + 1)
                    v.op("tensor_copy", phT[:, kt * 256:(kt + 1) * 256],
                         psT[0][:, 0:256], inc=sm["sTE"])
                    v.wait_ge(sm["sTB"].h, bq * 32 + 2 * kt + 2)
                    v.op("tensor_copy", gT[:, kt * D:(kt + 1) * D],
                         psT[1][:, 0:D], inc=sm["sTE"])
                for qt in range(16):
                    gq = bq * 16 + qt
                    if gq >= 2:
                        v.wait_ge(sst2[gq % 2].h, 16 * (gq // 2))
                    v.wait_ge(sxq[qt % 2].h, 16 * (gq // 2 + 1))
                    for oc in range(2):
                        v.wait_ge(sm["sMs"].h, 2 * gq + oc + 1)
                        v.op("tensor_tensor",
                             xoq[qt % 2][:, oc * 512:(oc + 1) * 512],
                             psA[2 * (qt % 2) + oc][:],
                             xq[qt % 2][:, oc * 512:oc * 512 + 512],
                             ALU.add, inc=sm["sRes"])

            @b.scalar
            def _(s_, bq=bq):
                s = SC
                for tau in range(4):
                    s.wait_ge(sm["sVr"].h, bq * 4 + tau + 1)
                    s.op("activation", stdv[:], var[:], AF.Sqrt,
                         bias=epsb[:, 0:1], inc=sm["sSd"])
                for pi, (nmc, obuf, owid) in enumerate(
                        ((2, th, SEQ), (2, ph, SEQ), (8, gf, SEQ))):
                    base = [0, 8, 16][pi] + bq * 52
                    for mc in range(nmc):
                        for nt in range(4):
                            gidx = base + mc * 4 + nt
                            s.wait_ge(sm["sMM"].h, gidx + 1)
                            s.op("activation",
                                 obuf[:, mc * owid + nt * 512:mc * owid + nt * 512 + 512],
                                 psA[gidx % 4][:], AF.Copy, inc=sm["sEv"])
                for idx in range(4):
                    s.wait_ge(sm["sMM"].h, bq * 52 + 48 + idx + 1)
                    lc, oc = idx // 2, idx % 2
                    s.op("activation",
                         m_sb[:, lc * D + oc * 512:lc * D + oc * 512 + 512],
                         psA[idx][:], AF.Copy, scale=msc[:, 0:1], inc=sm["sEv"])

            @b.tensor
            def _(t, bq=bq):
                if bq == 0:
                    t.wait_ge(sm["sC"].h, 16)
                for tau in range(4):
                    gt = bq * 4 + tau
                    t.wait_ge(sm["sln"].h, 4 * (gt + 1))
                    for c in range(8):
                        CB = gt * 8 + c
                        if CB >= 2:
                            t.wait_ge(sm["szf"].h, CB - 1)
                        for g in range(4):
                            tr = t.transpose(psT[c % 2][:, g * 128:(g + 1) * 128],
                                             zt[:, g * D + c * 128:g * D + c * 128 + 128],
                                             idb[:])
                        sm["stp"].inc(tr)
                t.wait_ge(sm["szf"].h, 32 * (bq + 1))
                gidx = bq * 52
                for wi, (nmc, mwid) in enumerate(((2, 256), (2, 256), (8, D))):
                    t.wait_ge(swb.h, 16 * (bq * 3 + wi + 1))
                    for mc in range(nmc):
                        for nt in range(4):
                            if gidx >= 4 + bq * 52:
                                t.wait_ge(sm["sEv"].h, gidx - 3)
                            for k in range(8):
                                mm = t.matmul(psA[gidx % 4][:],
                                              wpb[:, k * mwid + mc * 128:k * mwid + mc * 128 + 128],
                                              zf[:, k * SEQ + nt * 512:k * SEQ + nt * 512 + 512],
                                              start=(k == 0), stop=(k == 7))
                            sm["sMM"].inc(mm)
                            gidx += 1
                t.wait_ge(sm["sEv"].h, bq * 52 + 16)
                for kt in range(16):
                    TB = bq * 32 + 2 * kt
                    if TB >= 2:
                        t.wait_ge(sm["sTE"].h, TB - 1)
                    for lc in range(2):
                        tr = t.transpose(psT[0][:, lc * 128:(lc + 1) * 128],
                                         ph[:, lc * SEQ + kt * 128:lc * SEQ + kt * 128 + 128],
                                         idb[:])
                    sm["sTB"].inc(tr)
                    if TB + 1 >= 2:
                        t.wait_ge(sm["sTE"].h, TB)
                    if kt == 0:
                        t.wait_ge(sm["sEv"].h, bq * 52 + 48)
                    for mc in range(8):
                        tr = t.transpose(psT[1][:, mc * 128:(mc + 1) * 128],
                                         gf[:, mc * SEQ + kt * 128:mc * SEQ + kt * 128 + 128],
                                         idb[:])
                    sm["sTB"].inc(tr)
                t.wait_ge(sm["sTE"].h, 32 * (bq + 1))
                for idx in range(4):
                    lc, oc = idx // 2, idx % 2
                    for kt in range(16):
                        mm = t.matmul(psA[idx][:],
                                      phT[:, kt * 256 + lc * 128:kt * 256 + lc * 128 + 128],
                                      gT[:, kt * D + oc * 512:kt * D + oc * 512 + 512],
                                      start=(kt == 0), stop=(kt == 15))
                    sm["sMM"].inc(mm)
                t.wait_ge(sm["sEv"].h, bq * 52 + 52)
                for qt in range(16):
                    gq = bq * 16 + qt
                    if gq >= 2:
                        t.wait_ge(sm["sRes"].h, 2 * (gq - 1))
                    for oc in range(2):
                        for lc in range(2):
                            mm = t.matmul(psA[2 * (qt % 2) + oc][:],
                                          th[:, lc * SEQ + qt * 128:lc * SEQ + qt * 128 + 128],
                                          m_sb[:, lc * D + oc * 512:lc * D + oc * 512 + 512],
                                          start=(lc == 0), stop=(lc == 1))
                        sm["sMs"].inc(mm)

    with nc.Block() as bf:
        @bf.sync
        def _(sp):
            sp.wait_ge(sst2[0].h, 16 * 16)
            sp.wait_ge(sst2[1].h, 16 * 16)

    _st.close()
    _emit_sem_reset(nc)


def build_l0attn():
    nc = bass.Bass("TRN2", target_bir_lowering=False)
    x_in = nc.dram_tensor("x_in", [TPC, D], F32, kind="ExternalInput").ap()
    x_out = nc.dram_tensor("x_out", [TPC, D], F32, kind="ExternalOutput").ap()
    wd = _declare_io(nc, L0_KEYS, _SHAPES)
    emit_l0attn(nc, "a_", x_in, x_out, wd)
    return nc




L1F_KEYS = ["WrT", "ATall", "Bpk", "Eall", "identb", "identf"]


def emit_l1ffn(nc, pfx, x_in, x_out, wd):
    _st = contextlib.ExitStack()
    sb = lambda n, s, d: _st.enter_context(nc.sbuf_tensor(pfx + n, s, d))
    pst = lambda n, s, d: _st.enter_context(nc.psum_tensor(pfx + n, s, d))
    wrd = wd["WrT"]; atd = wd["ATall"]; bpd = wd["Bpk"]; ed = wd["Eall"]
    idb_d = wd["identb"]; idf_d = wd["identf"]

    x_in_t = x_in.rearrange("(t g p) f -> t p g f", g=4, p=128)
    x_out_t = x_out.rearrange("(t g p) f -> t p g f", g=4, p=128)
    x_cls = x_in.rearrange("(b s) f -> b s f", s=SEQ)[:, 0, :]

    xt = [sb(f"xt{q}", [128, 4 * D], F32) for q in range(2)]
    xo = sb("xo", [128, 4 * D], F32)
    zt = sb("zt", [128, 4 * D], BF16)
    zf = sb("zf", [128, 8 * TT], BF16)
    sq = sb("sq", [128, D], F32)
    xzb = sb("xzb", [128, D], F32)
    mid_sb = sb("mid_sb", [128, 4 * TT], BF16)
    wf = sb("wf", [128, 8 * TT], F32)
    ats = sb("ats", [128, 8 * 512], BF16)
    bps = sb("bps", [128, 4 * D], BF16)
    wrs = sb("wrs", [128, 8 * 8], BF16)
    ebs = sb("ebs", [8, 512], BF16)
    cls_x = sb("cls_x", [2, D], F32)
    clsz = sb("clsz", [2, D], BF16)
    cls_fm = sb("cls_fm", [128, 16], BF16)
    lg = sb("lg", [2, 8], F32)
    gsc = sb("gsc", [2, 16], F32)
    gs2 = sb("gs2", [2, 40], F32)
    gb = sb("gb", [2, 8], BF16)
    gts = sb("gts", [8, 2], BF16)
    g_all = sb("g_all", [128, 8], F32)
    idb = sb("idb", [128, 128], BF16)
    idf = sb("idf", [128, 128], F32)
    epsb = sb("epsb", [128, 1], F32)
    sums = sb("sums", [128, 4], F32)
    sumsq = sb("sumsq", [128, 4], F32)
    mu = sb("mu", [128, 4], F32)
    ex2 = sb("ex2", [128, 4], F32)
    musq = sb("musq", [128, 4], F32)
    var = sb("var", [128, 4], F32)
    stdv = sb("stdv", [128, 4], F32)
    istd = sb("istd", [128, 4], F32)
    half = sb("half", [128, 4], F32)
    aZ = sb("aZ", [128, 4], F32)
    bZmu = sb("bZmu", [128, 4], F32)

    psA = [pst(f"psA{q}", [128, TT], F32) for q in range(4)]
    psW = pst("psW", [128, D], F32)
    psT = [pst(f"psT{q}", [128, 1024], BF16) for q in range(2)]

    sm = _sems(nc, ["sC", "sst", "sln", "stp", "szf", "sMD", "sME", "sEO",
                    "sWf", "sYT", "sRes", "sVr", "sSd", "sLNz", "sTz", "sFM",
                    "sLG", "sGpre", "sLGe", "sTg", "sGT", "sGV", "sGA"])
    sm["sCW"] = S(nc, pfx + "sCW", persist=True)
    sx2 = [S(nc, f"sx{q}") for q in range(2)]
    VC = Chain(nc.vector, S(nc, "chV"))
    SC = Chain(nc.scalar, S(nc, "chS"))

    with nc.Block() as b0:
        @b0.sync
        def _(sp):
            sm["sC"].inc(sp.dma_start(idb[:], idb_d), 16)
            sm["sC"].inc(sp.dma_start(idf[:], idf_d), 16)
            sm["sC"].inc(sp.dma_start(cls_x[:], x_cls), 16)

        @b0.gpsimd
        def _(g):
            sm["sCW"].inc(g.dma_start(
                wrs[:].rearrange("p (a e) -> p a e", e=8),
                wrd.rearrange("(a p) e -> p a e", p=128)), 16)
            sm["sCW"].inc(g.dma_start(
                ats[:].rearrange("p (a m) -> p a m", m=512),
                atd.rearrange("(a p) m -> p a m", p=128)), 16)
            sm["sCW"].inc(g.dma_start(
                bps[:].rearrange("p (a m) -> p a m", m=D),
                bpd.rearrange("(a p) m -> p a m", p=128)), 16)
            sm["sCW"].inc(g.dma_start(ebs[:], ed), 16)

        @b0.vector
        def _(v_):
            v = VC
            v.op("memset", epsb[:], EPS)
            v.wait_ge(sm["sC"].h, 3 * 16)
            v.wait_ge(sm["sCW"].h, 4 * 16)
            v.op("tensor_tensor", sq[0:2, :], cls_x[:], cls_x[:], ALU.mult)
            v.op("reduce_sum", gsc[:, 0:1], sq[0:2, :], AX)
            v.op("reduce_sum", gsc[:, 1:2], cls_x[:], AX)
            v.op("tensor_scalar_mul", gsc[:, 2:3], gsc[:, 1:2], 1.0 / D)
            v.op("tensor_scalar_mul", gsc[:, 3:4], gsc[:, 0:1], 1.0 / D)
            v.op("tensor_tensor", gsc[:, 4:5], gsc[:, 2:3], gsc[:, 2:3], ALU.mult)
            v.op("tensor_tensor", gsc[:, 5:6], gsc[:, 3:4], gsc[:, 4:5],
                 ALU.subtract, inc=sm["sVr"])
            v.wait_ge(sm["sSd"].h, 1)
            v.op("reciprocal", gsc[:, 7:8], gsc[:, 6:7])
            v.op("tensor_scalar", clsz[:], cls_x[:], gsc[:, 2:3], gsc[:, 7:8],
                 ALU.subtract, ALU.mult, inc=sm["sLNz"])
            v.wait_ge(sm["sTz"].h, 1)
            v.op("tensor_copy", cls_fm[:], psT[0][:, 0:16], inc=sm["sFM"])
            v.wait_ge(sm["sLG"].h, 1)
            v.op("tensor_copy", lg[:], psA[0][0:2, 0:8])
            v.op("reduce_max", gsc[:, 8:9], lg[:], AX, negate=True)
            v.op("tensor_scalar", gs2[:, 0:8], lg[:], gsc[:, 8:9], None,
                 ALU.add)
            v.op("tensor_single_scalar", gs2[:, 8:16], gs2[:, 0:8], 0.0, ALU.is_ge)
            v.op("scalar_tensor_tensor", gs2[:, 16:24], gs2[:, 8:16], -1e30,
                 gs2[:, 0:8], ALU.mult, ALU.add)
            v.op("reduce_max", gsc[:, 9:10], gs2[:, 16:24], AX, negate=True)
            v.op("tensor_scalar_mul", gsc[:, 10:11], gsc[:, 9:10], -1.0)
            v.op("tensor_single_scalar", gs2[:, 24:32], gs2[:, 0:8],
                 gsc[:, 10:11], ALU.is_ge, inc=sm["sGpre"])
            v.wait_ge(sm["sSd"].h, 2)
            v.op("tensor_tensor", gs2[:, 32:40], gs2[:, 32:40], gs2[:, 24:32],
                 ALU.mult)
            v.op("reduce_sum", gsc[:, 11:12], gs2[:, 32:40], AX)
            v.op("reciprocal", gsc[:, 12:13], gsc[:, 11:12])
            v.op("tensor_scalar_mul", gb[:], gs2[:, 32:40], gsc[:, 12:13],
                 inc=sm["sLGe"])
            v.wait_ge(sm["sTg"].h, 1)
            v.op("tensor_copy", gts[:], psT[0][0:8, 16:18], inc=sm["sGT"])
            v.wait_ge(sm["sGV"].h, 4)
            v.op("tensor_copy", g_all[:], psA[1][:, 0:8], inc=sm["sGA"])

        @b0.scalar
        def _(s_):
            s = SC
            s.wait_ge(sm["sVr"].h, 1)
            s.op("activation", gsc[:, 6:7], gsc[:, 5:6], AF.Sqrt,
                 bias=epsb[0:2, 0:1], inc=sm["sSd"])
            s.wait_ge(sm["sGpre"].h, 1)
            s.op("activation", gs2[:, 32:40], gs2[:, 0:8], AF.Exp, inc=sm["sSd"])

        @b0.tensor
        def _(t):
            t.wait_ge(sm["sC"].h, 3 * 16)
            t.wait_ge(sm["sCW"].h, 4 * 16)
            t.wait_ge(sm["sLNz"].h, 1)
            for c in range(8):
                tr = t.transpose(psT[0][:, c * 2:(c + 1) * 2],
                                 clsz[:, c * 128:(c + 1) * 128], idb[0:2, 0:2])
            sm["sTz"].inc(tr)
            t.wait_ge(sm["sFM"].h, 1)
            for kc in range(8):
                mm = t.matmul(psA[0][0:2, 0:8], cls_fm[:, kc * 2:(kc + 1) * 2],
                              wrs[:, kc * 8:(kc + 1) * 8],
                              start=(kc == 0), stop=(kc == 7))
            sm["sLG"].inc(mm)
            t.wait_ge(sm["sLGe"].h, 1)
            tr = t.transpose(psT[0][0:8, 16:18], gb[:], idb[0:2, 0:2])
            sm["sTg"].inc(tr)
            t.wait_ge(sm["sGT"].h, 1)
            for p in range(4):
                mm = t.matmul(psA[1][:, p * 2:(p + 1) * 2],
                              ebs[:, p * 128:(p + 1) * 128], gts[:],
                              start=True, stop=True)
                sm["sGV"].inc(mm)

    with nc.Block() as b:
        for i in range(NTILES):
            bqi = i // 4

            @b.sync
            def _(sp, i=i):
                if i >= 2:
                    sp.wait_ge(sm["sRes"].h, 4 * (i - 1))
                sx2[i % 2].inc(sp.dma_start(
                    xt[i % 2][:].rearrange("p (g f) -> p g f", f=D), x_in_t[i]), 16)
                sp.wait_ge(sm["sRes"].h, 4 * (i + 1))
                sm["sst"].inc(sp.dma_start(
                    x_out_t[i], xo[:].rearrange("p (g f) -> p g f", f=D)), 16)

            @b.vector
            def _(v_, i=i):
                v = VC
                v.wait_ge(sx2[i % 2].h, 16 * (i // 2 + 1))
                for g in range(4):
                    xg = xt[i % 2][:, g * D:(g + 1) * D]
                    v.op("tensor_tensor", sq[:], xg, xg, ALU.mult)
                    v.op("reduce_sum", sumsq[:, g:g + 1], sq[:], AX)
                    v.op("reduce_sum", sums[:, g:g + 1], xg, AX)
                v.op("tensor_scalar_mul", mu[:], sums[:], 1.0 / D)
                v.op("tensor_scalar_mul", ex2[:], sumsq[:], 1.0 / D)
                v.op("tensor_tensor", musq[:], mu[:], mu[:], ALU.mult)
                v.op("tensor_tensor", var[:], ex2[:], musq[:], ALU.subtract,
                     inc=sm["sVr"])
                v.wait_ge(sm["sSd"].h, i + 3)
                v.op("reciprocal", istd[:], stdv[:])
                v.op("tensor_scalar_mul", half[:], istd[:], 0.5)
                v.op("tensor_scalar_add", aZ[:], half[:], 1.0)
                v.op("tensor_tensor", bZmu[:], mu[:], half[:], ALU.mult)
                v.wait_ge(sm["stp"].h, 8 * i)
                for g in range(4):
                    v.op("tensor_scalar", zt[:, g * D:(g + 1) * D],
                         xt[i % 2][:, g * D:(g + 1) * D],
                         mu[:, g:g + 1], istd[:, g:g + 1],
                         ALU.subtract, ALU.mult, inc=sm["sln"])
                v.wait_ge(sm["sMD"].h, 4 * i)
                for c in range(8):
                    v.wait_ge(sm["stp"].h, i * 8 + c + 1)
                    v.op("tensor_copy", zf[:, c * TT:(c + 1) * TT],
                         psT[c % 2][:, 0:TT], inc=sm["szf"])
                if i >= 1:
                    v.wait_ge(sm["sst"].h, 16 * i)
                for g in range(4):
                    v.wait_ge(sm["sYT"].h, i * 4 + g + 1)
                    v.op("tensor_scalar", xzb[:], xt[i % 2][:, g * D:(g + 1) * D],
                         aZ[:, g:g + 1], bZmu[:, g:g + 1], ALU.mult, ALU.subtract)
                    v.op("scalar_tensor_tensor", xo[:, g * D:(g + 1) * D],
                         psW[:, 0:D], 0.5, xzb[:], ALU.mult, ALU.add,
                         inc=sm["sRes"])

            @b.scalar
            def _(s_, i=i):
                s = SC
                s.wait_ge(sm["sVr"].h, i + 2)
                s.op("activation", stdv[:], var[:], AF.Sqrt, bias=epsb[:, 0:1],
                     inc=sm["sSd"])
                s.wait_ge(sm["sGA"].h, 1)
                for p in range(4):
                    s.wait_ge(sm["sMD"].h, i * 4 + p + 1)
                    s.op("activation", mid_sb[:, p * TT:(p + 1) * TT],
                         psA[p][:], AF.Copy,
                         scale=g_all[:, (p * 2 + (i // 4)):(p * 2 + (i // 4)) + 1],
                         inc=sm["sME"])
                if i >= 1:
                    s.wait_ge(sm["sYT"].h, 4 * i)
                for m in range(8):
                    s.wait_ge(sm["sEO"].h, i * 8 + m + 1)
                    s.op("activation", wf[:, m * TT:(m + 1) * TT],
                         psA[m % 2][:], AF.Copy, inc=sm["sWf"])

            @b.tensor
            def _(t, i=i):
                if i == 0:
                    t.wait_ge(sm["sGA"].h, 1)
                t.wait_ge(sm["sln"].h, 4 * (i + 1))
                for c in range(8):
                    CB = i * 8 + c
                    if CB >= 2:
                        t.wait_ge(sm["szf"].h, CB - 1)
                    for g in range(4):
                        tr = t.transpose(psT[c % 2][:, g * 128:(g + 1) * 128],
                                         zt[:, g * D + c * 128:g * D + c * 128 + 128],
                                         idb[:])
                    sm["stp"].inc(tr)
                t.wait_ge(sm["szf"].h, 8 * (i + 1))
                if i >= 1:
                    t.wait_ge(sm["sME"].h, 4 * i)
                for p in range(4):
                    for k in range(8):
                        mm = t.matmul(psA[p][:],
                                      ats[:, k * 512 + p * 128:k * 512 + p * 128 + 128],
                                      zf[:, k * TT:(k + 1) * TT],
                                      start=(k == 0), stop=(k == 7))
                    sm["sMD"].inc(mm)
                t.wait_ge(sm["sME"].h, 4 * (i + 1))
                for m in range(8):
                    EOg = i * 8 + m
                    if EOg >= 2:
                        t.wait_ge(sm["sWf"].h, EOg - 1)
                    for p in range(4):
                        mm = t.matmul(psA[m % 2][:],
                                      bps[:, p * D + m * 128:p * D + m * 128 + 128],
                                      mid_sb[:, p * TT:(p + 1) * TT],
                                      start=(p == 0), stop=(p == 3))
                    sm["sEO"].inc(mm)
                t.wait_ge(sm["sWf"].h, 8 * (i + 1))
                for g in range(4):
                    t.wait_ge(sm["sRes"].h, 4 * i + g)
                    for c in range(8):
                        tr = t.transpose(psW[:, c * 128:(c + 1) * 128],
                                         wf[:, c * TT + g * 128:c * TT + g * 128 + 128],
                                         idf[:])
                    sm["sYT"].inc(tr)

    with nc.Block() as bf:
        @bf.sync
        def _(sp):
            sp.wait_ge(sm["sst"].h, 16 * NTILES)

    _st.close()
    _emit_sem_reset(nc)


def build_l1ffn():
    nc = bass.Bass("TRN2", target_bir_lowering=False)
    x_in = nc.dram_tensor("x_in", [TPC, D], F32, kind="ExternalInput").ap()
    x_out = nc.dram_tensor("x_out", [TPC, D], F32, kind="ExternalOutput").ap()
    wd = _declare_io(nc, L1F_KEYS, _SHAPES)
    emit_l1ffn(nc, "m_", x_in, x_out, wd)
    return nc




L1A_KEYS = ["WQT", "WKT", "WVT", "WOT", "Ctab", "Stab", "Mtabb",
            "identb", "identf", "Pswap"]


def emit_l1attn(nc, pfx, x_in, x_out, wd):
    _st = contextlib.ExitStack()
    sb = lambda n, s, d: _st.enter_context(nc.sbuf_tensor(pfx + n, s, d))
    pst = lambda n, s, d: _st.enter_context(nc.psum_tensor(pfx + n, s, d))
    wqd = wd["WQT"]; wkd = wd["WKT"]; wvd = wd["WVT"]; wod = wd["WOT"]
    cthd = wd["Ctab"]; sthd = wd["Stab"]; mskd = wd["Mtabb"]
    idb_d = wd["identb"]; idf_d = wd["identf"]; psw_d = wd["Pswap"]

    x_in_q = x_in.rearrange("(q p) f -> q p f", p=128)
    x_out_q = x_out.rearrange("(q p) f -> q p f", p=128)
    r_w = {"K": wkd.rearrange("(a p) m -> p a m", p=128),
           "Q": wqd.rearrange("(a p) m -> p a m", p=128),
           "V": wvd.rearrange("(a p) m -> p a m", p=128)}
    r_wo = wod.rearrange("(a p) m -> p a m", p=128)

    Kf = sb("Kf", [128, 8 * SEQ], BF16)
    Qf = sb("Qf", [128, 8 * SEQ], BF16)
    VT = sb("VT", [128, 16 * D], BF16)
    wo_sb = sb("wo_sb", [128, 8 * D], BF16)
    idb = sb("idb", [128, 128], BF16)
    idf = sb("idf", [128, 128], F32)
    psw = sb("psw", [128, 128], BF16)
    Ct = sb("Ct", [128, SEQ], F32)
    St = sb("St", [128, SEQ], F32)
    epsb = sb("epsb", [128, 1], F32)
    q8b = sb("q8b", [128, 1], F32)
    stt1 = sb("stt1", [128, 8], F32)

    sm = _sems(nc, ["sC", "sln", "stp", "szf", "sPj", "sStg", "sSw",
                    "sDn", "sVr", "sSd", "sSmm", "sSe", "sNg", "sEx", "sPn",
                    "sPTb", "sPTe", "sAVa", "sAVe", "sWOc", "sYe", "sYTb",
                    "sResB", "sMtl", "sRH"])
    sm["sCW"] = S(nc, pfx + "sCW", persist=True)
    sxg = [S(nc, f"sxg{q}") for q in range(2)]
    sxq = [S(nc, f"sxq{q}") for q in range(2)]
    sstB = [S(nc, f"sstB{q}") for q in range(2)]
    swv = [S(nc, pfx + f"swv{q}", persist=True) for q in range(2)]
    VCH = Chain(nc.vector, S(nc, "chV"))
    SCH = Chain(nc.scalar, S(nc, "chS"))

    with nc.Block() as b0:
        @b0.sync
        def _(sp):
            sm["sC"].inc(sp.dma_start(idb[:], idb_d), 16)
            sm["sC"].inc(sp.dma_start(idf[:], idf_d), 16)
            sm["sC"].inc(sp.dma_start(psw[:], psw_d), 16)
            sm["sC"].inc(sp.dma_start(Ct[:], cthd), 16)
            sm["sC"].inc(sp.dma_start(St[:], sthd), 16)

        @b0.gpsimd
        def _(g):
            sm["sCW"].inc(g.dma_start(
                wo_sb[:].rearrange("p (a m) -> p a m", m=D), r_wo), 16)

        @b0.vector
        def _(v_):
            VCH.op("memset", epsb[:], EPS)
            VCH.op("memset", q8b[:], 0.125)

    for bq in range(2):
        with (
            nc.sbuf_tensor(f"{pfx}ztA{bq}", [128, 2 * D], BF16) as ztA,
            nc.sbuf_tensor(f"{pfx}zfA{bq}", [128, 8 * SEQ], BF16) as zf,
            nc.sbuf_tensor(f"{pfx}xtA{bq}", [128, 2 * D], F32) as xtA,
            nc.sbuf_tensor(f"{pfx}wbA{bq}", [128, 2 * 8 * 512], BF16) as wbA,
            nc.sbuf_tensor(f"{pfx}stg{bq}", [128, 2 * 512], BF16) as stg,
            nc.sbuf_tensor(f"{pfx}v1b{bq}", [128, 512], F32) as v1b,
            nc.sbuf_tensor(f"{pfx}v2b{bq}", [128, 512], F32) as v2b,
            nc.sbuf_tensor(f"{pfx}sqA{bq}", [128, D], F32) as sqA,
            nc.psum_tensor(f"{pfx}psA0{bq}", [128, 512], F32) as psA0,
            nc.psum_tensor(f"{pfx}psA1{bq}", [128, 512], F32) as psA1,
            nc.psum_tensor(f"{pfx}psB0{bq}", [128, 512], F32) as psB0,
            nc.psum_tensor(f"{pfx}psB1{bq}", [128, 512], F32) as psB1,
            nc.psum_tensor(f"{pfx}psTa{bq}", [128, 1024], BF16) as psTa,
            nc.psum_tensor(f"{pfx}psTb{bq}", [128, 1024], BF16) as psTb,
        ):
            psA2 = [psA0, psA1]
            psB2 = [psB0, psB1]
            psT2 = [psTa, psTb]

            def pg_info(pg):
                proj = pg // 32
                mc = (pg % 32) // 4
                nt = pg % 4
                mg = mc // 4
                return proj, mc, nt, mg

            with nc.Block() as b:
                @b.sync
                def _(sp, bq=bq):
                    for g in range(16):
                        gg = bq * 16 + g
                        if gg >= 2:
                            sp.wait_ge(sm["sln"].h, gg - 1)
                        sxg[g % 2].inc(sp.dma_start(
                            xtA[:, (g % 2) * D:(g % 2) * D + D], x_in_q[gg]), 16)

                @b.gpsimd
                def _(g, bq=bq):
                    for si in range(6):
                        SI = bq * 6 + si
                        proj = "KQV"[si // 2]
                        mg = si % 2
                        if SI >= 2:
                            g.wait_ge(sm["sPj"].h, (SI - 1) * 16)
                        swv[SI % 2].inc(g.dma_start(
                            wbA[:, (SI % 2) * 4096:(SI % 2) * 4096 + 4096]
                            .rearrange("p (a m) -> p a m", m=512),
                            r_w[proj][:, :, mg * 512:(mg + 1) * 512]), 16)

                @b.vector
                def _(v_, bq=bq):
                    v = VCH
                    for g in range(16):
                        gg = bq * 16 + g
                        v.wait_ge(sxg[g % 2].h, 16 * (gg // 2 + 1))
                        xg = xtA[:, (g % 2) * D:(g % 2) * D + D]
                        v.op("tensor_tensor", sqA[:], xg, xg, ALU.mult)
                        v.op("reduce_sum", stt1[:, 0:1], sqA[:], AX)
                        v.op("reduce_sum", stt1[:, 1:2], xg, AX)
                        v.op("tensor_scalar_mul", stt1[:, 2:3], stt1[:, 1:2], 1.0 / D)
                        v.op("tensor_scalar_mul", stt1[:, 3:4], stt1[:, 0:1], 1.0 / D)
                        v.op("tensor_tensor", stt1[:, 4:5], stt1[:, 2:3],
                             stt1[:, 2:3], ALU.mult)
                        v.op("tensor_tensor", stt1[:, 5:6], stt1[:, 3:4],
                             stt1[:, 4:5], ALU.subtract, inc=sm["sVr"])
                        v.wait_ge(sm["sSd"].h, gg + 1)
                        v.op("reciprocal", stt1[:, 7:8], stt1[:, 6:7])
                        v.op("tensor_scalar", ztA[:, (g % 2) * D:(g % 2) * D + D],
                             xg, stt1[:, 2:3], stt1[:, 7:8],
                             ALU.subtract, ALU.mult, inc=sm["sln"])
                        v.wait_ge(sm["stp"].h, gg + 1)
                        v.op("tensor_copy",
                             zf[:].rearrange("p (c t) -> p c t", t=SEQ)[:, :, g * 128:(g + 1) * 128],
                             psT2[g % 2][:, 0:1024].rearrange("p (c t) -> p c t", t=128),
                             inc=sm["szf"])
                    for pg in range(96):
                        G = bq * 96 + pg
                        proj, mc, nt, mg = pg_info(pg)
                        v.wait_ge(sm["sSw"].h, G + 1)
                        if proj < 2:
                            v.op("tensor_tensor", v1b[:],
                                 stg[:, (pg % 2) * 512:(pg % 2) * 512 + 512],
                                 Ct[:, nt * 512:(nt + 1) * 512], ALU.mult)
                            v.op("tensor_tensor", v2b[:], psB2[pg % 2][:],
                                 St[:, nt * 512:(nt + 1) * 512], ALU.mult)
                            obuf = Kf if proj == 0 else Qf
                            v.op("tensor_tensor",
                                 obuf[:, mc * SEQ + nt * 512:mc * SEQ + nt * 512 + 512],
                                 v1b[:], v2b[:], ALU.add, inc=sm["sDn"])
                        else:
                            v.op("tensor_copy",
                                 VT[:].rearrange("p (kt d) -> p kt d", d=D)
                                 [:, nt * 4:(nt + 1) * 4, mc * 128:(mc + 1) * 128],
                                 psT2[pg % 2][:, 0:512].rearrange("p (s t) -> p s t", t=128),
                                 inc=sm["sDn"])

                @b.scalar
                def _(s_, bq=bq):
                    s = SCH
                    for g in range(16):
                        gg = bq * 16 + g
                        s.wait_ge(sm["sVr"].h, gg + 1)
                        s.op("activation", stt1[:, 6:7], stt1[:, 5:6], AF.Sqrt,
                             bias=epsb[:, 0:1], inc=sm["sSd"])
                    for pg in range(96):
                        G = bq * 96 + pg
                        proj = pg // 32
                        s.wait_ge(sm["sPj"].h, G + 1)
                        if G >= 2:
                            s.wait_ge(sm["sDn"].h, G - 1)
                            s.wait_ge(sm["sSw"].h, G - 1)
                        if proj == 1:
                            s.op("activation", stg[:, (pg % 2) * 512:(pg % 2) * 512 + 512],
                                 psA2[pg % 2][:], AF.Copy, scale=q8b[:, 0:1],
                                 inc=sm["sStg"])
                        else:
                            s.op("activation", stg[:, (pg % 2) * 512:(pg % 2) * 512 + 512],
                                 psA2[pg % 2][:], AF.Copy, inc=sm["sStg"])

                @b.tensor
                def _(t, bq=bq):
                    if bq == 0:
                        t.wait_ge(sm["sC"].h, 5 * 16)
                        t.wait_ge(sm["sCW"].h, 16)

                    def emit_post(item):
                        G, pg = item
                        proj = pg // 32
                        t.wait_ge(sm["sStg"].h, G + 1)
                        if G >= 2:
                            t.wait_ge(sm["sDn"].h, G - 1)
                        if proj < 2:
                            mm = t.matmul(psB2[pg % 2][:], psw[:],
                                          stg[:, (pg % 2) * 512:(pg % 2) * 512 + 512],
                                          start=True, stop=True)
                            sm["sSw"].inc(mm)
                        else:
                            for sblk in range(4):
                                tr = t.transpose(
                                    psT2[pg % 2][:, sblk * 128:(sblk + 1) * 128],
                                    stg[:, (pg % 2) * 512 + sblk * 128:(pg % 2) * 512 + sblk * 128 + 128],
                                    idb[:])
                            sm["sSw"].inc(tr)

                    for g in range(16):
                        gg = bq * 16 + g
                        t.wait_ge(sm["sln"].h, gg + 1)
                        if gg >= 2:
                            t.wait_ge(sm["szf"].h, gg - 1)
                        for c in range(8):
                            tr = t.transpose(psT2[g % 2][:, c * 128:(c + 1) * 128],
                                             ztA[:, (g % 2) * D + c * 128:(g % 2) * D + c * 128 + 128],
                                             idb[:])
                        sm["stp"].inc(tr)
                    t.wait_ge(sm["szf"].h, 16 * (bq + 1))
                    pend = None
                    for pg in range(96):
                        G = bq * 96 + pg
                        proj, mc, nt, mg = pg_info(pg)
                        SI = bq * 6 + proj * 2 + mg
                        if pg % 16 == 0:
                            t.wait_ge(swv[SI % 2].h, 16 * (SI // 2 + 1))
                        if G >= 2:
                            t.wait_ge(sm["sStg"].h, G - 1)
                        mcl = mc % 4
                        wofs = (SI % 2) * 4096
                        for k in range(8):
                            mm = t.matmul(psA2[pg % 2][:],
                                          wbA[:, wofs + k * 512 + mcl * 128:wofs + k * 512 + mcl * 128 + 128],
                                          zf[:, k * SEQ + nt * 512:k * SEQ + nt * 512 + 512],
                                          start=(k == 0), stop=(k == 7))
                        sm["sPj"].inc(mm)
                        if pend is not None:
                            emit_post(pend)
                        pend = (G, pg)
                    emit_post(pend)

        with (
            nc.sbuf_tensor(f"{pfx}mskb{bq}", [128, 2048], BF16) as mskb,
            nc.sbuf_tensor(f"{pfx}Psb{bq}", [128, 2 * SEQ], BF16) as Psb,
            nc.sbuf_tensor(f"{pfx}PTsb{bq}", [128, 2 * SEQ], BF16) as PTsb,
            nc.sbuf_tensor(f"{pfx}xqb{bq}", [128, 2 * D], F32) as xqb,
            nc.sbuf_tensor(f"{pfx}xob{bq}", [128, 2 * D], F32) as xob,
            nc.sbuf_tensor(f"{pfx}avsb{bq}", [128, 2 * 8 * 128], BF16) as avsb,
            nc.sbuf_tensor(f"{pfx}zhb{bq}", [128, 8], F32) as zhb,
            nc.sbuf_tensor(f"{pfx}ngb{bq}", [128, 4], F32) as ngb,
            nc.psum_tensor(f"{pfx}psS0{bq}", [128, 512], F32) as psS0,
            nc.psum_tensor(f"{pfx}psS1{bq}", [128, 512], F32) as psS1,
            nc.psum_tensor(f"{pfx}psPTa{bq}", [128, 1024], BF16) as psPTa,
            nc.psum_tensor(f"{pfx}psPTb{bq}", [128, 1024], BF16) as psPTb,
            nc.psum_tensor(f"{pfx}psAVa{bq}", [128, 128], F32) as psAVa,
            nc.psum_tensor(f"{pfx}psAVb{bq}", [128, 128], F32) as psAVb,
            nc.psum_tensor(f"{pfx}psWO{bq}", [128, 1024], F32) as psWO,
        ):
            psS2 = [psS0, psS1]
            psPT2 = [psPTa, psPTb]
            psAV2 = [psAVa, psAVb]
            c0 = {"ch": sm["sSmm"].n, "h": sm["sPn"].n, "bat": sm["sPTb"].n,
                  "hp": sm["sAVe"].n, "j": sm["sYe"].n, "av": sm["sAVa"].n,
                  "res": sm["sResB"].n, "ytb": sm["sYTb"].n, "woc": sm["sWOc"].n,
                  "rh": sm["sRH"].n}
            jorder = []
            for k in range(8):
                jorder += [k, 15 - k]
            rank = {j: r for r, j in enumerate(jorder)}
            HL = []
            for k in range(8):
                for h in range(16):
                    HL.append((k, h))
                    HL.append((15 - k, h))

            def nch_of(j):
                return j // 4 + 1

            def nb8(j):
                return (j + 8) // 8

            chA = []; batA = []; oddA = []
            c_ch = c0["ch"]; c_bat = c0["bat"]; c_odd = 0
            for (j, h) in HL:
                chA.append(c_ch); batA.append(c_bat); oddA.append(c_odd)
                c_ch += nch_of(j); c_bat += nb8(j)
                if h % 2 == 1:
                    c_odd += 1
            av_done = {}
            c_odd = 0
            for idx, (j, h) in enumerate(HL):
                if h % 2 == 1:
                    c_odd += 1
                    if h == 15:
                        av_done[j] = c_odd

            with nc.Block() as b:
                @b.sync
                def _(sp, bq=bq):
                    sm["sMtl"].inc(sp.dma_start(mskb[:], mskd), 16)
                    for r, j in enumerate(jorder):
                        gj = bq * 16 + j
                        if r >= 2:
                            sp.wait_ge(sm["sResB"].h, c0["res"] + r - 1)
                        sxq[r % 2].inc(sp.dma_start(
                            xqb[:, (r % 2) * D:(r % 2) * D + D], x_in_q[gj]), 16)
                        sp.wait_ge(sm["sResB"].h, c0["res"] + r + 1)
                        sstB[r % 2].inc(sp.dma_start(
                            x_out_q[gj], xob[:, (r % 2) * D:(r % 2) * D + D]), 16)
                    sp.wait_ge(sstB[0].h, 16 * 8 * (bq + 1))
                    sp.wait_ge(sstB[1].h, 16 * 8 * (bq + 1))

                @b.vector
                def _(v_, bq=bq, c0=c0):
                    v = VCH

                    def zdmat(idx):
                        j, h = HL[idx]
                        par = idx % 2
                        nch = nch_of(j)
                        wS = 512 * nch
                        v.wait_ge(sm["sSe"].h, chA[idx] + nch)
                        if nch > 1:
                            v.op("reduce_sum", ngb[:, par:par + 1],
                                 zhb[:, par * 4:par * 4 + nch], AX)
                            v.op("reciprocal", ngb[:, 2 + par:3 + par],
                                 ngb[:, par:par + 1])
                        else:
                            v.op("reciprocal", ngb[:, 2 + par:3 + par],
                                 zhb[:, par * 4:par * 4 + 1])
                        v.op("tensor_scalar_mul",
                             Psb[:, par * SEQ:par * SEQ + wS],
                             Psb[:, par * SEQ:par * SEQ + wS],
                             ngb[:, 2 + par:3 + par], inc=sm["sPn"])

                    def resid(j2):
                        r = rank[j2]
                        v.wait_ge(sm["sWOc"].h, c0["woc"] + 2 * (r + 1))
                        if r >= 2:
                            v.wait_ge(sstB[r % 2].h, 16 * 8 * bq + 16 * (r // 2))
                        v.wait_ge(sxq[r % 2].h, 16 * 8 * bq + 16 * (r // 2 + 1))
                        v.op("tensor_tensor", xob[:, (r % 2) * D:(r % 2) * D + D],
                             psWO[:, 0:D], xqb[:, (r % 2) * D:(r % 2) * D + D],
                             ALU.add, inc=sm["sResB"])

                    for idx in range(len(HL)):
                        zdmat(idx)
                        if idx >= 3 and HL[idx - 3][1] == 15:
                            resid(HL[idx - 3][0])
                    for idx in (len(HL) - 3, len(HL) - 2, len(HL) - 1):
                        if HL[idx][1] == 15:
                            resid(HL[idx][0])

                @b.scalar
                def _(s_, bq=bq, c0=c0):
                    s = SCH

                    def pt_evac(idx):
                        j, h = HL[idx]
                        par = idx % 2
                        if idx >= 2:
                            s.wait_ge(sm["sAVa"].h, c0["av"] + idx - 1)
                        bat = batA[idx]
                        for bi in range(nb8(j)):
                            bat += 1
                            wl = 128 * min(8, j + 1 - bi * 8)
                            s.wait_ge(sm["sPTb"].h, bat)
                            s.op("copy",
                                 PTsb[:, par * SEQ + bi * 1024:par * SEQ + bi * 1024 + wl],
                                 psPT2[par][:, 0:wl], inc=sm["sPTe"])

                    def exp_h(idx):
                        j, h = HL[idx]
                        par = idx % 2
                        nch = nch_of(j)
                        if idx >= 2:
                            s.wait_ge(sm["sPn"].h, c0["h"] + idx - 1)
                            s.wait_ge(sm["sPTb"].h, batA[idx - 1])
                        ch = chA[idx]
                        for kk in range(nch):
                            bank = ch % 2
                            ch += 1
                            s.wait_ge(sm["sSmm"].h, ch)
                            s.op("activation",
                                 Psb[:, par * SEQ + kk * 512:par * SEQ + kk * 512 + 512],
                                 psS2[bank][:], AF.Exp,
                                 accum_out=zhb[:, par * 4 + kk:par * 4 + kk + 1],
                                 inc=sm["sSe"])

                    def post(idx):
                        j, h = HL[idx]
                        r = rank[j]
                        if h % 2 == 1:
                            hp = h // 2
                            s.wait_ge(sm["sAVa"].h, c0["av"] + idx + 1)
                            if r >= 2 and hp == 0:
                                s.wait_ge(sm["sWOc"].h, c0["woc"] + 2 * (r - 1))
                            ofs = (r % 2) * 1024 + hp * 128
                            s.op("activation", avsb[:, ofs:ofs + 128],
                                 psAV2[idx % 2][:], AF.Copy, inc=sm["sAVe"])


                    for idx in range(len(HL)):
                        if idx >= 2:
                            pt_evac(idx - 2)
                        exp_h(idx)
                        if idx >= 3:
                            post(idx - 3)
                    n = len(HL)
                    pt_evac(n - 2)
                    post(n - 3)
                    pt_evac(n - 1)
                    post(n - 2)
                    post(n - 1)

                @b.tensor
                def _(t, bq=bq, c0=c0):
                    t.wait_ge(sm["sMtl"].h, 16 * (bq + 1))

                    def s_mms(idx):
                        j, h = HL[idx]
                        qsl = Qf[64 * (h % 2):64 * (h % 2) + 64,
                                 (h // 2) * SEQ + j * 128:(h // 2) * SEQ + j * 128 + 128]
                        ch = chA[idx]
                        for kk in range(nch_of(j)):
                            bank = ch % 2
                            ch += 1
                            if ch > 2:
                                t.wait_ge(sm["sSe"].h, ch - 2)
                            diag = (kk == j // 4)
                            mm = t.matmul(psS2[bank][:], qsl,
                                          Kf[64 * (h % 2):64 * (h % 2) + 64,
                                             (h // 2) * SEQ + kk * 512:(h // 2) * SEQ + kk * 512 + 512],
                                          start=True, stop=not diag)
                            if diag:
                                mm = t.matmul(psS2[bank][:], idb[:],
                                              mskb[:, (j % 4) * 512:(j % 4) * 512 + 512],
                                              start=False, stop=True)
                            sm["sSmm"].inc(mm)

                    def pt_part(idx):
                        j, h = HL[idx]
                        par = idx % 2
                        t.wait_ge(sm["sPn"].h, c0["h"] + idx + 1)
                        bat = batA[idx]
                        for bi in range(nb8(j)):
                            bat += 1
                            if bi > 0:
                                t.wait_ge(sm["sPTe"].h, bat - 1)
                            elif idx >= 2:
                                t.wait_ge(sm["sPTe"].h, batA[idx - 1])
                            for s8 in range(8):
                                kt = bi * 8 + s8
                                if kt > j:
                                    break
                                tr = t.transpose(psPT2[par][:, s8 * 128:(s8 + 1) * 128],
                                                 Psb[:, par * SEQ + kt * 128:par * SEQ + kt * 128 + 128],
                                                 idb[:])
                            sm["sPTb"].inc(tr)

                    def av_part(idx):
                        j, h = HL[idx]
                        par = idx % 2
                        hp, hs = h // 2, h % 2
                        t.wait_ge(sm["sPTe"].h, batA[idx] + nb8(j))
                        if hs == 0:
                            t.wait_ge(sm["sAVe"].h, c0["hp"] + oddA[idx])
                        for kt in range(j + 1):
                            mm = t.matmul(psAV2[idx % 2][64 * hs:64 * hs + 64, :],
                                          VT[:, kt * D + hp * 128 + hs * 64:kt * D + hp * 128 + hs * 64 + 64],
                                          PTsb[:, par * SEQ + kt * 128:par * SEQ + kt * 128 + 128],
                                          start=(kt == 0), stop=(kt == j),
                                          tile_position=(0, 64 * hs))
                        sm["sAVa"].inc(mm)

                    def wo_yt(j):
                        r = rank[j]
                        t.wait_ge(sm["sAVe"].h, c0["hp"] + av_done[j])
                        if r >= 1:
                            t.wait_ge(sm["sResB"].h, c0["res"] + r)
                        for m in range(2):
                            for hp in range(8):
                                mm = t.matmul(psWO[:, m * 512:(m + 1) * 512],
                                              avsb[:, (r % 2) * 1024 + hp * 128:(r % 2) * 1024 + hp * 128 + 128],
                                              wo_sb[:, hp * D + m * 512:hp * D + m * 512 + 512],
                                              start=(hp == 0), stop=(hp == 7))
                            sm["sWOc"].inc(mm)

                    for idx in range(len(HL)):
                        if idx >= 2:
                            pt_part(idx - 2)
                        if idx >= 3:
                            av_part(idx - 3)
                        s_mms(idx)
                        if idx >= 3 and HL[idx - 3][1] == 15:
                            wo_yt(HL[idx - 3][0])
                    n = len(HL)
                    pt_part(n - 2)
                    av_part(n - 3)
                    pt_part(n - 1)
                    av_part(n - 2)
                    wo_yt(HL[n - 2][0])
                    av_part(n - 1)
                    wo_yt(HL[n - 1][0])

    with nc.Block() as bf:
        @bf.sync
        def _(sp):
            sp.wait_ge(sstB[0].h, 16 * 16)
            sp.wait_ge(sstB[1].h, 16 * 16)

    _st.close()
    _emit_sem_reset(nc)


def build_l1attn():
    nc = bass.Bass("TRN2", target_bir_lowering=False)
    x_in = nc.dram_tensor("x_in", [TPC, D], F32, kind="ExternalInput").ap()
    x_out = nc.dram_tensor("x_out", [TPC, D], F32, kind="ExternalOutput").ap()
    wd = _declare_io(nc, L1A_KEYS, _SHAPES)
    emit_l1attn(nc, "t_", x_in, x_out, wd)
    return nc


FUSE_PARTS = [("ffn0", "f1_"), ("l0attn", "a1_"), ("ffn0", "f2_"),
              ("l1ffn", "m1_"), ("l1attn", "t1_"), ("l1ffn", "m2_")]
_EMITTERS = {"ffn0": None, "l0attn": None, "l1ffn": None, "l1attn": None}


def build_fused(parts=None):
    if parts is None:
        parts = FUSE_PARTS
    emitters = {"ffn0": emit_ffn0, "l0attn": emit_l0attn,
                "l1ffn": emit_l1ffn, "l1attn": emit_l1attn}
    nc = bass.Bass("TRN2", target_bir_lowering=False)
    x_in = nc.dram_tensor("x_in", [TPC, D], F32, kind="ExternalInput").ap()
    x_out = nc.dram_tensor("x_out", [TPC, D], F32, kind="ExternalOutput").ap()
    wd = _declare_io(nc, sorted(set(FFN0_KEYS + L0_KEYS + L1F_KEYS + L1A_KEYS)),
                     _SHAPES)
    wscr = {k: nc.dram_tensor(k + "_b16", _SHAPES[k][0], BF16).ap()
            for k in ("w1T", "w2T", "h1T", "h2T", "w3T")}
    cur = x_in
    seen_ffn = False
    for i, (name, pfx) in enumerate(parts):
        dst = (x_out if i == len(parts) - 1
               else nc.dram_tensor(f"xs{i}", [TPC, D], F32).ap())
        if name == "ffn0":
            emitters[name](nc, pfx, cur, dst, wd, wscr=wscr, first=not seen_ffn)
            seen_ffn = True
        else:
            emitters[name](nc, pfx, cur, dst, wd)
        cur = dst
    return nc



def _stage_common(inp):
    st = {}
    f32 = np.float32

    def tr(a):
        return np.ascontiguousarray(np.asarray(a, f32).T)

    st["w1T"] = tr(inp["L0_ffn_w1_w"])
    st["w2T"] = tr(inp["L0_ffn_w2_w"])
    st["h1T"] = tr(inp["L0_ffn_h1_w"])
    st["h2T"] = tr(inp["L0_ffn_h2_w"])
    st["w3T"] = tr(inp["L0_ffn_w3_w"])
    for nm, key, nchunk in (("s1c", "L0_ffn_w1_s", 32), ("s2c", "L0_ffn_w2_s", 32),
                            ("sh1c", "L0_ffn_h1_s", 32), ("sh2c", "L0_ffn_h2_s", 32),
                            ("s3c", "L0_ffn_w3_s", 8)):
        st[nm] = np.ascontiguousarray(
            np.asarray(inp[key], f32).reshape(nchunk, 128).T)
    st["identb"] = np.eye(128, dtype=BF)
    st["identf"] = np.eye(128, dtype=f32)
    st["WthT"] = tr(inp["L0_attn_W_theta"])
    st["WphT"] = tr(inp["L0_attn_W_phi"])
    st["WgT"] = tr(inp["L0_attn_W_g"])
    A = np.asarray(inp["L1_ffn_A"], f32)
    Bw = np.asarray(inp["L1_ffn_Bw"], f32)
    st["WrT"] = tr(inp["L1_ffn_Wr"])
    st["ATall"] = np.ascontiguousarray(
        np.concatenate([np.concatenate([A[2 * p].T, A[2 * p + 1].T], axis=1)
                        for p in range(4)], axis=1))
    st["Bpk"] = np.ascontiguousarray(
        np.concatenate([np.concatenate([Bw[2 * p].T, Bw[2 * p + 1].T], axis=0)
                        for p in range(4)], axis=0))
    E = np.zeros((8, 512), np.float32)
    for p in range(4):
        for r in range(128):
            E[2 * p + r // 64, p * 128 + r] = 1.0
    st["Eall"] = E
    st["WQT"] = tr(inp["L1_attn_W_Q"])
    st["WKT"] = tr(inp["L1_attn_W_K"])
    st["WVT"] = tr(inp["L1_attn_W_V"])
    st["WOT"] = tr(inp["L1_attn_W_O"])
    kk = np.arange(32, dtype=f32)
    invf = np.power(np.float32(10000.0), (np.float32(-2.0) * kk / np.float32(64.0)),
                    dtype=f32)
    pos = np.arange(SEQ, dtype=f32)
    ang = pos[None, :] * invf[:, None]
    cosv = np.cos(ang).astype(f32)
    sinv = np.sin(ang).astype(f32)
    C = np.empty((128, SEQ), f32)
    Sg = np.empty((128, SEQ), f32)
    for r in range(128):
        d = r % 64
        k = d // 2
        C[r] = cosv[k]
        Sg[r] = -sinv[k] if d % 2 == 0 else sinv[k]
    st["Ctab"] = C
    st["Stab"] = Sg
    M = np.zeros((128, 2048), f32)
    r_ = np.arange(128)[:, None]
    for jj in range(4):
        u = np.arange(512)[None, :]
        M[:, jj * 512:(jj + 1) * 512] = np.where(u <= 128 * jj + r_, 0.0, -1e30)
    st["Mtab"] = M
    st["Mtabb"] = M.astype(BF)
    P = np.zeros((128, 128), f32)
    for i in range(128):
        P[i, i ^ 1] = 1.0
    st["Pswap"] = P.astype(BF)
    return st


def _run(nc, in_maps, out_key):
    res = run_bass_kernel_spmd(nc, in_maps, core_ids=list(range(NCORES)))
    return [np.asarray(res.results[c][out_key]) for c in range(NCORES)]


_CACHE = {}


def _get(name, builder):
    if name not in _CACHE:
        _CACHE[name] = builder()
    return _CACHE[name]


_KEYS = {
    "ffn0": ["w1T", "w2T", "h1T", "h2T", "w3T", "s1c", "s2c", "sh1c", "sh2c",
             "s3c", "identb", "identf"],
    "l0attn": ["WthT", "WphT", "WgT", "identb"],
    "l1ffn": ["WrT", "ATall", "Bpk", "Eall", "identb", "identf"],
    "l1attn": ["WQT", "WKT", "WVT", "WOT", "Ctab", "Stab", "Mtabb",
               "identb", "identf", "Pswap"],
}
_BUILDERS = {"ffn0": build_ffn0, "l0attn": build_l0attn,
             "l1ffn": build_l1ffn, "l1attn": build_l1attn}


def kernel(**inp):
    st = _stage_common(inp)
    x = np.asarray(inp["L0_attn_x"], np.float32).reshape(NCORES, TPC, D)
    shards = [np.ascontiguousarray(x[c]) for c in range(NCORES)]

    nc = _get("fused", build_fused)
    allk = sorted(set(_KEYS["ffn0"] + _KEYS["l0attn"] + _KEYS["l1ffn"] +
                      _KEYS["l1attn"]))
    maps = [dict({k: st[k] for k in allk}, x_in=shards[c])
            for c in range(NCORES)]
    shards = _run(nc, maps, "x_out")
    out = np.stack(shards).reshape(16, SEQ, D).astype(np.float32)
    return out


# version 12
# speedup vs baseline: 1.2309x; 1.0248x over previous
import contextlib
import numpy as np
import ml_dtypes
import concourse.bass as bass
import concourse.mybir as mybir
from concourse.bass_utils import run_bass_kernel_spmd

F32 = mybir.dt.float32
BF16 = mybir.dt.bfloat16
AF = mybir.ActivationFunctionType
ALU = mybir.AluOpType
AX = mybir.AxisListType.X
BF = ml_dtypes.bfloat16

NCORES = 8
SEQ = 2048
D = 1024
TPC = 2 * SEQ
TT = 512
NTILES = TPC // TT
DH = 4096
EPS = 1e-5
DEBUG_TAPS = False


def _reg_sem(nc, name, persist=False):
    attr = "_sem_persist" if persist else "_sem_registry"
    reg = getattr(nc, attr, None)
    if reg is None:
        reg = {}
        setattr(nc, attr, reg)
    if name not in reg:
        reg[name] = nc.alloc_semaphore(name)
    return reg[name]


def _emit_sem_reset(nc):
    with nc.Block() as rb:
        @rb.sync
        def _(sp):
            for h in nc._sem_registry.values():
                sp.sem_clear(h)


class S:
    def __init__(self, nc, name, persist=False):
        self.h = _reg_sem(nc, name, persist)
        self.n = 0

    def inc(self, instr, k=1):
        instr.then_inc(self.h, k)
        self.n += k
        return instr


def _sems(nc, names):
    return {nm: S(nc, nm) for nm in names}


class Chain:

    def __init__(self, eng, util_sem):
        self.e = eng
        self._u = util_sem
        self._last = None

    def wait_ge(self, h, v):
        self.e.wait_ge(h, v)

    def op(self, name, *a, inc=None, k=1, **kw):
        if self._last is not None:
            self.e.wait_ge(*self._last)
        instr = getattr(self.e, name)(*a, **kw)
        if inc is None:
            self._u.inc(instr, 1)
            self._last = (self._u.h, self._u.n)
        else:
            inc.inc(instr, k)
            self._last = (inc.h, inc.n)
        return instr



FFN0_KEYS = ["w1T", "w2T", "h1T", "h2T", "w3T", "s1c", "s2c", "sh1c", "sh2c",
             "s3c", "identb", "identf"]


def emit_ffn0(nc, pfx, x_in, x_out, wd, wscr=None, first=True):
    _st = contextlib.ExitStack()
    sb = lambda n, s, d: _st.enter_context(nc.sbuf_tensor(pfx + n, s, d))
    pst = lambda n, s, d: _st.enter_context(nc.psum_tensor(pfx + n, s, d))
    w1T = wd["w1T"]; w2T = wd["w2T"]; h1T = wd["h1T"]; h2T = wd["h2T"]
    w3T = wd["w3T"]; s1d = wd["s1c"]; s2d = wd["s2c"]; sh1d = wd["sh1c"]
    sh2d = wd["sh2c"]; s3d = wd["s3c"]; idb_d = wd["identb"]; idf_d = wd["identf"]

    x_in_t = x_in.rearrange("(t g p) f -> t p g f", g=4, p=128)
    x_out_t = x_out.rearrange("(t g p) f -> t p g f", g=4, p=128)
    r_w1 = w1T.rearrange("(a p) m -> p a m", p=128)
    r_w2 = w2T.rearrange("(a p) m -> p a m", p=128)
    r_h1 = h1T.rearrange("(a p) m -> p a m", p=128)
    r_h2 = h2T.rearrange("(a p) m -> p a m", p=128)
    r_w3 = w3T.rearrange("(a p) m -> p a m", p=128)

    xt = [sb(f"xt{q}", [128, 4 * D], F32) for q in range(2)]
    xo = sb("xo", [128, 4 * D], F32)
    zt = sb("zt", [128, 4 * D], BF16)
    zf = sb("zf", [128, 8 * TT], BF16)
    sq = sb("sq", [128, D], F32)
    ha = sb("ha", [128, 32 * TT], BF16)
    hb = sb("hb", [128, 32 * TT], BF16)
    yf = sb("yf", [128, 8 * TT], F32)
    wbuf = [sb(f"wbuf{q}", [128, 8 * 512], BF16) for q in range(4)]
    t1 = [sb(f"t1{q}", [128, TT], BF16) for q in range(2)]
    t2 = [sb(f"t2{q}", [128, TT], BF16) for q in range(2)]
    s1c = sb("s1cs", [128, 32], F32)
    s2c = sb("s2cs", [128, 32], F32)
    sh1c = sb("sh1cs", [128, 32], F32)
    sh2c = sb("sh2cs", [128, 32], F32)
    s3c = sb("s3cs", [128, 8], F32)
    idb = sb("idb", [128, 128], BF16)
    idf = sb("idf", [128, 128], F32)
    epsb = sb("epsb", [128, 1], F32)
    sums = sb("sums", [128, 4], F32)
    sumsq = sb("sumsq", [128, 4], F32)
    mu = sb("mu", [128, 4], F32)
    ex2 = sb("ex2", [128, 4], F32)
    musq = sb("musq", [128, 4], F32)
    var = sb("var", [128, 4], F32)
    stdv = sb("stdv", [128, 4], F32)
    istd = sb("istd", [128, 4], F32)

    psA = [pst(f"psA{q}", [128, TT], F32) for q in range(4)]
    psW = pst("psW", [128, D], F32)
    psT = [pst(f"psT{q}", [128, 1024], BF16) for q in range(2)]

    sm = _sems(nc, ["sst", "sKG", "sln", "stp", "szf", "sA1", "sA2",
                    "sE1", "sE2", "sPr", "sM1", "sH1", "sM2", "sH2", "sM3",
                    "sYf", "sYT", "sRes", "sVr", "sSd", "sC"])
    sw4 = [S(nc, pfx + f"sw{q}", persist=True) for q in range(4)]
    swb4 = [S(nc, f"swb4_{q}") for q in range(4)]
    sx2 = [S(nc, f"sx{q}") for q in range(2)]
    VC = Chain(nc.vector, S(nc, "chV"))
    SC = Chain(nc.scalar, S(nc, "chS"))

    if wscr is not None:
        rb_w1 = wscr["w1T"].rearrange("(a p) m -> p a m", p=128)
        rb_w2 = wscr["w2T"].rearrange("(a p) m -> p a m", p=128)
        rb_h1 = wscr["h1T"].rearrange("(a p) m -> p a m", p=128)
        rb_h2 = wscr["h2T"].rearrange("(a p) m -> p a m", p=128)
        rb_w3 = wscr["w3T"].rearrange("(a p) m -> p a m", p=128)

    stripes = []
    for mg in range(8):
        stripes.append(("w1", mg, 0))
        stripes.append(("w2", mg, 0))
    for mg in range(8):
        for kg in range(4):
            stripes.append(("h1", mg, kg))
    for mg in range(8):
        for kg in range(4):
            stripes.append(("h2", mg, kg))
    for mg in range(2):
        for kg in range(4):
            stripes.append(("w3", mg, kg))
    NS = len(stripes)
    stripe_free = {}

    def stripe_src(kind, mg, kg, f32src):
        if f32src:
            t = {"w1": r_w1, "w2": r_w2, "h1": r_h1, "h2": r_h2, "w3": r_w3}[kind]
        else:
            t = {"w1": rb_w1, "w2": rb_w2, "h1": rb_h1, "h2": rb_h2,
                 "w3": rb_w3}[kind]
        if kind in ("w1", "w2"):
            return t[:, :, mg * 512:(mg + 1) * 512]
        return t[:, kg * 8:(kg + 1) * 8, mg * 512:(mg + 1) * 512]

    with nc.Block() as b:
        @b.sync
        def _(sp):
            sp.dma_start(s1c[:], s1d).then_inc(sm["sC"].h, 16)
            sp.dma_start(s2c[:], s2d).then_inc(sm["sC"].h, 16)
            sp.dma_start(sh1c[:], sh1d).then_inc(sm["sC"].h, 16)
            sp.dma_start(sh2c[:], sh2d).then_inc(sm["sC"].h, 16)
            sp.dma_start(s3c[:], s3d).then_inc(sm["sC"].h, 16)
            sp.dma_start(idb[:], idb_d).then_inc(sm["sC"].h, 16)
            sp.dma_start(idf[:], idf_d).then_inc(sm["sC"].h, 16)
            sm["sC"].n = 7 * 16

        for i in range(NTILES):
            @b.sync
            def _(sp, i=i):
                if i >= 2:
                    sp.wait_ge(sm["sRes"].h, 4 * (i - 1))
                sx2[i % 2].inc(sp.dma_start(xt[i % 2][:].rearrange("p (g f) -> p g f", f=D),
                                            x_in_t[i]), 16)
                if i == 0 and wscr is not None and first:
                    for idx, (kind, mg, kg) in enumerate(stripes):
                        sp.wait_ge(sw4[idx % 4].h, 16 * (idx // 4 + 1))
                        dstw = stripe_src(kind, mg, kg, False)
                        swb4[idx % 4].inc(sp.dma_start(
                            dstw,
                            wbuf[idx % 4][:].rearrange("p (a m) -> p a m", m=512)), 16)
                sp.wait_ge(sm["sRes"].h, 4 * (i + 1))
                sm["sst"].inc(sp.dma_start(x_out_t[i],
                                           xo[:].rearrange("p (g f) -> p g f", f=D)), 16)

            @b.vector
            def _(v_, i=i):
                v = VC
                if i == 0:
                    v.wait_ge(sm["sC"].h, 7 * 16)
                    v.op("memset", epsb[:], EPS)
                v.wait_ge(sx2[i % 2].h, 16 * (i // 2 + 1))
                for g in range(4):
                    xg = xt[i % 2][:, g * D:(g + 1) * D]
                    v.op("tensor_tensor", sq[:], xg, xg, ALU.mult)
                    v.op("reduce_sum", sumsq[:, g:g + 1], sq[:], AX)
                    v.op("reduce_sum", sums[:, g:g + 1], xg, AX)
                v.op("tensor_scalar_mul", mu[:], sums[:], 1.0 / D)
                v.op("tensor_scalar_mul", ex2[:], sumsq[:], 1.0 / D)
                v.op("tensor_tensor", musq[:], mu[:], mu[:], ALU.mult)
                v.op("tensor_tensor", var[:], ex2[:], musq[:], ALU.subtract,
                     inc=sm["sVr"])
                v.wait_ge(sm["sSd"].h, i + 1)
                v.op("reciprocal", istd[:], stdv[:])
                v.wait_ge(sm["stp"].h, 8 * i)
                for g in range(4):
                    v.op("tensor_scalar",
                         zt[:, g * D:(g + 1) * D], xt[i % 2][:, g * D:(g + 1) * D],
                         mu[:, g:g + 1], istd[:, g:g + 1], ALU.subtract, ALU.mult,
                         inc=sm["sln"])
                v.wait_ge(sm["sA2"].h, 32 * i)
                for c in range(8):
                    v.wait_ge(sm["stp"].h, i * 8 + c + 1)
                    v.op("tensor_copy", zf[:, c * TT:(c + 1) * TT],
                         psT[c % 2][:, 0:TT], inc=sm["szf"])
                v.wait_ge(sm["sM3"].h, 8 * i)
                for mci in range(32):
                    G = i * 32 + mci
                    v.wait_ge(sm["sA2"].h, G + 1)
                    v.op("tensor_scalar_mul", t2[mci % 2][:],
                         psA[2 + mci % 2][:], s2c[:, mci:mci + 1], inc=sm["sE2"])
                    v.wait_ge(sm["sE1"].h, G + 1)
                    v.op("tensor_tensor", ha[:, mci * TT:(mci + 1) * TT],
                         t1[mci % 2][:], t2[mci % 2][:], ALU.mult, inc=sm["sPr"])
                if i >= 1:
                    v.wait_ge(sm["sst"].h, 16 * i)
                for g in range(4):
                    v.wait_ge(sm["sYT"].h, i * 4 + g + 1)
                    v.op("scalar_tensor_tensor",
                         xo[:, g * D:(g + 1) * D], psW[:, 0:D], 0.5,
                         xt[i % 2][:, g * D:(g + 1) * D], ALU.mult, ALU.add,
                         inc=sm["sRes"])

            @b.scalar
            def _(s_, i=i):
                s = SC
                if i == 0:
                    s.wait_ge(sm["sC"].h, 7 * 16)
                s.wait_ge(sm["sVr"].h, i + 1)
                s.op("activation", stdv[:], var[:], AF.Sqrt, bias=epsb[:, 0:1],
                     inc=sm["sSd"])
                s.wait_ge(sm["sYT"].h, 4 * i)
                for mci in range(32):
                    G = i * 32 + mci
                    s.wait_ge(sm["sA1"].h, G + 1)
                    if G >= 2:
                        s.wait_ge(sm["sPr"].h, G - 1)
                    s.op("activation", t1[mci % 2][:], psA[mci % 2][:],
                         AF.Silu, scale=s1c[:, mci:mci + 1], inc=sm["sE1"])
                s.wait_ge(sm["sM2"].h, 32 * i)
                for mci in range(32):
                    s.wait_ge(sm["sM1"].h, i * 32 + mci + 1)
                    s.op("activation", hb[:, mci * TT:(mci + 1) * TT],
                         psA[mci % 4][:], AF.Gelu, scale=sh1c[:, mci:mci + 1],
                         inc=sm["sH1"])
                for mci in range(32):
                    s.wait_ge(sm["sM2"].h, i * 32 + mci + 1)
                    s.op("activation", ha[:, mci * TT:(mci + 1) * TT],
                         psA[mci % 4][:], AF.Gelu, scale=sh2c[:, mci:mci + 1],
                         inc=sm["sH2"])
                for mci in range(8):
                    s.wait_ge(sm["sM3"].h, i * 8 + mci + 1)
                    s.op("activation", yf[:, mci * TT:(mci + 1) * TT],
                         psA[mci % 4][:], AF.Copy, scale=s3c[:, mci:mci + 1],
                         inc=sm["sYf"])

            @b.tensor
            def _(t, i=i):
                J0 = i * NS
                if i == 0:
                    t.wait_ge(sm["sC"].h, 7 * 16)
                t.wait_ge(sm["sln"].h, 4 * (i + 1))
                for c in range(8):
                    C = i * 8 + c
                    if C >= 2:
                        t.wait_ge(sm["szf"].h, C - 1)
                    for g in range(4):
                        tr = t.transpose(psT[c % 2][:, g * 128:(g + 1) * 128],
                                         zt[:, g * D + c * 128:g * D + c * 128 + 128],
                                         idb[:])
                    sm["stp"].inc(tr)
                t.wait_ge(sm["szf"].h, 8 * (i + 1))
                t.wait_ge(sm["sYf"].h, 8 * i)
                for mg in range(8):
                    Jw1 = J0 + 2 * mg
                    Jw2 = Jw1 + 1
                    t.wait_ge(sw4[Jw1 % 4].h, 16 * (Jw1 // 4 + 1))
                    t.wait_ge(sw4[Jw2 % 4].h, 16 * (Jw2 // 4 + 1))
                    for mc in range(4):
                        mci = mg * 4 + mc
                        G = i * 32 + mci
                        if G >= 2:
                            t.wait_ge(sm["sE1"].h, G - 1)
                        for k in range(8):
                            mm = t.matmul(psA[mci % 2][:],
                                          wbuf[Jw1 % 4][:, k * 512 + mc * 128:k * 512 + mc * 128 + 128],
                                          zf[:, k * TT:(k + 1) * TT],
                                          start=(k == 0), stop=(k == 7))
                        sm["sA1"].inc(mm)
                        if G >= 2:
                            t.wait_ge(sm["sE2"].h, G - 1)
                        for k in range(8):
                            mm = t.matmul(psA[2 + mci % 2][:],
                                          wbuf[Jw2 % 4][:, k * 512 + mc * 128:k * 512 + mc * 128 + 128],
                                          zf[:, k * TT:(k + 1) * TT],
                                          start=(k == 0), stop=(k == 7))
                        sm["sA2"].inc(mm)
                    stripe_free[Jw1] = (sm["sE1"], i * 32 + mg * 4 + 4)
                    stripe_free[Jw2] = (sm["sE2"], i * 32 + mg * 4 + 4)
                for kind, s_in, sMx, sHx, base in (("h1", ha, sm["sM1"], sm["sH1"], J0 + 16),
                                                   ("h2", hb, sm["sM2"], sm["sH2"], J0 + 48)):
                    if kind == "h1":
                        t.wait_ge(sm["sPr"].h, 32 * (i + 1))
                    else:
                        t.wait_ge(sm["sH1"].h, 32 * (i + 1))
                    for mg in range(8):
                        for kg in range(4):
                            J = base + mg * 4 + kg
                            t.wait_ge(sw4[J % 4].h, 16 * (J // 4 + 1))
                            for mc in range(4):
                                mci = mg * 4 + mc
                                if kg == 0 and mg >= 1:
                                    t.wait_ge(sHx.h, i * 32 + (mg - 1) * 4 + mc + 1)
                                for k in range(8):
                                    mm = t.matmul(
                                        psA[mc][:],
                                        wbuf[J % 4][:, k * 512 + mc * 128:k * 512 + mc * 128 + 128],
                                        s_in[:, (kg * 8 + k) * TT:(kg * 8 + k + 1) * TT],
                                        start=(kg == 0 and k == 0),
                                        stop=(kg == 3 and k == 7))
                                if kg == 3:
                                    sMx.inc(mm)
                            if kg < 3:
                                sm["sKG"].inc(mm)
                                stripe_free[J] = (sm["sKG"], sm["sKG"].n)
                            else:
                                stripe_free[J] = (sMx, i * 32 + mg * 4 + 4)
                t.wait_ge(sm["sH2"].h, 32 * (i + 1))
                for mg in range(2):
                    for kg in range(4):
                        J = J0 + 80 + mg * 4 + kg
                        t.wait_ge(sw4[J % 4].h, 16 * (J // 4 + 1))
                        for mc in range(4):
                            mci = mg * 4 + mc
                            if kg == 0 and mg >= 1:
                                t.wait_ge(sm["sYf"].h, i * 8 + (mg - 1) * 4 + mc + 1)
                            for k in range(8):
                                mm = t.matmul(
                                    psA[mc][:],
                                    wbuf[J % 4][:, k * 512 + mc * 128:k * 512 + mc * 128 + 128],
                                    ha[:, (kg * 8 + k) * TT:(kg * 8 + k + 1) * TT],
                                    start=(kg == 0 and k == 0),
                                    stop=(kg == 3 and k == 7))
                            if kg == 3:
                                sm["sM3"].inc(mm)
                        if kg < 3:
                            sm["sKG"].inc(mm)
                            stripe_free[J] = (sm["sKG"], sm["sKG"].n)
                        else:
                            stripe_free[J] = (sm["sM3"], i * 8 + mg * 4 + 4)
                t.wait_ge(sm["sYf"].h, 8 * (i + 1))
                for g in range(4):
                    t.wait_ge(sm["sRes"].h, 4 * i + g)
                    for c in range(8):
                        tr = t.transpose(psW[:, c * 128:(c + 1) * 128],
                                         yf[:, c * TT + g * 128:c * TT + g * 128 + 128],
                                         idf[:])
                    sm["sYT"].inc(tr)

            @b.gpsimd
            def _(g, i=i):
                wb_on = wscr is not None and first
                for idx, (kind, mg, kg) in enumerate(stripes):
                    J = i * NS + idx
                    if J >= 4:
                        fs, fv = stripe_free[J - 4]
                        g.wait_ge(fs.h, fv)
                        if wb_on and J - 4 < NS:
                            g.wait_ge(swb4[(J - 4) % 4].h,
                                      16 * ((J - 4) // 4 + 1))
                    f32src = (wscr is None) or (first and i == 0)
                    src = stripe_src(kind, mg, kg, f32src)
                    dst = wbuf[J % 4][:].rearrange("p (a m) -> p a m", m=512)
                    sw4[J % 4].inc(g.dma_start(dst, src), 16)

        if DEBUG_TAPS:
            @b.gpsimd
            def _(g):
                g.wait_ge(sm["sst"].h, 16 * NTILES)
                for nm, buf in (("d_zt", zt), ("d_zf", zf), ("d_ha", ha),
                                ("d_hb", hb), ("d_yf", yf), ("d_xt", xt[(NTILES - 1) % 2])):
                    dt_ = nc.dram_tensor(nm, list(buf[:].shape), buf.dtype,
                                         kind="ExternalOutput").ap()
                    g.dma_start(dt_, buf[:]).then_inc(sm["sC"].h, 16)
                g.wait_ge(sm["sC"].h, sm["sC"].n + 6 * 16)

        @b.sync
        def _(sp):
            sp.wait_ge(sm["sst"].h, 16 * NTILES)

    _st.close()
    _emit_sem_reset(nc)


def _declare_io(nc, keys, shapes):
    wd = {}
    for k in keys:
        shp, dt = shapes[k]
        wd[k] = nc.dram_tensor(k, shp, dt, kind="ExternalInput").ap()
    return wd


_SHAPES = {
    "w1T": ([D, DH], F32), "w2T": ([D, DH], F32), "h1T": ([DH, DH], F32),
    "h2T": ([DH, DH], F32), "w3T": ([DH, D], F32), "s1c": ([128, 32], F32),
    "s2c": ([128, 32], F32), "sh1c": ([128, 32], F32), "sh2c": ([128, 32], F32),
    "s3c": ([128, 8], F32), "identb": ([128, 128], BF16), "identf": ([128, 128], F32),
    "WthT": ([D, 256], F32), "WphT": ([D, 256], F32), "WgT": ([D, D], F32),
    "WrT": ([D, 8], F32), "ATall": ([D, 512], F32), "Bpk": ([512, D], F32),
    "Eall": ([8, 512], F32),
    "WQT": ([D, D], F32), "WKT": ([D, D], F32), "WVT": ([D, D], F32),
    "WOT": ([D, D], F32), "Ctab": ([128, SEQ], F32), "Stab": ([128, SEQ], F32),
    "Mtab": ([128, 2048], F32), "Mtabb": ([128, 2048], BF16),
    "Pswap": ([128, 128], BF16),
}


def build_ffn0():
    nc = bass.Bass("TRN2", target_bir_lowering=False)
    x_in = nc.dram_tensor("x_in", [TPC, D], F32, kind="ExternalInput").ap()
    x_out = nc.dram_tensor("x_out", [TPC, D], F32, kind="ExternalOutput").ap()
    wd = _declare_io(nc, FFN0_KEYS, _SHAPES)
    emit_ffn0(nc, "f_", x_in, x_out, wd)
    return nc




L0_KEYS = ["WthT", "WphT", "WgT", "identb"]


def emit_l0attn(nc, pfx, x_in, x_out, wd):
    _st = contextlib.ExitStack()
    sb = lambda n, s, d: _st.enter_context(nc.sbuf_tensor(pfx + n, s, d))
    pst = lambda n, s, d: _st.enter_context(nc.psum_tensor(pfx + n, s, d))
    wthd = wd["WthT"]; wphd = wd["WphT"]; wgd = wd["WgT"]; idb_d = wd["identb"]

    x_in_t = x_in.rearrange("(t g p) f -> t p g f", g=4, p=128)
    x_in_q = x_in.rearrange("(q p) f -> q p f", p=128)
    x_out_q = x_out.rearrange("(q p) f -> q p f", p=128)
    r_wth = wthd.rearrange("(a p) m -> p a m", p=128)
    r_wph = wphd.rearrange("(a p) m -> p a m", p=128)
    r_wg = wgd.rearrange("(a p) m -> p a m", p=128)

    xt2 = [sb(f"xt2{q}", [128, 4 * D], F32) for q in range(2)]
    xq = [sb(f"xq{q}", [128, D], F32) for q in range(2)]
    xoq = [sb(f"xoq{q}", [128, D], F32) for q in range(2)]
    zt = sb("zt", [128, 4 * D], BF16)
    zf = sb("zf", [128, 8 * SEQ], BF16)
    sq = sb("sq", [128, D], F32)
    th = sb("th", [128, 2 * SEQ], BF16)
    ph = sb("ph", [128, 2 * SEQ], BF16)
    gf = sb("gf", [128, 8 * SEQ], BF16)
    phT = sb("phT", [128, 16 * 256], BF16)
    gT = sb("gT", [128, 16 * D], BF16)
    m_sb = sb("m_sb", [128, 2 * D], BF16)
    wpb = sb("wpb", [128, 8 * D], BF16)
    idb = sb("idb", [128, 128], BF16)
    msc = sb("msc", [128, 1], F32)
    sums = sb("sums", [128, 4], F32)
    sumsq = sb("sumsq", [128, 4], F32)
    mu = sb("mu", [128, 4], F32)
    ex2 = sb("ex2", [128, 4], F32)
    musq = sb("musq", [128, 4], F32)
    var = sb("var", [128, 4], F32)
    stdv = sb("stdv", [128, 4], F32)
    istd = sb("istd", [128, 4], F32)
    epsb = sb("epsb", [128, 1], F32)

    psA = [pst(f"psA{q}", [128, 512], F32) for q in range(4)]
    psW = pst("psW", [128, D], F32)
    psT = [pst(f"psT{q}", [128, 1024], BF16) for q in range(2)]

    sm = _sems(nc, ["sC", "sst", "sln", "stp", "szf", "sMM", "sEv", "sTB",
                    "sTE", "sMs", "sVr", "sSd", "sRes"])
    sxt = [S(nc, f"sxt{q}") for q in range(2)]
    sst2 = [S(nc, f"sst2{q}") for q in range(2)]
    sxq = [S(nc, f"sxq{q}") for q in range(2)]
    swb = S(nc, pfx + "swb", persist=True)
    VC = Chain(nc.vector, S(nc, "chV"))
    SC = Chain(nc.scalar, S(nc, "chS"))

    def wview(mwid):
        return wpb[:, 0:8 * mwid].rearrange("p (a m) -> p a m", m=mwid)

    with nc.Block() as b0:
        @b0.sync
        def _(sp):
            sm["sC"].inc(sp.dma_start(idb[:], idb_d), 16)

        @b0.vector
        def _(v_):
            VC.op("memset", epsb[:], EPS)
            VC.op("memset", msc[:], 1.0 / 32.0)

    for bq in range(2):
        with nc.Block() as b:
            @b.sync
            def _(sp, bq=bq):
                for tau in range(4):
                    gt = bq * 4 + tau
                    if gt >= 2:
                        sp.wait_ge(sm["sln"].h, 4 * (gt - 1))
                    sxt[tau % 2].inc(sp.dma_start(
                        xt2[tau % 2][:].rearrange("p (g f) -> p g f", f=D),
                        x_in_t[gt]), 16)
                for qt in range(16):
                    gq = bq * 16 + qt
                    if gq >= 2:
                        sp.wait_ge(sm["sRes"].h, 2 * (gq - 1))
                    sxq[qt % 2].inc(sp.dma_start(xq[qt % 2][:], x_in_q[gq]), 16)
                    sp.wait_ge(sm["sRes"].h, 2 * (gq + 1))
                    sst2[qt % 2].inc(sp.dma_start(x_out_q[gq], xoq[qt % 2][:]), 16)

            @b.gpsimd
            def _(g, bq=bq):
                if bq == 0:
                    g.wait_ge(sm["sC"].h, 16)
                swb.inc(g.dma_start(wview(256), r_wth), 16)
                g.wait_ge(sm["sMM"].h, bq * 52 + 8)
                swb.inc(g.dma_start(wview(256), r_wph), 16)
                g.wait_ge(sm["sMM"].h, bq * 52 + 16)
                swb.inc(g.dma_start(wview(D), r_wg), 16)

            @b.vector
            def _(v_, bq=bq):
                v = VC
                for tau in range(4):
                    gt = bq * 4 + tau
                    v.wait_ge(sxt[tau % 2].h, 16 * (gt // 2 + 1))
                    for g in range(4):
                        xg = xt2[tau % 2][:, g * D:(g + 1) * D]
                        v.op("tensor_tensor", sq[:], xg, xg, ALU.mult)
                        v.op("reduce_sum", sumsq[:, g:g + 1], sq[:], AX)
                        v.op("reduce_sum", sums[:, g:g + 1], xg, AX)
                    v.op("tensor_scalar_mul", mu[:], sums[:], 1.0 / D)
                    v.op("tensor_scalar_mul", ex2[:], sumsq[:], 1.0 / D)
                    v.op("tensor_tensor", musq[:], mu[:], mu[:], ALU.mult)
                    v.op("tensor_tensor", var[:], ex2[:], musq[:], ALU.subtract,
                         inc=sm["sVr"])
                    v.wait_ge(sm["sSd"].h, gt + 1)
                    v.op("reciprocal", istd[:], stdv[:])
                    v.wait_ge(sm["stp"].h, 8 * gt)
                    for g in range(4):
                        v.op("tensor_scalar", zt[:, g * D:(g + 1) * D],
                             xt2[tau % 2][:, g * D:(g + 1) * D],
                             mu[:, g:g + 1], istd[:, g:g + 1],
                             ALU.subtract, ALU.mult, inc=sm["sln"])
                    for c in range(8):
                        CB = gt * 8 + c
                        v.wait_ge(sm["stp"].h, CB + 1)
                        v.op("tensor_copy",
                             zf[:, c * SEQ + tau * 512:c * SEQ + tau * 512 + 512],
                             psT[c % 2][:, 0:512], inc=sm["szf"])
                for kt in range(16):
                    v.wait_ge(sm["sTB"].h, bq * 32 + 2 * kt + 1)
                    v.op("tensor_copy", phT[:, kt * 256:(kt + 1) * 256],
                         psT[0][:, 0:256], inc=sm["sTE"])
                    v.wait_ge(sm["sTB"].h, bq * 32 + 2 * kt + 2)
                    v.op("tensor_copy", gT[:, kt * D:(kt + 1) * D],
                         psT[1][:, 0:D], inc=sm["sTE"])
                for qt in range(16):
                    gq = bq * 16 + qt
                    if gq >= 2:
                        v.wait_ge(sst2[gq % 2].h, 16 * (gq // 2))
                    v.wait_ge(sxq[qt % 2].h, 16 * (gq // 2 + 1))
                    for oc in range(2):
                        v.wait_ge(sm["sMs"].h, 2 * gq + oc + 1)
                        v.op("tensor_tensor",
                             xoq[qt % 2][:, oc * 512:(oc + 1) * 512],
                             psA[2 * (qt % 2) + oc][:],
                             xq[qt % 2][:, oc * 512:oc * 512 + 512],
                             ALU.add, inc=sm["sRes"])

            @b.scalar
            def _(s_, bq=bq):
                s = SC
                for tau in range(4):
                    s.wait_ge(sm["sVr"].h, bq * 4 + tau + 1)
                    s.op("activation", stdv[:], var[:], AF.Sqrt,
                         bias=epsb[:, 0:1], inc=sm["sSd"])
                for pi, (nmc, obuf, owid) in enumerate(
                        ((2, th, SEQ), (2, ph, SEQ), (8, gf, SEQ))):
                    base = [0, 8, 16][pi] + bq * 52
                    for mc in range(nmc):
                        for nt in range(4):
                            gidx = base + mc * 4 + nt
                            s.wait_ge(sm["sMM"].h, gidx + 1)
                            s.op("activation",
                                 obuf[:, mc * owid + nt * 512:mc * owid + nt * 512 + 512],
                                 psA[gidx % 4][:], AF.Copy, inc=sm["sEv"])
                for idx in range(4):
                    s.wait_ge(sm["sMM"].h, bq * 52 + 48 + idx + 1)
                    lc, oc = idx // 2, idx % 2
                    s.op("activation",
                         m_sb[:, lc * D + oc * 512:lc * D + oc * 512 + 512],
                         psA[idx][:], AF.Copy, scale=msc[:, 0:1], inc=sm["sEv"])

            @b.tensor
            def _(t, bq=bq):
                if bq == 0:
                    t.wait_ge(sm["sC"].h, 16)
                for tau in range(4):
                    gt = bq * 4 + tau
                    t.wait_ge(sm["sln"].h, 4 * (gt + 1))
                    for c in range(8):
                        CB = gt * 8 + c
                        if CB >= 2:
                            t.wait_ge(sm["szf"].h, CB - 1)
                        for g in range(4):
                            tr = t.transpose(psT[c % 2][:, g * 128:(g + 1) * 128],
                                             zt[:, g * D + c * 128:g * D + c * 128 + 128],
                                             idb[:])
                        sm["stp"].inc(tr)
                t.wait_ge(sm["szf"].h, 32 * (bq + 1))
                gidx = bq * 52
                for wi, (nmc, mwid) in enumerate(((2, 256), (2, 256), (8, D))):
                    t.wait_ge(swb.h, 16 * (bq * 3 + wi + 1))
                    for mc in range(nmc):
                        for nt in range(4):
                            if gidx >= 4 + bq * 52:
                                t.wait_ge(sm["sEv"].h, gidx - 3)
                            for k in range(8):
                                mm = t.matmul(psA[gidx % 4][:],
                                              wpb[:, k * mwid + mc * 128:k * mwid + mc * 128 + 128],
                                              zf[:, k * SEQ + nt * 512:k * SEQ + nt * 512 + 512],
                                              start=(k == 0), stop=(k == 7))
                            sm["sMM"].inc(mm)
                            gidx += 1
                t.wait_ge(sm["sEv"].h, bq * 52 + 16)
                for kt in range(16):
                    TB = bq * 32 + 2 * kt
                    if TB >= 2:
                        t.wait_ge(sm["sTE"].h, TB - 1)
                    for lc in range(2):
                        tr = t.transpose(psT[0][:, lc * 128:(lc + 1) * 128],
                                         ph[:, lc * SEQ + kt * 128:lc * SEQ + kt * 128 + 128],
                                         idb[:])
                    sm["sTB"].inc(tr)
                    if TB + 1 >= 2:
                        t.wait_ge(sm["sTE"].h, TB)
                    if kt == 0:
                        t.wait_ge(sm["sEv"].h, bq * 52 + 48)
                    for mc in range(8):
                        tr = t.transpose(psT[1][:, mc * 128:(mc + 1) * 128],
                                         gf[:, mc * SEQ + kt * 128:mc * SEQ + kt * 128 + 128],
                                         idb[:])
                    sm["sTB"].inc(tr)
                t.wait_ge(sm["sTE"].h, 32 * (bq + 1))
                for idx in range(4):
                    lc, oc = idx // 2, idx % 2
                    for kt in range(16):
                        mm = t.matmul(psA[idx][:],
                                      phT[:, kt * 256 + lc * 128:kt * 256 + lc * 128 + 128],
                                      gT[:, kt * D + oc * 512:kt * D + oc * 512 + 512],
                                      start=(kt == 0), stop=(kt == 15))
                    sm["sMM"].inc(mm)
                t.wait_ge(sm["sEv"].h, bq * 52 + 52)
                for qt in range(16):
                    gq = bq * 16 + qt
                    if gq >= 2:
                        t.wait_ge(sm["sRes"].h, 2 * (gq - 1))
                    for oc in range(2):
                        for lc in range(2):
                            mm = t.matmul(psA[2 * (qt % 2) + oc][:],
                                          th[:, lc * SEQ + qt * 128:lc * SEQ + qt * 128 + 128],
                                          m_sb[:, lc * D + oc * 512:lc * D + oc * 512 + 512],
                                          start=(lc == 0), stop=(lc == 1))
                        sm["sMs"].inc(mm)

    with nc.Block() as bf:
        @bf.sync
        def _(sp):
            sp.wait_ge(sst2[0].h, 16 * 16)
            sp.wait_ge(sst2[1].h, 16 * 16)

    _st.close()
    _emit_sem_reset(nc)


def build_l0attn():
    nc = bass.Bass("TRN2", target_bir_lowering=False)
    x_in = nc.dram_tensor("x_in", [TPC, D], F32, kind="ExternalInput").ap()
    x_out = nc.dram_tensor("x_out", [TPC, D], F32, kind="ExternalOutput").ap()
    wd = _declare_io(nc, L0_KEYS, _SHAPES)
    emit_l0attn(nc, "a_", x_in, x_out, wd)
    return nc




L1F_KEYS = ["WrT", "ATall", "Bpk", "Eall", "identb", "identf"]


def emit_l1ffn(nc, pfx, x_in, x_out, wd):
    _st = contextlib.ExitStack()
    sb = lambda n, s, d: _st.enter_context(nc.sbuf_tensor(pfx + n, s, d))
    pst = lambda n, s, d: _st.enter_context(nc.psum_tensor(pfx + n, s, d))
    wrd = wd["WrT"]; atd = wd["ATall"]; bpd = wd["Bpk"]; ed = wd["Eall"]
    idb_d = wd["identb"]; idf_d = wd["identf"]

    x_in_t = x_in.rearrange("(t g p) f -> t p g f", g=4, p=128)
    x_out_t = x_out.rearrange("(t g p) f -> t p g f", g=4, p=128)
    x_cls = x_in.rearrange("(b s) f -> b s f", s=SEQ)[:, 0, :]

    xt = [sb(f"xt{q}", [128, 4 * D], F32) for q in range(2)]
    xo = sb("xo", [128, 4 * D], F32)
    zt = sb("zt", [128, 4 * D], BF16)
    zf = sb("zf", [128, 8 * TT], BF16)
    sq = sb("sq", [128, D], F32)
    xzb = sb("xzb", [128, D], F32)
    mid_sb = sb("mid_sb", [128, 4 * TT], BF16)
    wf = sb("wf", [128, 8 * TT], F32)
    ats = sb("ats", [128, 8 * 512], BF16)
    bps = sb("bps", [128, 4 * D], BF16)
    wrs = sb("wrs", [128, 8 * 8], BF16)
    ebs = sb("ebs", [8, 512], BF16)
    cls_x = sb("cls_x", [2, D], F32)
    clsz = sb("clsz", [2, D], BF16)
    cls_fm = sb("cls_fm", [128, 16], BF16)
    lg = sb("lg", [2, 8], F32)
    gsc = sb("gsc", [2, 16], F32)
    gs2 = sb("gs2", [2, 40], F32)
    gb = sb("gb", [2, 8], BF16)
    gts = sb("gts", [8, 2], BF16)
    g_all = sb("g_all", [128, 8], F32)
    idb = sb("idb", [128, 128], BF16)
    idf = sb("idf", [128, 128], F32)
    epsb = sb("epsb", [128, 1], F32)
    sums = sb("sums", [128, 4], F32)
    sumsq = sb("sumsq", [128, 4], F32)
    mu = sb("mu", [128, 4], F32)
    ex2 = sb("ex2", [128, 4], F32)
    musq = sb("musq", [128, 4], F32)
    var = sb("var", [128, 4], F32)
    stdv = sb("stdv", [128, 4], F32)
    istd = sb("istd", [128, 4], F32)
    half = sb("half", [128, 4], F32)
    aZ = sb("aZ", [128, 4], F32)
    bZmu = sb("bZmu", [128, 4], F32)

    psA = [pst(f"psA{q}", [128, TT], F32) for q in range(4)]
    psW = pst("psW", [128, D], F32)
    psT = [pst(f"psT{q}", [128, 1024], BF16) for q in range(2)]

    sm = _sems(nc, ["sC", "sst", "sln", "stp", "szf", "sMD", "sME", "sEO",
                    "sWf", "sYT", "sRes", "sVr", "sSd", "sLNz", "sTz", "sFM",
                    "sLG", "sGpre", "sLGe", "sTg", "sGT", "sGV", "sGA"])
    sm["sCW"] = S(nc, pfx + "sCW", persist=True)
    sx2 = [S(nc, f"sx{q}") for q in range(2)]
    VC = Chain(nc.vector, S(nc, "chV"))
    SC = Chain(nc.scalar, S(nc, "chS"))

    with nc.Block() as b0:
        @b0.sync
        def _(sp):
            sm["sC"].inc(sp.dma_start(idb[:], idb_d), 16)
            sm["sC"].inc(sp.dma_start(idf[:], idf_d), 16)
            sm["sC"].inc(sp.dma_start(cls_x[:], x_cls), 16)

        @b0.gpsimd
        def _(g):
            sm["sCW"].inc(g.dma_start(
                wrs[:].rearrange("p (a e) -> p a e", e=8),
                wrd.rearrange("(a p) e -> p a e", p=128)), 16)
            sm["sCW"].inc(g.dma_start(
                ats[:].rearrange("p (a m) -> p a m", m=512),
                atd.rearrange("(a p) m -> p a m", p=128)), 16)
            sm["sCW"].inc(g.dma_start(
                bps[:].rearrange("p (a m) -> p a m", m=D),
                bpd.rearrange("(a p) m -> p a m", p=128)), 16)
            sm["sCW"].inc(g.dma_start(ebs[:], ed), 16)

        @b0.vector
        def _(v_):
            v = VC
            v.op("memset", epsb[:], EPS)
            v.wait_ge(sm["sC"].h, 3 * 16)
            v.wait_ge(sm["sCW"].h, 4 * 16)
            v.op("tensor_tensor", sq[0:2, :], cls_x[:], cls_x[:], ALU.mult)
            v.op("reduce_sum", gsc[:, 0:1], sq[0:2, :], AX)
            v.op("reduce_sum", gsc[:, 1:2], cls_x[:], AX)
            v.op("tensor_scalar_mul", gsc[:, 2:3], gsc[:, 1:2], 1.0 / D)
            v.op("tensor_scalar_mul", gsc[:, 3:4], gsc[:, 0:1], 1.0 / D)
            v.op("tensor_tensor", gsc[:, 4:5], gsc[:, 2:3], gsc[:, 2:3], ALU.mult)
            v.op("tensor_tensor", gsc[:, 5:6], gsc[:, 3:4], gsc[:, 4:5],
                 ALU.subtract, inc=sm["sVr"])
            v.wait_ge(sm["sSd"].h, 1)
            v.op("reciprocal", gsc[:, 7:8], gsc[:, 6:7])
            v.op("tensor_scalar", clsz[:], cls_x[:], gsc[:, 2:3], gsc[:, 7:8],
                 ALU.subtract, ALU.mult, inc=sm["sLNz"])
            v.wait_ge(sm["sTz"].h, 1)
            v.op("tensor_copy", cls_fm[:], psT[0][:, 0:16], inc=sm["sFM"])
            v.wait_ge(sm["sLG"].h, 1)
            v.op("tensor_copy", lg[:], psA[0][0:2, 0:8])
            v.op("reduce_max", gsc[:, 8:9], lg[:], AX, negate=True)
            v.op("tensor_scalar", gs2[:, 0:8], lg[:], gsc[:, 8:9], None,
                 ALU.add)
            v.op("tensor_single_scalar", gs2[:, 8:16], gs2[:, 0:8], 0.0, ALU.is_ge)
            v.op("scalar_tensor_tensor", gs2[:, 16:24], gs2[:, 8:16], -1e30,
                 gs2[:, 0:8], ALU.mult, ALU.add)
            v.op("reduce_max", gsc[:, 9:10], gs2[:, 16:24], AX, negate=True)
            v.op("tensor_scalar_mul", gsc[:, 10:11], gsc[:, 9:10], -1.0)
            v.op("tensor_single_scalar", gs2[:, 24:32], gs2[:, 0:8],
                 gsc[:, 10:11], ALU.is_ge, inc=sm["sGpre"])
            v.wait_ge(sm["sSd"].h, 2)
            v.op("tensor_tensor", gs2[:, 32:40], gs2[:, 32:40], gs2[:, 24:32],
                 ALU.mult)
            v.op("reduce_sum", gsc[:, 11:12], gs2[:, 32:40], AX)
            v.op("reciprocal", gsc[:, 12:13], gsc[:, 11:12])
            v.op("tensor_scalar_mul", gb[:], gs2[:, 32:40], gsc[:, 12:13],
                 inc=sm["sLGe"])
            v.wait_ge(sm["sTg"].h, 1)
            v.op("tensor_copy", gts[:], psT[0][0:8, 16:18], inc=sm["sGT"])
            v.wait_ge(sm["sGV"].h, 4)
            v.op("tensor_copy", g_all[:], psA[1][:, 0:8], inc=sm["sGA"])

        @b0.scalar
        def _(s_):
            s = SC
            s.wait_ge(sm["sVr"].h, 1)
            s.op("activation", gsc[:, 6:7], gsc[:, 5:6], AF.Sqrt,
                 bias=epsb[0:2, 0:1], inc=sm["sSd"])
            s.wait_ge(sm["sGpre"].h, 1)
            s.op("activation", gs2[:, 32:40], gs2[:, 0:8], AF.Exp, inc=sm["sSd"])

        @b0.tensor
        def _(t):
            t.wait_ge(sm["sC"].h, 3 * 16)
            t.wait_ge(sm["sCW"].h, 4 * 16)
            t.wait_ge(sm["sLNz"].h, 1)
            for c in range(8):
                tr = t.transpose(psT[0][:, c * 2:(c + 1) * 2],
                                 clsz[:, c * 128:(c + 1) * 128], idb[0:2, 0:2])
            sm["sTz"].inc(tr)
            t.wait_ge(sm["sFM"].h, 1)
            for kc in range(8):
                mm = t.matmul(psA[0][0:2, 0:8], cls_fm[:, kc * 2:(kc + 1) * 2],
                              wrs[:, kc * 8:(kc + 1) * 8],
                              start=(kc == 0), stop=(kc == 7))
            sm["sLG"].inc(mm)
            t.wait_ge(sm["sLGe"].h, 1)
            tr = t.transpose(psT[0][0:8, 16:18], gb[:], idb[0:2, 0:2])
            sm["sTg"].inc(tr)
            t.wait_ge(sm["sGT"].h, 1)
            for p in range(4):
                mm = t.matmul(psA[1][:, p * 2:(p + 1) * 2],
                              ebs[:, p * 128:(p + 1) * 128], gts[:],
                              start=True, stop=True)
                sm["sGV"].inc(mm)

    with nc.Block() as b:
        for i in range(NTILES):
            bqi = i // 4

            @b.sync
            def _(sp, i=i):
                if i >= 2:
                    sp.wait_ge(sm["sRes"].h, 4 * (i - 1))
                sx2[i % 2].inc(sp.dma_start(
                    xt[i % 2][:].rearrange("p (g f) -> p g f", f=D), x_in_t[i]), 16)
                sp.wait_ge(sm["sRes"].h, 4 * (i + 1))
                sm["sst"].inc(sp.dma_start(
                    x_out_t[i], xo[:].rearrange("p (g f) -> p g f", f=D)), 16)

            @b.vector
            def _(v_, i=i):
                v = VC
                v.wait_ge(sx2[i % 2].h, 16 * (i // 2 + 1))
                for g in range(4):
                    xg = xt[i % 2][:, g * D:(g + 1) * D]
                    v.op("tensor_tensor", sq[:], xg, xg, ALU.mult)
                    v.op("reduce_sum", sumsq[:, g:g + 1], sq[:], AX)
                    v.op("reduce_sum", sums[:, g:g + 1], xg, AX)
                v.op("tensor_scalar_mul", mu[:], sums[:], 1.0 / D)
                v.op("tensor_scalar_mul", ex2[:], sumsq[:], 1.0 / D)
                v.op("tensor_tensor", musq[:], mu[:], mu[:], ALU.mult)
                v.op("tensor_tensor", var[:], ex2[:], musq[:], ALU.subtract,
                     inc=sm["sVr"])
                v.wait_ge(sm["sSd"].h, i + 3)
                v.op("reciprocal", istd[:], stdv[:])
                v.op("tensor_scalar_mul", half[:], istd[:], 0.5)
                v.op("tensor_scalar_add", aZ[:], half[:], 1.0)
                v.op("tensor_tensor", bZmu[:], mu[:], half[:], ALU.mult)
                v.wait_ge(sm["stp"].h, 8 * i)
                for g in range(4):
                    v.op("tensor_scalar", zt[:, g * D:(g + 1) * D],
                         xt[i % 2][:, g * D:(g + 1) * D],
                         mu[:, g:g + 1], istd[:, g:g + 1],
                         ALU.subtract, ALU.mult, inc=sm["sln"])
                v.wait_ge(sm["sMD"].h, 4 * i)
                for c in range(8):
                    v.wait_ge(sm["stp"].h, i * 8 + c + 1)
                    v.op("tensor_copy", zf[:, c * TT:(c + 1) * TT],
                         psT[c % 2][:, 0:TT], inc=sm["szf"])
                if i >= 1:
                    v.wait_ge(sm["sst"].h, 16 * i)
                for g in range(4):
                    v.wait_ge(sm["sYT"].h, i * 4 + g + 1)
                    v.op("tensor_scalar", xzb[:], xt[i % 2][:, g * D:(g + 1) * D],
                         aZ[:, g:g + 1], bZmu[:, g:g + 1], ALU.mult, ALU.subtract)
                    v.op("scalar_tensor_tensor", xo[:, g * D:(g + 1) * D],
                         psW[:, 0:D], 0.5, xzb[:], ALU.mult, ALU.add,
                         inc=sm["sRes"])

            @b.scalar
            def _(s_, i=i):
                s = SC
                s.wait_ge(sm["sVr"].h, i + 2)
                s.op("activation", stdv[:], var[:], AF.Sqrt, bias=epsb[:, 0:1],
                     inc=sm["sSd"])
                s.wait_ge(sm["sGA"].h, 1)
                for p in range(4):
                    s.wait_ge(sm["sMD"].h, i * 4 + p + 1)
                    s.op("activation", mid_sb[:, p * TT:(p + 1) * TT],
                         psA[p][:], AF.Copy,
                         scale=g_all[:, (p * 2 + (i // 4)):(p * 2 + (i // 4)) + 1],
                         inc=sm["sME"])
                if i >= 1:
                    s.wait_ge(sm["sYT"].h, 4 * i)
                for m in range(8):
                    s.wait_ge(sm["sEO"].h, i * 8 + m + 1)
                    s.op("activation", wf[:, m * TT:(m + 1) * TT],
                         psA[m % 2][:], AF.Copy, inc=sm["sWf"])

            @b.tensor
            def _(t, i=i):
                if i == 0:
                    t.wait_ge(sm["sGA"].h, 1)
                t.wait_ge(sm["sln"].h, 4 * (i + 1))
                for c in range(8):
                    CB = i * 8 + c
                    if CB >= 2:
                        t.wait_ge(sm["szf"].h, CB - 1)
                    for g in range(4):
                        tr = t.transpose(psT[c % 2][:, g * 128:(g + 1) * 128],
                                         zt[:, g * D + c * 128:g * D + c * 128 + 128],
                                         idb[:])
                    sm["stp"].inc(tr)
                t.wait_ge(sm["szf"].h, 8 * (i + 1))
                if i >= 1:
                    t.wait_ge(sm["sME"].h, 4 * i)
                for p in range(4):
                    for k in range(8):
                        mm = t.matmul(psA[p][:],
                                      ats[:, k * 512 + p * 128:k * 512 + p * 128 + 128],
                                      zf[:, k * TT:(k + 1) * TT],
                                      start=(k == 0), stop=(k == 7))
                    sm["sMD"].inc(mm)
                t.wait_ge(sm["sME"].h, 4 * (i + 1))
                for m in range(8):
                    EOg = i * 8 + m
                    if EOg >= 2:
                        t.wait_ge(sm["sWf"].h, EOg - 1)
                    for p in range(4):
                        mm = t.matmul(psA[m % 2][:],
                                      bps[:, p * D + m * 128:p * D + m * 128 + 128],
                                      mid_sb[:, p * TT:(p + 1) * TT],
                                      start=(p == 0), stop=(p == 3))
                    sm["sEO"].inc(mm)
                t.wait_ge(sm["sWf"].h, 8 * (i + 1))
                for g in range(4):
                    t.wait_ge(sm["sRes"].h, 4 * i + g)
                    for c in range(8):
                        tr = t.transpose(psW[:, c * 128:(c + 1) * 128],
                                         wf[:, c * TT + g * 128:c * TT + g * 128 + 128],
                                         idf[:])
                    sm["sYT"].inc(tr)

    with nc.Block() as bf:
        @bf.sync
        def _(sp):
            sp.wait_ge(sm["sst"].h, 16 * NTILES)

    _st.close()
    _emit_sem_reset(nc)


def build_l1ffn():
    nc = bass.Bass("TRN2", target_bir_lowering=False)
    x_in = nc.dram_tensor("x_in", [TPC, D], F32, kind="ExternalInput").ap()
    x_out = nc.dram_tensor("x_out", [TPC, D], F32, kind="ExternalOutput").ap()
    wd = _declare_io(nc, L1F_KEYS, _SHAPES)
    emit_l1ffn(nc, "m_", x_in, x_out, wd)
    return nc




L1A_KEYS = ["WQT", "WKT", "WVT", "WOT", "Ctab", "Stab", "Mtabb",
            "identb", "identf", "Pswap"]


def emit_l1attn(nc, pfx, x_in, x_out, wd):
    _st = contextlib.ExitStack()
    sb = lambda n, s, d: _st.enter_context(nc.sbuf_tensor(pfx + n, s, d))
    pst = lambda n, s, d: _st.enter_context(nc.psum_tensor(pfx + n, s, d))
    wqd = wd["WQT"]; wkd = wd["WKT"]; wvd = wd["WVT"]; wod = wd["WOT"]
    cthd = wd["Ctab"]; sthd = wd["Stab"]; mskd = wd["Mtabb"]
    idb_d = wd["identb"]; idf_d = wd["identf"]; psw_d = wd["Pswap"]

    x_in_q = x_in.rearrange("(q p) f -> q p f", p=128)
    x_out_q = x_out.rearrange("(q p) f -> q p f", p=128)
    r_w = {"K": wkd.rearrange("(a p) m -> p a m", p=128),
           "Q": wqd.rearrange("(a p) m -> p a m", p=128),
           "V": wvd.rearrange("(a p) m -> p a m", p=128)}
    r_wo = wod.rearrange("(a p) m -> p a m", p=128)

    Kf = sb("Kf", [128, 8 * SEQ], BF16)
    Qf = sb("Qf", [128, 8 * SEQ], BF16)
    VT = sb("VT", [128, 16 * D], BF16)
    wo_sb = sb("wo_sb", [128, 8 * D], BF16)
    idb = sb("idb", [128, 128], BF16)
    idf = sb("idf", [128, 128], F32)
    psw = sb("psw", [128, 128], BF16)
    Ct = sb("Ct", [128, SEQ], F32)
    St = sb("St", [128, SEQ], F32)
    epsb = sb("epsb", [128, 1], F32)
    q8b = sb("q8b", [128, 1], F32)
    stt1 = sb("stt1", [128, 8], F32)

    sm = _sems(nc, ["sC", "sln", "stp", "szf", "sPj", "sStg", "sSw",
                    "sDn", "sVr", "sSd", "sSmm", "sSe", "sNg", "sEx", "sPn",
                    "sPTb", "sPTe", "sAVa", "sAVe", "sWOc", "sYe", "sYTb",
                    "sResB", "sMtl", "sRH"])
    sm["sCW"] = S(nc, pfx + "sCW", persist=True)
    sxg = [S(nc, f"sxg{q}") for q in range(2)]
    sxq = [S(nc, f"sxq{q}") for q in range(2)]
    sstB = [S(nc, f"sstB{q}") for q in range(2)]
    swv = [S(nc, pfx + f"swv{q}", persist=True) for q in range(2)]
    VCH = Chain(nc.vector, S(nc, "chV"))
    SCH = Chain(nc.scalar, S(nc, "chS"))

    with nc.Block() as b0:
        @b0.sync
        def _(sp):
            sm["sC"].inc(sp.dma_start(idb[:], idb_d), 16)
            sm["sC"].inc(sp.dma_start(idf[:], idf_d), 16)
            sm["sC"].inc(sp.dma_start(psw[:], psw_d), 16)
            sm["sC"].inc(sp.dma_start(Ct[:], cthd), 16)
            sm["sC"].inc(sp.dma_start(St[:], sthd), 16)

        @b0.gpsimd
        def _(g):
            sm["sCW"].inc(g.dma_start(
                wo_sb[:].rearrange("p (a m) -> p a m", m=D), r_wo), 16)

        @b0.vector
        def _(v_):
            VCH.op("memset", epsb[:], EPS)
            VCH.op("memset", q8b[:], 0.125)

    for bq in range(2):
        with (
            nc.sbuf_tensor(f"{pfx}ztA{bq}", [128, 2 * D], BF16) as ztA,
            nc.sbuf_tensor(f"{pfx}zfA{bq}", [128, 8 * SEQ], BF16) as zf,
            nc.sbuf_tensor(f"{pfx}xtA{bq}", [128, 2 * D], F32) as xtA,
            nc.sbuf_tensor(f"{pfx}wbA{bq}", [128, 2 * 8 * 512], BF16) as wbA,
            nc.sbuf_tensor(f"{pfx}stg{bq}", [128, 2 * 512], BF16) as stg,
            nc.sbuf_tensor(f"{pfx}v1b{bq}", [128, 512], F32) as v1b,
            nc.sbuf_tensor(f"{pfx}v2b{bq}", [128, 512], F32) as v2b,
            nc.sbuf_tensor(f"{pfx}sqA{bq}", [128, D], F32) as sqA,
            nc.psum_tensor(f"{pfx}psA0{bq}", [128, 512], F32) as psA0,
            nc.psum_tensor(f"{pfx}psA1{bq}", [128, 512], F32) as psA1,
            nc.psum_tensor(f"{pfx}psB0{bq}", [128, 512], F32) as psB0,
            nc.psum_tensor(f"{pfx}psB1{bq}", [128, 512], F32) as psB1,
            nc.psum_tensor(f"{pfx}psTa{bq}", [128, 1024], BF16) as psTa,
            nc.psum_tensor(f"{pfx}psTb{bq}", [128, 1024], BF16) as psTb,
        ):
            psA2 = [psA0, psA1]
            psB2 = [psB0, psB1]
            psT2 = [psTa, psTb]

            def pg_info(pg):
                proj = pg // 32
                mc = (pg % 32) // 4
                nt = pg % 4
                mg = mc // 4
                return proj, mc, nt, mg

            with nc.Block() as b:
                @b.sync
                def _(sp, bq=bq):
                    for g in range(16):
                        gg = bq * 16 + g
                        if gg >= 2:
                            sp.wait_ge(sm["sln"].h, gg - 1)
                        sxg[g % 2].inc(sp.dma_start(
                            xtA[:, (g % 2) * D:(g % 2) * D + D], x_in_q[gg]), 16)

                @b.gpsimd
                def _(g, bq=bq):
                    for si in range(6):
                        SI = bq * 6 + si
                        proj = "KQV"[si // 2]
                        mg = si % 2
                        if SI >= 2:
                            g.wait_ge(sm["sPj"].h, (SI - 1) * 16)
                        swv[SI % 2].inc(g.dma_start(
                            wbA[:, (SI % 2) * 4096:(SI % 2) * 4096 + 4096]
                            .rearrange("p (a m) -> p a m", m=512),
                            r_w[proj][:, :, mg * 512:(mg + 1) * 512]), 16)

                @b.vector
                def _(v_, bq=bq):
                    v = VCH
                    for g in range(16):
                        gg = bq * 16 + g
                        v.wait_ge(sxg[g % 2].h, 16 * (gg // 2 + 1))
                        xg = xtA[:, (g % 2) * D:(g % 2) * D + D]
                        v.op("tensor_tensor", sqA[:], xg, xg, ALU.mult)
                        v.op("reduce_sum", stt1[:, 0:1], sqA[:], AX)
                        v.op("reduce_sum", stt1[:, 1:2], xg, AX)
                        v.op("tensor_scalar_mul", stt1[:, 2:3], stt1[:, 1:2], 1.0 / D)
                        v.op("tensor_scalar_mul", stt1[:, 3:4], stt1[:, 0:1], 1.0 / D)
                        v.op("tensor_tensor", stt1[:, 4:5], stt1[:, 2:3],
                             stt1[:, 2:3], ALU.mult)
                        v.op("tensor_tensor", stt1[:, 5:6], stt1[:, 3:4],
                             stt1[:, 4:5], ALU.subtract, inc=sm["sVr"])
                        v.wait_ge(sm["sSd"].h, gg + 1)
                        v.op("reciprocal", stt1[:, 7:8], stt1[:, 6:7])
                        v.op("tensor_scalar", ztA[:, (g % 2) * D:(g % 2) * D + D],
                             xg, stt1[:, 2:3], stt1[:, 7:8],
                             ALU.subtract, ALU.mult, inc=sm["sln"])
                        v.wait_ge(sm["stp"].h, gg + 1)
                        v.op("tensor_copy",
                             zf[:].rearrange("p (c t) -> p c t", t=SEQ)[:, :, g * 128:(g + 1) * 128],
                             psT2[g % 2][:, 0:1024].rearrange("p (c t) -> p c t", t=128),
                             inc=sm["szf"])
                    for pg in range(96):
                        G = bq * 96 + pg
                        proj, mc, nt, mg = pg_info(pg)
                        v.wait_ge(sm["sSw"].h, G + 1)
                        if proj < 2:
                            v.op("tensor_tensor", v1b[:],
                                 stg[:, (pg % 2) * 512:(pg % 2) * 512 + 512],
                                 Ct[:, nt * 512:(nt + 1) * 512], ALU.mult)
                            v.op("tensor_tensor", v2b[:], psB2[pg % 2][:],
                                 St[:, nt * 512:(nt + 1) * 512], ALU.mult)
                            obuf = Kf if proj == 0 else Qf
                            v.op("tensor_tensor",
                                 obuf[:, mc * SEQ + nt * 512:mc * SEQ + nt * 512 + 512],
                                 v1b[:], v2b[:], ALU.add, inc=sm["sDn"])
                        else:
                            v.op("tensor_copy",
                                 VT[:].rearrange("p (kt d) -> p kt d", d=D)
                                 [:, nt * 4:(nt + 1) * 4, mc * 128:(mc + 1) * 128],
                                 psT2[pg % 2][:, 0:512].rearrange("p (s t) -> p s t", t=128),
                                 inc=sm["sDn"])

                @b.scalar
                def _(s_, bq=bq):
                    s = SCH
                    for g in range(16):
                        gg = bq * 16 + g
                        s.wait_ge(sm["sVr"].h, gg + 1)
                        s.op("activation", stt1[:, 6:7], stt1[:, 5:6], AF.Sqrt,
                             bias=epsb[:, 0:1], inc=sm["sSd"])
                    for pg in range(96):
                        G = bq * 96 + pg
                        proj = pg // 32
                        s.wait_ge(sm["sPj"].h, G + 1)
                        if G >= 2:
                            s.wait_ge(sm["sDn"].h, G - 1)
                            s.wait_ge(sm["sSw"].h, G - 1)
                        if proj == 1:
                            s.op("activation", stg[:, (pg % 2) * 512:(pg % 2) * 512 + 512],
                                 psA2[pg % 2][:], AF.Copy, scale=q8b[:, 0:1],
                                 inc=sm["sStg"])
                        else:
                            s.op("activation", stg[:, (pg % 2) * 512:(pg % 2) * 512 + 512],
                                 psA2[pg % 2][:], AF.Copy, inc=sm["sStg"])

                @b.tensor
                def _(t, bq=bq):
                    if bq == 0:
                        t.wait_ge(sm["sC"].h, 5 * 16)
                        t.wait_ge(sm["sCW"].h, 16)

                    def emit_post(item):
                        G, pg = item
                        proj = pg // 32
                        t.wait_ge(sm["sStg"].h, G + 1)
                        if G >= 2:
                            t.wait_ge(sm["sDn"].h, G - 1)
                        if proj < 2:
                            mm = t.matmul(psB2[pg % 2][:], psw[:],
                                          stg[:, (pg % 2) * 512:(pg % 2) * 512 + 512],
                                          start=True, stop=True)
                            sm["sSw"].inc(mm)
                        else:
                            for sblk in range(4):
                                tr = t.transpose(
                                    psT2[pg % 2][:, sblk * 128:(sblk + 1) * 128],
                                    stg[:, (pg % 2) * 512 + sblk * 128:(pg % 2) * 512 + sblk * 128 + 128],
                                    idb[:])
                            sm["sSw"].inc(tr)

                    for g in range(16):
                        gg = bq * 16 + g
                        t.wait_ge(sm["sln"].h, gg + 1)
                        if gg >= 2:
                            t.wait_ge(sm["szf"].h, gg - 1)
                        for c in range(8):
                            tr = t.transpose(psT2[g % 2][:, c * 128:(c + 1) * 128],
                                             ztA[:, (g % 2) * D + c * 128:(g % 2) * D + c * 128 + 128],
                                             idb[:])
                        sm["stp"].inc(tr)
                    t.wait_ge(sm["szf"].h, 16 * (bq + 1))
                    pend = None
                    for pg in range(96):
                        G = bq * 96 + pg
                        proj, mc, nt, mg = pg_info(pg)
                        SI = bq * 6 + proj * 2 + mg
                        if pg % 16 == 0:
                            t.wait_ge(swv[SI % 2].h, 16 * (SI // 2 + 1))
                        if G >= 2:
                            t.wait_ge(sm["sStg"].h, G - 1)
                        mcl = mc % 4
                        wofs = (SI % 2) * 4096
                        for k in range(8):
                            mm = t.matmul(psA2[pg % 2][:],
                                          wbA[:, wofs + k * 512 + mcl * 128:wofs + k * 512 + mcl * 128 + 128],
                                          zf[:, k * SEQ + nt * 512:k * SEQ + nt * 512 + 512],
                                          start=(k == 0), stop=(k == 7))
                        sm["sPj"].inc(mm)
                        if pend is not None:
                            emit_post(pend)
                        pend = (G, pg)
                    emit_post(pend)

        with (
            nc.sbuf_tensor(f"{pfx}mskb{bq}", [128, 2048], BF16) as mskb,
            nc.sbuf_tensor(f"{pfx}Psb{bq}", [128, 2 * SEQ], BF16) as Psb,
            nc.sbuf_tensor(f"{pfx}PTsb{bq}", [128, 2 * SEQ], BF16) as PTsb,
            nc.sbuf_tensor(f"{pfx}xqb{bq}", [128, 2 * D], F32) as xqb,
            nc.sbuf_tensor(f"{pfx}xob{bq}", [128, 2 * D], F32) as xob,
            nc.sbuf_tensor(f"{pfx}avsb{bq}", [128, 2 * 8 * 128], BF16) as avsb,
            nc.sbuf_tensor(f"{pfx}zhb{bq}", [128, 8], F32) as zhb,
            nc.sbuf_tensor(f"{pfx}ngb{bq}", [128, 4], F32) as ngb,
            nc.psum_tensor(f"{pfx}psS0{bq}", [128, 512], F32) as psS0,
            nc.psum_tensor(f"{pfx}psS1{bq}", [128, 512], F32) as psS1,
            nc.psum_tensor(f"{pfx}psPTa{bq}", [128, 1024], BF16) as psPTa,
            nc.psum_tensor(f"{pfx}psPTb{bq}", [128, 1024], BF16) as psPTb,
            nc.psum_tensor(f"{pfx}psAVa{bq}", [128, 128], F32) as psAVa,
            nc.psum_tensor(f"{pfx}psAVb{bq}", [128, 128], F32) as psAVb,
            nc.psum_tensor(f"{pfx}psWO{bq}", [128, 1024], F32) as psWO,
        ):
            psS2 = [psS0, psS1]
            psPT2 = [psPTa, psPTb]
            psAV2 = [psAVa, psAVb]
            c0 = {"ch": sm["sSmm"].n, "h": sm["sPn"].n, "bat": sm["sPTb"].n,
                  "hp": sm["sAVe"].n, "j": sm["sYe"].n, "av": sm["sAVa"].n,
                  "res": sm["sResB"].n, "ytb": sm["sYTb"].n, "woc": sm["sWOc"].n,
                  "rh": sm["sRH"].n}
            jorder = []
            for k in range(8):
                jorder += [k, 15 - k]
            rank = {j: r for r, j in enumerate(jorder)}
            HL = []
            for k in range(8):
                for h in range(16):
                    HL.append((k, h))
                    HL.append((15 - k, h))

            def nch_of(j):
                return j // 4 + 1

            def nb8(j):
                return (j + 8) // 8

            chA = []; batA = []; oddA = []
            c_ch = c0["ch"]; c_bat = c0["bat"]; c_odd = 0
            for (j, h) in HL:
                chA.append(c_ch); batA.append(c_bat); oddA.append(c_odd)
                c_ch += nch_of(j); c_bat += nb8(j)
                if h % 2 == 1:
                    c_odd += 1
            av_done = {}
            c_odd = 0
            for idx, (j, h) in enumerate(HL):
                if h % 2 == 1:
                    c_odd += 1
                    if h == 15:
                        av_done[j] = c_odd

            with nc.Block() as b:
                @b.sync
                def _(sp, bq=bq):
                    sm["sMtl"].inc(sp.dma_start(mskb[:], mskd), 16)
                    for r, j in enumerate(jorder):
                        gj = bq * 16 + j
                        if r >= 2:
                            sp.wait_ge(sm["sResB"].h, c0["res"] + r - 1)
                        sxq[r % 2].inc(sp.dma_start(
                            xqb[:, (r % 2) * D:(r % 2) * D + D], x_in_q[gj]), 16)
                        sp.wait_ge(sm["sResB"].h, c0["res"] + r + 1)
                        sstB[r % 2].inc(sp.dma_start(
                            x_out_q[gj], xob[:, (r % 2) * D:(r % 2) * D + D]), 16)
                    sp.wait_ge(sstB[0].h, 16 * 8 * (bq + 1))
                    sp.wait_ge(sstB[1].h, 16 * 8 * (bq + 1))

                @b.vector
                def _(v_, bq=bq, c0=c0):
                    v = VCH

                    def pt_evac(idx):
                        j, h = HL[idx]
                        par = idx % 2
                        if idx >= 2:
                            v.wait_ge(sm["sAVa"].h, c0["av"] + idx - 1)
                        bat = batA[idx]
                        for bi in range(nb8(j)):
                            bat += 1
                            wl = 128 * min(8, j + 1 - bi * 8)
                            v.wait_ge(sm["sPTb"].h, bat)
                            v.op("tensor_copy",
                                 PTsb[:, par * SEQ + bi * 1024:par * SEQ + bi * 1024 + wl],
                                 psPT2[par][:, 0:wl], inc=sm["sPTe"])

                    def zdmat(idx):
                        j, h = HL[idx]
                        par = idx % 2
                        nch = nch_of(j)
                        wS = 512 * nch
                        v.wait_ge(sm["sSe"].h, chA[idx] + nch)
                        if nch > 1:
                            v.op("reduce_sum", ngb[:, par:par + 1],
                                 zhb[:, par * 4:par * 4 + nch], AX)
                            v.op("reciprocal", ngb[:, 2 + par:3 + par],
                                 ngb[:, par:par + 1])
                        else:
                            v.op("reciprocal", ngb[:, 2 + par:3 + par],
                                 zhb[:, par * 4:par * 4 + 1])
                        v.op("tensor_scalar_mul",
                             Psb[:, par * SEQ:par * SEQ + wS],
                             Psb[:, par * SEQ:par * SEQ + wS],
                             ngb[:, 2 + par:3 + par], inc=sm["sPn"])

                    def resid(j2):
                        r = rank[j2]
                        v.wait_ge(sm["sWOc"].h, c0["woc"] + 2 * (r + 1))
                        if r >= 2:
                            v.wait_ge(sstB[r % 2].h, 16 * 8 * bq + 16 * (r // 2))
                        v.wait_ge(sxq[r % 2].h, 16 * 8 * bq + 16 * (r // 2 + 1))
                        v.op("tensor_tensor", xob[:, (r % 2) * D:(r % 2) * D + D],
                             psWO[:, 0:D], xqb[:, (r % 2) * D:(r % 2) * D + D],
                             ALU.add, inc=sm["sResB"])

                    for idx in range(len(HL)):
                        if idx >= 2:
                            pt_evac(idx - 2)
                        zdmat(idx)
                        if idx >= 3 and HL[idx - 3][1] == 15:
                            resid(HL[idx - 3][0])
                    n = len(HL)
                    pt_evac(n - 2)
                    pt_evac(n - 1)
                    for idx in (n - 3, n - 2, n - 1):
                        if HL[idx][1] == 15:
                            resid(HL[idx][0])

                @b.scalar
                def _(s_, bq=bq, c0=c0):
                    s = SCH

                    def exp_h(idx):
                        j, h = HL[idx]
                        par = idx % 2
                        nch = nch_of(j)
                        if idx >= 2:
                            s.wait_ge(sm["sPn"].h, c0["h"] + idx - 1)
                            s.wait_ge(sm["sPTb"].h, batA[idx - 1])
                        ch = chA[idx]
                        for kk in range(nch):
                            bank = ch % 2
                            ch += 1
                            s.wait_ge(sm["sSmm"].h, ch)
                            s.op("activation",
                                 Psb[:, par * SEQ + kk * 512:par * SEQ + kk * 512 + 512],
                                 psS2[bank][:], AF.Exp,
                                 accum_out=zhb[:, par * 4 + kk:par * 4 + kk + 1],
                                 inc=sm["sSe"])

                    def post(idx):
                        j, h = HL[idx]
                        r = rank[j]
                        if h % 2 == 1:
                            hp = h // 2
                            s.wait_ge(sm["sAVa"].h, c0["av"] + idx + 1)
                            if r >= 2 and hp == 0:
                                s.wait_ge(sm["sWOc"].h, c0["woc"] + 2 * (r - 1))
                            ofs = (r % 2) * 1024 + hp * 128
                            s.op("activation", avsb[:, ofs:ofs + 128],
                                 psAV2[idx % 2][:], AF.Copy, inc=sm["sAVe"])


                    for idx in range(len(HL)):
                        exp_h(idx)
                        if idx >= 3:
                            post(idx - 3)
                    n = len(HL)
                    post(n - 3)
                    post(n - 2)
                    post(n - 1)

                @b.tensor
                def _(t, bq=bq, c0=c0):
                    t.wait_ge(sm["sMtl"].h, 16 * (bq + 1))

                    def s_mms(idx):
                        j, h = HL[idx]
                        qsl = Qf[64 * (h % 2):64 * (h % 2) + 64,
                                 (h // 2) * SEQ + j * 128:(h // 2) * SEQ + j * 128 + 128]
                        ch = chA[idx]
                        for kk in range(nch_of(j)):
                            bank = ch % 2
                            ch += 1
                            if ch > 2:
                                t.wait_ge(sm["sSe"].h, ch - 2)
                            diag = (kk == j // 4)
                            mm = t.matmul(psS2[bank][:], qsl,
                                          Kf[64 * (h % 2):64 * (h % 2) + 64,
                                             (h // 2) * SEQ + kk * 512:(h // 2) * SEQ + kk * 512 + 512],
                                          start=True, stop=not diag)
                            if diag:
                                mm = t.matmul(psS2[bank][:], idb[:],
                                              mskb[:, (j % 4) * 512:(j % 4) * 512 + 512],
                                              start=False, stop=True)
                            sm["sSmm"].inc(mm)

                    def pt_part(idx):
                        j, h = HL[idx]
                        par = idx % 2
                        t.wait_ge(sm["sPn"].h, c0["h"] + idx + 1)
                        bat = batA[idx]
                        for bi in range(nb8(j)):
                            bat += 1
                            if bi > 0:
                                t.wait_ge(sm["sPTe"].h, bat - 1)
                            elif idx >= 2:
                                t.wait_ge(sm["sPTe"].h, batA[idx - 1])
                            for s8 in range(8):
                                kt = bi * 8 + s8
                                if kt > j:
                                    break
                                tr = t.transpose(psPT2[par][:, s8 * 128:(s8 + 1) * 128],
                                                 Psb[:, par * SEQ + kt * 128:par * SEQ + kt * 128 + 128],
                                                 idb[:])
                            sm["sPTb"].inc(tr)

                    def av_part(idx):
                        j, h = HL[idx]
                        par = idx % 2
                        hp, hs = h // 2, h % 2
                        t.wait_ge(sm["sPTe"].h, batA[idx] + nb8(j))
                        if hs == 0:
                            t.wait_ge(sm["sAVe"].h, c0["hp"] + oddA[idx])
                        for kt in range(j + 1):
                            mm = t.matmul(psAV2[idx % 2][64 * hs:64 * hs + 64, :],
                                          VT[:, kt * D + hp * 128 + hs * 64:kt * D + hp * 128 + hs * 64 + 64],
                                          PTsb[:, par * SEQ + kt * 128:par * SEQ + kt * 128 + 128],
                                          start=(kt == 0), stop=(kt == j),
                                          tile_position=(0, 64 * hs))
                        sm["sAVa"].inc(mm)

                    def wo_yt(j):
                        r = rank[j]
                        t.wait_ge(sm["sAVe"].h, c0["hp"] + av_done[j])
                        if r >= 1:
                            t.wait_ge(sm["sResB"].h, c0["res"] + r)
                        for m in range(2):
                            for hp in range(8):
                                mm = t.matmul(psWO[:, m * 512:(m + 1) * 512],
                                              avsb[:, (r % 2) * 1024 + hp * 128:(r % 2) * 1024 + hp * 128 + 128],
                                              wo_sb[:, hp * D + m * 512:hp * D + m * 512 + 512],
                                              start=(hp == 0), stop=(hp == 7))
                            sm["sWOc"].inc(mm)

                    for idx in range(len(HL)):
                        if idx >= 2:
                            pt_part(idx - 2)
                        if idx >= 3:
                            av_part(idx - 3)
                        s_mms(idx)
                        if idx >= 3 and HL[idx - 3][1] == 15:
                            wo_yt(HL[idx - 3][0])
                    n = len(HL)
                    pt_part(n - 2)
                    av_part(n - 3)
                    pt_part(n - 1)
                    av_part(n - 2)
                    wo_yt(HL[n - 2][0])
                    av_part(n - 1)
                    wo_yt(HL[n - 1][0])

    with nc.Block() as bf:
        @bf.sync
        def _(sp):
            sp.wait_ge(sstB[0].h, 16 * 16)
            sp.wait_ge(sstB[1].h, 16 * 16)

    _st.close()
    _emit_sem_reset(nc)


def build_l1attn():
    nc = bass.Bass("TRN2", target_bir_lowering=False)
    x_in = nc.dram_tensor("x_in", [TPC, D], F32, kind="ExternalInput").ap()
    x_out = nc.dram_tensor("x_out", [TPC, D], F32, kind="ExternalOutput").ap()
    wd = _declare_io(nc, L1A_KEYS, _SHAPES)
    emit_l1attn(nc, "t_", x_in, x_out, wd)
    return nc


FUSE_PARTS = [("ffn0", "f1_"), ("l0attn", "a1_"), ("ffn0", "f2_"),
              ("l1ffn", "m1_"), ("l1attn", "t1_"), ("l1ffn", "m2_")]
_EMITTERS = {"ffn0": None, "l0attn": None, "l1ffn": None, "l1attn": None}


def build_fused(parts=None):
    if parts is None:
        parts = FUSE_PARTS
    emitters = {"ffn0": emit_ffn0, "l0attn": emit_l0attn,
                "l1ffn": emit_l1ffn, "l1attn": emit_l1attn}
    nc = bass.Bass("TRN2", target_bir_lowering=False)
    x_in = nc.dram_tensor("x_in", [TPC, D], F32, kind="ExternalInput").ap()
    x_out = nc.dram_tensor("x_out", [TPC, D], F32, kind="ExternalOutput").ap()
    wd = _declare_io(nc, sorted(set(FFN0_KEYS + L0_KEYS + L1F_KEYS + L1A_KEYS)),
                     _SHAPES)
    wscr = {k: nc.dram_tensor(k + "_b16", _SHAPES[k][0], BF16).ap()
            for k in ("w1T", "w2T", "h1T", "h2T", "w3T")}
    cur = x_in
    seen_ffn = False
    for i, (name, pfx) in enumerate(parts):
        dst = (x_out if i == len(parts) - 1
               else nc.dram_tensor(f"xs{i}", [TPC, D], F32).ap())
        if name == "ffn0":
            emitters[name](nc, pfx, cur, dst, wd, wscr=wscr, first=not seen_ffn)
            seen_ffn = True
        else:
            emitters[name](nc, pfx, cur, dst, wd)
        cur = dst
    return nc



def _stage_common(inp):
    st = {}
    f32 = np.float32

    def tr(a):
        return np.ascontiguousarray(np.asarray(a, f32).T)

    st["w1T"] = tr(inp["L0_ffn_w1_w"])
    st["w2T"] = tr(inp["L0_ffn_w2_w"])
    st["h1T"] = tr(inp["L0_ffn_h1_w"])
    st["h2T"] = tr(inp["L0_ffn_h2_w"])
    st["w3T"] = tr(inp["L0_ffn_w3_w"])
    for nm, key, nchunk in (("s1c", "L0_ffn_w1_s", 32), ("s2c", "L0_ffn_w2_s", 32),
                            ("sh1c", "L0_ffn_h1_s", 32), ("sh2c", "L0_ffn_h2_s", 32),
                            ("s3c", "L0_ffn_w3_s", 8)):
        st[nm] = np.ascontiguousarray(
            np.asarray(inp[key], f32).reshape(nchunk, 128).T)
    st["identb"] = np.eye(128, dtype=BF)
    st["identf"] = np.eye(128, dtype=f32)
    st["WthT"] = tr(inp["L0_attn_W_theta"])
    st["WphT"] = tr(inp["L0_attn_W_phi"])
    st["WgT"] = tr(inp["L0_attn_W_g"])
    A = np.asarray(inp["L1_ffn_A"], f32)
    Bw = np.asarray(inp["L1_ffn_Bw"], f32)
    st["WrT"] = tr(inp["L1_ffn_Wr"])
    st["ATall"] = np.ascontiguousarray(
        np.concatenate([np.concatenate([A[2 * p].T, A[2 * p + 1].T], axis=1)
                        for p in range(4)], axis=1))
    st["Bpk"] = np.ascontiguousarray(
        np.concatenate([np.concatenate([Bw[2 * p].T, Bw[2 * p + 1].T], axis=0)
                        for p in range(4)], axis=0))
    E = np.zeros((8, 512), np.float32)
    for p in range(4):
        for r in range(128):
            E[2 * p + r // 64, p * 128 + r] = 1.0
    st["Eall"] = E
    st["WQT"] = tr(inp["L1_attn_W_Q"])
    st["WKT"] = tr(inp["L1_attn_W_K"])
    st["WVT"] = tr(inp["L1_attn_W_V"])
    st["WOT"] = tr(inp["L1_attn_W_O"])
    kk = np.arange(32, dtype=f32)
    invf = np.power(np.float32(10000.0), (np.float32(-2.0) * kk / np.float32(64.0)),
                    dtype=f32)
    pos = np.arange(SEQ, dtype=f32)
    ang = pos[None, :] * invf[:, None]
    cosv = np.cos(ang).astype(f32)
    sinv = np.sin(ang).astype(f32)
    C = np.empty((128, SEQ), f32)
    Sg = np.empty((128, SEQ), f32)
    for r in range(128):
        d = r % 64
        k = d // 2
        C[r] = cosv[k]
        Sg[r] = -sinv[k] if d % 2 == 0 else sinv[k]
    st["Ctab"] = C
    st["Stab"] = Sg
    M = np.zeros((128, 2048), f32)
    r_ = np.arange(128)[:, None]
    for jj in range(4):
        u = np.arange(512)[None, :]
        M[:, jj * 512:(jj + 1) * 512] = np.where(u <= 128 * jj + r_, 0.0, -1e30)
    st["Mtab"] = M
    st["Mtabb"] = M.astype(BF)
    P = np.zeros((128, 128), f32)
    for i in range(128):
        P[i, i ^ 1] = 1.0
    st["Pswap"] = P.astype(BF)
    return st


def _run(nc, in_maps, out_key):
    res = run_bass_kernel_spmd(nc, in_maps, core_ids=list(range(NCORES)))
    return [np.asarray(res.results[c][out_key]) for c in range(NCORES)]


_CACHE = {}


def _get(name, builder):
    if name not in _CACHE:
        _CACHE[name] = builder()
    return _CACHE[name]


_KEYS = {
    "ffn0": ["w1T", "w2T", "h1T", "h2T", "w3T", "s1c", "s2c", "sh1c", "sh2c",
             "s3c", "identb", "identf"],
    "l0attn": ["WthT", "WphT", "WgT", "identb"],
    "l1ffn": ["WrT", "ATall", "Bpk", "Eall", "identb", "identf"],
    "l1attn": ["WQT", "WKT", "WVT", "WOT", "Ctab", "Stab", "Mtabb",
               "identb", "identf", "Pswap"],
}
_BUILDERS = {"ffn0": build_ffn0, "l0attn": build_l0attn,
             "l1ffn": build_l1ffn, "l1attn": build_l1attn}


def kernel(**inp):
    st = _stage_common(inp)
    x = np.asarray(inp["L0_attn_x"], np.float32).reshape(NCORES, TPC, D)
    shards = [np.ascontiguousarray(x[c]) for c in range(NCORES)]

    nc = _get("fused", build_fused)
    allk = sorted(set(_KEYS["ffn0"] + _KEYS["l0attn"] + _KEYS["l1ffn"] +
                      _KEYS["l1attn"]))
    maps = [dict({k: st[k] for k in allk}, x_in=shards[c])
            for c in range(NCORES)]
    shards = _run(nc, maps, "x_out")
    out = np.stack(shards).reshape(16, SEQ, D).astype(np.float32)
    return out


# version 14 (final)
# speedup vs baseline: 1.2650x; 1.0278x over previous
import contextlib
import numpy as np
import ml_dtypes
import concourse.bass as bass
import concourse.mybir as mybir
from concourse.bass_utils import run_bass_kernel_spmd

F32 = mybir.dt.float32
BF16 = mybir.dt.bfloat16
AF = mybir.ActivationFunctionType
ALU = mybir.AluOpType
AX = mybir.AxisListType.X
BF = ml_dtypes.bfloat16

NCORES = 8
SEQ = 2048
D = 1024
TPC = 2 * SEQ
TT = 512
NTILES = TPC // TT
DH = 4096
EPS = 1e-5
DEBUG_TAPS = False


def _reg_sem(nc, name, persist=False):
    attr = "_sem_persist" if persist else "_sem_registry"
    reg = getattr(nc, attr, None)
    if reg is None:
        reg = {}
        setattr(nc, attr, reg)
    if name not in reg:
        reg[name] = nc.alloc_semaphore(name)
    return reg[name]


def _emit_sem_reset(nc):
    with nc.Block() as rb:
        @rb.sync
        def _(sp):
            for h in nc._sem_registry.values():
                sp.sem_clear(h)


class S:
    def __init__(self, nc, name, persist=False):
        self.h = _reg_sem(nc, name, persist)
        self.n = 0

    def inc(self, instr, k=1):
        instr.then_inc(self.h, k)
        self.n += k
        return instr


def _sems(nc, names):
    return {nm: S(nc, nm) for nm in names}


class Chain:

    def __init__(self, eng, util_sem):
        self.e = eng
        self._u = util_sem
        self._last = None

    def wait_ge(self, h, v):
        self.e.wait_ge(h, v)

    def op(self, name, *a, inc=None, k=1, **kw):
        if self._last is not None:
            self.e.wait_ge(*self._last)
        instr = getattr(self.e, name)(*a, **kw)
        if inc is None:
            self._u.inc(instr, 1)
            self._last = (self._u.h, self._u.n)
        else:
            inc.inc(instr, k)
            self._last = (inc.h, inc.n)
        return instr



FFN0_KEYS = ["w1T", "w2T", "h1T", "h2T", "w3T", "s1c", "s2c", "sh1c", "sh2c",
             "s3c", "identb", "identf"]


def emit_ffn0(nc, pfx, x_in, x_out, wd, wscr=None, first=True):
    _st = contextlib.ExitStack()
    sb = lambda n, s, d: _st.enter_context(nc.sbuf_tensor(pfx + n, s, d))
    pst = lambda n, s, d: _st.enter_context(nc.psum_tensor(pfx + n, s, d))
    w1T = wd["w1T"]; w2T = wd["w2T"]; h1T = wd["h1T"]; h2T = wd["h2T"]
    w3T = wd["w3T"]; s1d = wd["s1c"]; s2d = wd["s2c"]; sh1d = wd["sh1c"]
    sh2d = wd["sh2c"]; s3d = wd["s3c"]; idb_d = wd["identb"]; idf_d = wd["identf"]

    x_in_t = x_in.rearrange("(t g p) f -> t p g f", g=4, p=128)
    x_out_t = x_out.rearrange("(t g p) f -> t p g f", g=4, p=128)
    r_w1 = w1T.rearrange("(a p) m -> p a m", p=128)
    r_w2 = w2T.rearrange("(a p) m -> p a m", p=128)
    r_h1 = h1T.rearrange("(a p) m -> p a m", p=128)
    r_h2 = h2T.rearrange("(a p) m -> p a m", p=128)
    r_w3 = w3T.rearrange("(a p) m -> p a m", p=128)

    xt = [sb(f"xt{q}", [128, 4 * D], F32) for q in range(2)]
    xo = sb("xo", [128, 4 * D], F32)
    zt = sb("zt", [128, 4 * D], BF16)
    zf = sb("zf", [128, 8 * TT], BF16)
    sq = sb("sq", [128, D], F32)
    ha = sb("ha", [128, 32 * TT], BF16)
    hb = sb("hb", [128, 32 * TT], BF16)
    yf = sb("yf", [128, 8 * TT], F32)
    wbuf = [sb(f"wbuf{q}", [128, 8 * 512], BF16) for q in range(4)]
    t1 = [sb(f"t1{q}", [128, TT], BF16) for q in range(2)]
    t2 = [sb(f"t2{q}", [128, TT], BF16) for q in range(2)]
    s1c = sb("s1cs", [128, 32], F32)
    s2c = sb("s2cs", [128, 32], F32)
    sh1c = sb("sh1cs", [128, 32], F32)
    sh2c = sb("sh2cs", [128, 32], F32)
    s3c = sb("s3cs", [128, 8], F32)
    idb = sb("idb", [128, 128], BF16)
    idf = sb("idf", [128, 128], F32)
    epsb = sb("epsb", [128, 1], F32)
    sums = sb("sums", [128, 4], F32)
    sumsq = sb("sumsq", [128, 4], F32)
    mu = sb("mu", [128, 4], F32)
    ex2 = sb("ex2", [128, 4], F32)
    musq = sb("musq", [128, 4], F32)
    var = sb("var", [128, 4], F32)
    stdv = sb("stdv", [128, 4], F32)
    istd = sb("istd", [128, 4], F32)

    psA = [pst(f"psA{q}", [128, TT], F32) for q in range(4)]
    psW = pst("psW", [128, D], F32)
    psT = [pst(f"psT{q}", [128, 1024], BF16) for q in range(2)]

    sm = _sems(nc, ["sst", "sKG", "sln", "stp", "szf", "sA1", "sA2",
                    "sE1", "sE2", "sPr", "sM1", "sH1", "sM2", "sH2", "sM3",
                    "sYf", "sYT", "sRes", "sVr", "sSd", "sC"])
    sw4 = [S(nc, pfx + f"sw{q}", persist=True) for q in range(4)]
    swb4 = [S(nc, f"swb4_{q}") for q in range(4)]
    sx2 = [S(nc, f"sx{q}") for q in range(2)]
    VC = Chain(nc.vector, S(nc, "chV"))
    SC = Chain(nc.scalar, S(nc, "chS"))

    if wscr is not None:
        rb_w1 = wscr["w1T"].rearrange("(a p) m -> p a m", p=128)
        rb_w2 = wscr["w2T"].rearrange("(a p) m -> p a m", p=128)
        rb_h1 = wscr["h1T"].rearrange("(a p) m -> p a m", p=128)
        rb_h2 = wscr["h2T"].rearrange("(a p) m -> p a m", p=128)
        rb_w3 = wscr["w3T"].rearrange("(a p) m -> p a m", p=128)

    stripes = []
    for mg in range(8):
        stripes.append(("w1", mg, 0))
        stripes.append(("w2", mg, 0))
    for mg in range(8):
        for kg in range(4):
            stripes.append(("h1", mg, kg))
    for mg in range(8):
        for kg in range(4):
            stripes.append(("h2", mg, kg))
    for mg in range(2):
        for kg in range(4):
            stripes.append(("w3", mg, kg))
    NS = len(stripes)
    stripe_free = {}

    def stripe_src(kind, mg, kg, f32src):
        if f32src:
            t = {"w1": r_w1, "w2": r_w2, "h1": r_h1, "h2": r_h2, "w3": r_w3}[kind]
        else:
            t = {"w1": rb_w1, "w2": rb_w2, "h1": rb_h1, "h2": rb_h2,
                 "w3": rb_w3}[kind]
        if kind in ("w1", "w2"):
            return t[:, :, mg * 512:(mg + 1) * 512]
        return t[:, kg * 8:(kg + 1) * 8, mg * 512:(mg + 1) * 512]

    with nc.Block() as b:
        @b.sync
        def _(sp):
            sp.dma_start(s1c[:], s1d).then_inc(sm["sC"].h, 16)
            sp.dma_start(s2c[:], s2d).then_inc(sm["sC"].h, 16)
            sp.dma_start(sh1c[:], sh1d).then_inc(sm["sC"].h, 16)
            sp.dma_start(sh2c[:], sh2d).then_inc(sm["sC"].h, 16)
            sp.dma_start(s3c[:], s3d).then_inc(sm["sC"].h, 16)
            sp.dma_start(idb[:], idb_d).then_inc(sm["sC"].h, 16)
            sp.dma_start(idf[:], idf_d).then_inc(sm["sC"].h, 16)
            sm["sC"].n = 7 * 16

        for i in range(NTILES):
            @b.sync
            def _(sp, i=i):
                if i >= 2:
                    sp.wait_ge(sm["sRes"].h, 4 * (i - 1))
                sx2[i % 2].inc(sp.dma_start(xt[i % 2][:].rearrange("p (g f) -> p g f", f=D),
                                            x_in_t[i]), 16)
                if i == 0 and wscr is not None and first:
                    for idx, (kind, mg, kg) in enumerate(stripes):
                        sp.wait_ge(sw4[idx % 4].h, 16 * (idx // 4 + 1))
                        dstw = stripe_src(kind, mg, kg, False)
                        swb4[idx % 4].inc(sp.dma_start(
                            dstw,
                            wbuf[idx % 4][:].rearrange("p (a m) -> p a m", m=512)), 16)
                sp.wait_ge(sm["sRes"].h, 4 * (i + 1))
                sm["sst"].inc(sp.dma_start(x_out_t[i],
                                           xo[:].rearrange("p (g f) -> p g f", f=D)), 16)

            @b.vector
            def _(v_, i=i):
                v = VC
                if i == 0:
                    v.wait_ge(sm["sC"].h, 7 * 16)
                    v.op("memset", epsb[:], EPS)
                v.wait_ge(sx2[i % 2].h, 16 * (i // 2 + 1))
                for g in range(4):
                    xg = xt[i % 2][:, g * D:(g + 1) * D]
                    v.op("tensor_tensor", sq[:], xg, xg, ALU.mult)
                    v.op("reduce_sum", sumsq[:, g:g + 1], sq[:], AX)
                    v.op("reduce_sum", sums[:, g:g + 1], xg, AX)
                v.op("tensor_scalar_mul", mu[:], sums[:], 1.0 / D)
                v.op("tensor_scalar_mul", ex2[:], sumsq[:], 1.0 / D)
                v.op("tensor_tensor", musq[:], mu[:], mu[:], ALU.mult)
                v.op("tensor_tensor", var[:], ex2[:], musq[:], ALU.subtract,
                     inc=sm["sVr"])
                v.wait_ge(sm["sSd"].h, i + 1)
                v.op("reciprocal", istd[:], stdv[:])
                v.wait_ge(sm["stp"].h, 8 * i)
                for g in range(4):
                    v.op("tensor_scalar",
                         zt[:, g * D:(g + 1) * D], xt[i % 2][:, g * D:(g + 1) * D],
                         mu[:, g:g + 1], istd[:, g:g + 1], ALU.subtract, ALU.mult,
                         inc=sm["sln"])
                v.wait_ge(sm["sA2"].h, 32 * i)
                for c in range(8):
                    v.wait_ge(sm["stp"].h, i * 8 + c + 1)
                    v.op("tensor_copy", zf[:, c * TT:(c + 1) * TT],
                         psT[c % 2][:, 0:TT], inc=sm["szf"])
                v.wait_ge(sm["sM3"].h, 8 * i)
                for mci in range(32):
                    G = i * 32 + mci
                    v.wait_ge(sm["sA2"].h, G + 1)
                    v.op("tensor_scalar_mul", t2[mci % 2][:],
                         psA[2 + mci % 2][:], s2c[:, mci:mci + 1], inc=sm["sE2"])
                    v.wait_ge(sm["sE1"].h, G + 1)
                    v.op("tensor_tensor", ha[:, mci * TT:(mci + 1) * TT],
                         t1[mci % 2][:], t2[mci % 2][:], ALU.mult, inc=sm["sPr"])
                if i >= 1:
                    v.wait_ge(sm["sst"].h, 16 * i)
                for g in range(4):
                    v.wait_ge(sm["sYT"].h, i * 4 + g + 1)
                    v.op("scalar_tensor_tensor",
                         xo[:, g * D:(g + 1) * D], psW[:, 0:D], 0.5,
                         xt[i % 2][:, g * D:(g + 1) * D], ALU.mult, ALU.add,
                         inc=sm["sRes"])

            @b.scalar
            def _(s_, i=i):
                s = SC
                if i == 0:
                    s.wait_ge(sm["sC"].h, 7 * 16)
                s.wait_ge(sm["sVr"].h, i + 1)
                s.op("activation", stdv[:], var[:], AF.Sqrt, bias=epsb[:, 0:1],
                     inc=sm["sSd"])
                s.wait_ge(sm["sYT"].h, 4 * i)
                for mci in range(32):
                    G = i * 32 + mci
                    s.wait_ge(sm["sA1"].h, G + 1)
                    if G >= 2:
                        s.wait_ge(sm["sPr"].h, G - 1)
                    s.op("activation", t1[mci % 2][:], psA[mci % 2][:],
                         AF.Silu, scale=s1c[:, mci:mci + 1], inc=sm["sE1"])
                s.wait_ge(sm["sM2"].h, 32 * i)
                for mci in range(32):
                    s.wait_ge(sm["sM1"].h, i * 32 + mci + 1)
                    s.op("activation", hb[:, mci * TT:(mci + 1) * TT],
                         psA[mci % 4][:], AF.Gelu, scale=sh1c[:, mci:mci + 1],
                         inc=sm["sH1"])
                for mci in range(32):
                    s.wait_ge(sm["sM2"].h, i * 32 + mci + 1)
                    s.op("activation", ha[:, mci * TT:(mci + 1) * TT],
                         psA[mci % 4][:], AF.Gelu, scale=sh2c[:, mci:mci + 1],
                         inc=sm["sH2"])
                for mci in range(8):
                    s.wait_ge(sm["sM3"].h, i * 8 + mci + 1)
                    s.op("activation", yf[:, mci * TT:(mci + 1) * TT],
                         psA[mci % 4][:], AF.Copy, scale=s3c[:, mci:mci + 1],
                         inc=sm["sYf"])

            @b.tensor
            def _(t, i=i):
                J0 = i * NS
                if i == 0:
                    t.wait_ge(sm["sC"].h, 7 * 16)
                t.wait_ge(sm["sln"].h, 4 * (i + 1))
                for c in range(8):
                    C = i * 8 + c
                    if C >= 2:
                        t.wait_ge(sm["szf"].h, C - 1)
                    for g in range(4):
                        tr = t.transpose(psT[c % 2][:, g * 128:(g + 1) * 128],
                                         zt[:, g * D + c * 128:g * D + c * 128 + 128],
                                         idb[:])
                    sm["stp"].inc(tr)
                t.wait_ge(sm["szf"].h, 8 * (i + 1))
                t.wait_ge(sm["sYf"].h, 8 * i)
                for mg in range(8):
                    Jw1 = J0 + 2 * mg
                    Jw2 = Jw1 + 1
                    t.wait_ge(sw4[Jw1 % 4].h, 16 * (Jw1 // 4 + 1))
                    t.wait_ge(sw4[Jw2 % 4].h, 16 * (Jw2 // 4 + 1))
                    for mc in range(4):
                        mci = mg * 4 + mc
                        G = i * 32 + mci
                        if G >= 2:
                            t.wait_ge(sm["sE1"].h, G - 1)
                        for k in range(8):
                            mm = t.matmul(psA[mci % 2][:],
                                          wbuf[Jw1 % 4][:, k * 512 + mc * 128:k * 512 + mc * 128 + 128],
                                          zf[:, k * TT:(k + 1) * TT],
                                          start=(k == 0), stop=(k == 7))
                        sm["sA1"].inc(mm)
                        if G >= 2:
                            t.wait_ge(sm["sE2"].h, G - 1)
                        for k in range(8):
                            mm = t.matmul(psA[2 + mci % 2][:],
                                          wbuf[Jw2 % 4][:, k * 512 + mc * 128:k * 512 + mc * 128 + 128],
                                          zf[:, k * TT:(k + 1) * TT],
                                          start=(k == 0), stop=(k == 7))
                        sm["sA2"].inc(mm)
                    stripe_free[Jw1] = (sm["sE1"], i * 32 + mg * 4 + 4)
                    stripe_free[Jw2] = (sm["sE2"], i * 32 + mg * 4 + 4)
                for kind, s_in, sMx, sHx, base in (("h1", ha, sm["sM1"], sm["sH1"], J0 + 16),
                                                   ("h2", hb, sm["sM2"], sm["sH2"], J0 + 48)):
                    if kind == "h1":
                        t.wait_ge(sm["sPr"].h, 32 * (i + 1))
                    else:
                        t.wait_ge(sm["sH1"].h, 32 * (i + 1))
                    for mg in range(8):
                        for kg in range(4):
                            J = base + mg * 4 + kg
                            t.wait_ge(sw4[J % 4].h, 16 * (J // 4 + 1))
                            for mc in range(4):
                                mci = mg * 4 + mc
                                if kg == 0 and mg >= 1:
                                    t.wait_ge(sHx.h, i * 32 + (mg - 1) * 4 + mc + 1)
                                for k in range(8):
                                    mm = t.matmul(
                                        psA[mc][:],
                                        wbuf[J % 4][:, k * 512 + mc * 128:k * 512 + mc * 128 + 128],
                                        s_in[:, (kg * 8 + k) * TT:(kg * 8 + k + 1) * TT],
                                        start=(kg == 0 and k == 0),
                                        stop=(kg == 3 and k == 7))
                                if kg == 3:
                                    sMx.inc(mm)
                            if kg < 3:
                                sm["sKG"].inc(mm)
                                stripe_free[J] = (sm["sKG"], sm["sKG"].n)
                            else:
                                stripe_free[J] = (sMx, i * 32 + mg * 4 + 4)
                t.wait_ge(sm["sH2"].h, 32 * (i + 1))
                for mg in range(2):
                    for kg in range(4):
                        J = J0 + 80 + mg * 4 + kg
                        t.wait_ge(sw4[J % 4].h, 16 * (J // 4 + 1))
                        for mc in range(4):
                            mci = mg * 4 + mc
                            if kg == 0 and mg >= 1:
                                t.wait_ge(sm["sYf"].h, i * 8 + (mg - 1) * 4 + mc + 1)
                            for k in range(8):
                                mm = t.matmul(
                                    psA[mc][:],
                                    wbuf[J % 4][:, k * 512 + mc * 128:k * 512 + mc * 128 + 128],
                                    ha[:, (kg * 8 + k) * TT:(kg * 8 + k + 1) * TT],
                                    start=(kg == 0 and k == 0),
                                    stop=(kg == 3 and k == 7))
                            if kg == 3:
                                sm["sM3"].inc(mm)
                        if kg < 3:
                            sm["sKG"].inc(mm)
                            stripe_free[J] = (sm["sKG"], sm["sKG"].n)
                        else:
                            stripe_free[J] = (sm["sM3"], i * 8 + mg * 4 + 4)
                t.wait_ge(sm["sYf"].h, 8 * (i + 1))
                for g in range(4):
                    t.wait_ge(sm["sRes"].h, 4 * i + g)
                    for c in range(8):
                        tr = t.transpose(psW[:, c * 128:(c + 1) * 128],
                                         yf[:, c * TT + g * 128:c * TT + g * 128 + 128],
                                         idf[:])
                    sm["sYT"].inc(tr)

            @b.gpsimd
            def _(g, i=i):
                wb_on = wscr is not None and first
                for idx, (kind, mg, kg) in enumerate(stripes):
                    J = i * NS + idx
                    if J >= 4:
                        fs, fv = stripe_free[J - 4]
                        g.wait_ge(fs.h, fv)
                        if wb_on and J - 4 < NS:
                            g.wait_ge(swb4[(J - 4) % 4].h,
                                      16 * ((J - 4) // 4 + 1))
                    f32src = (wscr is None) or (first and i == 0)
                    src = stripe_src(kind, mg, kg, f32src)
                    dst = wbuf[J % 4][:].rearrange("p (a m) -> p a m", m=512)
                    sw4[J % 4].inc(g.dma_start(dst, src), 16)

        if DEBUG_TAPS:
            @b.gpsimd
            def _(g):
                g.wait_ge(sm["sst"].h, 16 * NTILES)
                for nm, buf in (("d_zt", zt), ("d_zf", zf), ("d_ha", ha),
                                ("d_hb", hb), ("d_yf", yf), ("d_xt", xt[(NTILES - 1) % 2])):
                    dt_ = nc.dram_tensor(nm, list(buf[:].shape), buf.dtype,
                                         kind="ExternalOutput").ap()
                    g.dma_start(dt_, buf[:]).then_inc(sm["sC"].h, 16)
                g.wait_ge(sm["sC"].h, sm["sC"].n + 6 * 16)

        @b.sync
        def _(sp):
            sp.wait_ge(sm["sst"].h, 16 * NTILES)

    _st.close()
    _emit_sem_reset(nc)


def _declare_io(nc, keys, shapes):
    wd = {}
    for k in keys:
        shp, dt = shapes[k]
        wd[k] = nc.dram_tensor(k, shp, dt, kind="ExternalInput").ap()
    return wd


_SHAPES = {
    "w1T": ([D, DH], F32), "w2T": ([D, DH], F32), "h1T": ([DH, DH], F32),
    "h2T": ([DH, DH], F32), "w3T": ([DH, D], F32), "s1c": ([128, 32], F32),
    "s2c": ([128, 32], F32), "sh1c": ([128, 32], F32), "sh2c": ([128, 32], F32),
    "s3c": ([128, 8], F32), "identb": ([128, 128], BF16), "identf": ([128, 128], F32),
    "WthT": ([D, 256], F32), "WphT": ([D, 256], F32), "WgT": ([D, D], F32),
    "WrT": ([D, 8], F32), "ATall": ([D, 512], F32), "Bpk": ([512, D], F32),
    "Eall": ([8, 512], F32),
    "WQT": ([D, D], F32), "WKT": ([D, D], F32), "WVT": ([D, D], F32),
    "WOT": ([D, D], F32), "Ctab": ([128, SEQ], F32), "Stab": ([128, SEQ], F32),
    "Mtab": ([128, 2048], F32), "Mtabb": ([128, 2048], BF16),
    "Pswap": ([128, 128], BF16),
}


def build_ffn0():
    nc = bass.Bass("TRN2", target_bir_lowering=False)
    x_in = nc.dram_tensor("x_in", [TPC, D], F32, kind="ExternalInput").ap()
    x_out = nc.dram_tensor("x_out", [TPC, D], F32, kind="ExternalOutput").ap()
    wd = _declare_io(nc, FFN0_KEYS, _SHAPES)
    emit_ffn0(nc, "f_", x_in, x_out, wd)
    return nc




L0_KEYS = ["WthT", "WphT", "WgT", "identb"]


def emit_l0attn(nc, pfx, x_in, x_out, wd):
    _st = contextlib.ExitStack()
    sb = lambda n, s, d: _st.enter_context(nc.sbuf_tensor(pfx + n, s, d))
    pst = lambda n, s, d: _st.enter_context(nc.psum_tensor(pfx + n, s, d))
    wthd = wd["WthT"]; wphd = wd["WphT"]; wgd = wd["WgT"]; idb_d = wd["identb"]

    x_in_t = x_in.rearrange("(t g p) f -> t p g f", g=4, p=128)
    x_in_q = x_in.rearrange("(q p) f -> q p f", p=128)
    x_out_q = x_out.rearrange("(q p) f -> q p f", p=128)
    r_wth = wthd.rearrange("(a p) m -> p a m", p=128)
    r_wph = wphd.rearrange("(a p) m -> p a m", p=128)
    r_wg = wgd.rearrange("(a p) m -> p a m", p=128)

    xt2 = [sb(f"xt2{q}", [128, 4 * D], F32) for q in range(2)]
    xq = [sb(f"xq{q}", [128, D], F32) for q in range(2)]
    xoq = [sb(f"xoq{q}", [128, D], F32) for q in range(2)]
    zt = sb("zt", [128, 4 * D], BF16)
    zf = sb("zf", [128, 8 * SEQ], BF16)
    sq = sb("sq", [128, D], F32)
    th = sb("th", [128, 2 * SEQ], BF16)
    ph = sb("ph", [128, 2 * SEQ], BF16)
    gf = sb("gf", [128, 8 * SEQ], BF16)
    phT = sb("phT", [128, 16 * 256], BF16)
    gT = sb("gT", [128, 16 * D], BF16)
    m_sb = sb("m_sb", [128, 2 * D], BF16)
    wpb = sb("wpb", [128, 8 * D], BF16)
    idb = sb("idb", [128, 128], BF16)
    msc = sb("msc", [128, 1], F32)
    sums = sb("sums", [128, 4], F32)
    sumsq = sb("sumsq", [128, 4], F32)
    mu = sb("mu", [128, 4], F32)
    ex2 = sb("ex2", [128, 4], F32)
    musq = sb("musq", [128, 4], F32)
    var = sb("var", [128, 4], F32)
    stdv = sb("stdv", [128, 4], F32)
    istd = sb("istd", [128, 4], F32)
    epsb = sb("epsb", [128, 1], F32)

    psA = [pst(f"psA{q}", [128, 512], F32) for q in range(4)]
    psW = pst("psW", [128, D], F32)
    psT = [pst(f"psT{q}", [128, 1024], BF16) for q in range(2)]

    sm = _sems(nc, ["sC", "sst", "sln", "stp", "szf", "sMM", "sEv", "sTB",
                    "sTE", "sMs", "sVr", "sSd", "sRes"])
    sxt = [S(nc, f"sxt{q}") for q in range(2)]
    sst2 = [S(nc, f"sst2{q}") for q in range(2)]
    sxq = [S(nc, f"sxq{q}") for q in range(2)]
    swb = S(nc, pfx + "swb", persist=True)
    VC = Chain(nc.vector, S(nc, "chV"))
    SC = Chain(nc.scalar, S(nc, "chS"))

    def wview(mwid):
        return wpb[:, 0:8 * mwid].rearrange("p (a m) -> p a m", m=mwid)

    with nc.Block() as b0:
        @b0.sync
        def _(sp):
            sm["sC"].inc(sp.dma_start(idb[:], idb_d), 16)

        @b0.vector
        def _(v_):
            VC.op("memset", epsb[:], EPS)
            VC.op("memset", msc[:], 1.0 / 32.0)

    for bq in range(2):
        with nc.Block() as b:
            @b.sync
            def _(sp, bq=bq):
                for tau in range(4):
                    gt = bq * 4 + tau
                    if gt >= 2:
                        sp.wait_ge(sm["sln"].h, 4 * (gt - 1))
                    sxt[tau % 2].inc(sp.dma_start(
                        xt2[tau % 2][:].rearrange("p (g f) -> p g f", f=D),
                        x_in_t[gt]), 16)
                for qt in range(16):
                    gq = bq * 16 + qt
                    if gq >= 2:
                        sp.wait_ge(sm["sRes"].h, 2 * (gq - 1))
                    sxq[qt % 2].inc(sp.dma_start(xq[qt % 2][:], x_in_q[gq]), 16)
                    sp.wait_ge(sm["sRes"].h, 2 * (gq + 1))
                    sst2[qt % 2].inc(sp.dma_start(x_out_q[gq], xoq[qt % 2][:]), 16)

            @b.gpsimd
            def _(g, bq=bq):
                if bq == 0:
                    g.wait_ge(sm["sC"].h, 16)
                swb.inc(g.dma_start(wview(256), r_wth), 16)
                g.wait_ge(sm["sMM"].h, bq * 52 + 8)
                swb.inc(g.dma_start(wview(256), r_wph), 16)
                g.wait_ge(sm["sMM"].h, bq * 52 + 16)
                swb.inc(g.dma_start(wview(D), r_wg), 16)

            @b.vector
            def _(v_, bq=bq):
                v = VC
                for tau in range(4):
                    gt = bq * 4 + tau
                    v.wait_ge(sxt[tau % 2].h, 16 * (gt // 2 + 1))
                    for g in range(4):
                        xg = xt2[tau % 2][:, g * D:(g + 1) * D]
                        v.op("tensor_tensor", sq[:], xg, xg, ALU.mult)
                        v.op("reduce_sum", sumsq[:, g:g + 1], sq[:], AX)
                        v.op("reduce_sum", sums[:, g:g + 1], xg, AX)
                    v.op("tensor_scalar_mul", mu[:], sums[:], 1.0 / D)
                    v.op("tensor_scalar_mul", ex2[:], sumsq[:], 1.0 / D)
                    v.op("tensor_tensor", musq[:], mu[:], mu[:], ALU.mult)
                    v.op("tensor_tensor", var[:], ex2[:], musq[:], ALU.subtract,
                         inc=sm["sVr"])
                    v.wait_ge(sm["sSd"].h, gt + 1)
                    v.op("reciprocal", istd[:], stdv[:])
                    v.wait_ge(sm["stp"].h, 8 * gt)
                    for g in range(4):
                        v.op("tensor_scalar", zt[:, g * D:(g + 1) * D],
                             xt2[tau % 2][:, g * D:(g + 1) * D],
                             mu[:, g:g + 1], istd[:, g:g + 1],
                             ALU.subtract, ALU.mult, inc=sm["sln"])
                    for c in range(8):
                        CB = gt * 8 + c
                        v.wait_ge(sm["stp"].h, CB + 1)
                        v.op("tensor_copy",
                             zf[:, c * SEQ + tau * 512:c * SEQ + tau * 512 + 512],
                             psT[c % 2][:, 0:512], inc=sm["szf"])
                for kt in range(16):
                    v.wait_ge(sm["sTB"].h, bq * 32 + 2 * kt + 1)
                    v.op("tensor_copy", phT[:, kt * 256:(kt + 1) * 256],
                         psT[0][:, 0:256], inc=sm["sTE"])
                    v.wait_ge(sm["sTB"].h, bq * 32 + 2 * kt + 2)
                    v.op("tensor_copy", gT[:, kt * D:(kt + 1) * D],
                         psT[1][:, 0:D], inc=sm["sTE"])
                for qt in range(16):
                    gq = bq * 16 + qt
                    if gq >= 2:
                        v.wait_ge(sst2[gq % 2].h, 16 * (gq // 2))
                    v.wait_ge(sxq[qt % 2].h, 16 * (gq // 2 + 1))
                    for oc in range(2):
                        v.wait_ge(sm["sMs"].h, 2 * gq + oc + 1)
                        v.op("tensor_tensor",
                             xoq[qt % 2][:, oc * 512:(oc + 1) * 512],
                             psA[2 * (qt % 2) + oc][:],
                             xq[qt % 2][:, oc * 512:oc * 512 + 512],
                             ALU.add, inc=sm["sRes"])

            @b.scalar
            def _(s_, bq=bq):
                s = SC
                for tau in range(4):
                    s.wait_ge(sm["sVr"].h, bq * 4 + tau + 1)
                    s.op("activation", stdv[:], var[:], AF.Sqrt,
                         bias=epsb[:, 0:1], inc=sm["sSd"])
                for pi, (nmc, obuf, owid) in enumerate(
                        ((2, th, SEQ), (2, ph, SEQ), (8, gf, SEQ))):
                    base = [0, 8, 16][pi] + bq * 52
                    for mc in range(nmc):
                        for nt in range(4):
                            gidx = base + mc * 4 + nt
                            s.wait_ge(sm["sMM"].h, gidx + 1)
                            s.op("activation",
                                 obuf[:, mc * owid + nt * 512:mc * owid + nt * 512 + 512],
                                 psA[gidx % 4][:], AF.Copy, inc=sm["sEv"])
                for idx in range(4):
                    s.wait_ge(sm["sMM"].h, bq * 52 + 48 + idx + 1)
                    lc, oc = idx // 2, idx % 2
                    s.op("activation",
                         m_sb[:, lc * D + oc * 512:lc * D + oc * 512 + 512],
                         psA[idx][:], AF.Copy, scale=msc[:, 0:1], inc=sm["sEv"])

            @b.tensor
            def _(t, bq=bq):
                if bq == 0:
                    t.wait_ge(sm["sC"].h, 16)
                for tau in range(4):
                    gt = bq * 4 + tau
                    t.wait_ge(sm["sln"].h, 4 * (gt + 1))
                    for c in range(8):
                        CB = gt * 8 + c
                        if CB >= 2:
                            t.wait_ge(sm["szf"].h, CB - 1)
                        for g in range(4):
                            tr = t.transpose(psT[c % 2][:, g * 128:(g + 1) * 128],
                                             zt[:, g * D + c * 128:g * D + c * 128 + 128],
                                             idb[:])
                        sm["stp"].inc(tr)
                t.wait_ge(sm["szf"].h, 32 * (bq + 1))
                gidx = bq * 52
                for wi, (nmc, mwid) in enumerate(((2, 256), (2, 256), (8, D))):
                    t.wait_ge(swb.h, 16 * (bq * 3 + wi + 1))
                    for mc in range(nmc):
                        for nt in range(4):
                            if gidx >= 4 + bq * 52:
                                t.wait_ge(sm["sEv"].h, gidx - 3)
                            for k in range(8):
                                mm = t.matmul(psA[gidx % 4][:],
                                              wpb[:, k * mwid + mc * 128:k * mwid + mc * 128 + 128],
                                              zf[:, k * SEQ + nt * 512:k * SEQ + nt * 512 + 512],
                                              start=(k == 0), stop=(k == 7))
                            sm["sMM"].inc(mm)
                            gidx += 1
                t.wait_ge(sm["sEv"].h, bq * 52 + 16)
                for kt in range(16):
                    TB = bq * 32 + 2 * kt
                    if TB >= 2:
                        t.wait_ge(sm["sTE"].h, TB - 1)
                    for lc in range(2):
                        tr = t.transpose(psT[0][:, lc * 128:(lc + 1) * 128],
                                         ph[:, lc * SEQ + kt * 128:lc * SEQ + kt * 128 + 128],
                                         idb[:])
                    sm["sTB"].inc(tr)
                    if TB + 1 >= 2:
                        t.wait_ge(sm["sTE"].h, TB)
                    if kt == 0:
                        t.wait_ge(sm["sEv"].h, bq * 52 + 48)
                    for mc in range(8):
                        tr = t.transpose(psT[1][:, mc * 128:(mc + 1) * 128],
                                         gf[:, mc * SEQ + kt * 128:mc * SEQ + kt * 128 + 128],
                                         idb[:])
                    sm["sTB"].inc(tr)
                t.wait_ge(sm["sTE"].h, 32 * (bq + 1))
                for idx in range(4):
                    lc, oc = idx // 2, idx % 2
                    for kt in range(16):
                        mm = t.matmul(psA[idx][:],
                                      phT[:, kt * 256 + lc * 128:kt * 256 + lc * 128 + 128],
                                      gT[:, kt * D + oc * 512:kt * D + oc * 512 + 512],
                                      start=(kt == 0), stop=(kt == 15))
                    sm["sMM"].inc(mm)
                t.wait_ge(sm["sEv"].h, bq * 52 + 52)
                for qt in range(16):
                    gq = bq * 16 + qt
                    if gq >= 2:
                        t.wait_ge(sm["sRes"].h, 2 * (gq - 1))
                    for oc in range(2):
                        for lc in range(2):
                            mm = t.matmul(psA[2 * (qt % 2) + oc][:],
                                          th[:, lc * SEQ + qt * 128:lc * SEQ + qt * 128 + 128],
                                          m_sb[:, lc * D + oc * 512:lc * D + oc * 512 + 512],
                                          start=(lc == 0), stop=(lc == 1))
                        sm["sMs"].inc(mm)

    with nc.Block() as bf:
        @bf.sync
        def _(sp):
            sp.wait_ge(sst2[0].h, 16 * 16)
            sp.wait_ge(sst2[1].h, 16 * 16)

    _st.close()
    _emit_sem_reset(nc)


def build_l0attn():
    nc = bass.Bass("TRN2", target_bir_lowering=False)
    x_in = nc.dram_tensor("x_in", [TPC, D], F32, kind="ExternalInput").ap()
    x_out = nc.dram_tensor("x_out", [TPC, D], F32, kind="ExternalOutput").ap()
    wd = _declare_io(nc, L0_KEYS, _SHAPES)
    emit_l0attn(nc, "a_", x_in, x_out, wd)
    return nc




L1F_KEYS = ["WrT", "ATall", "Bpk", "Eall", "identb", "identf"]


def emit_l1ffn(nc, pfx, x_in, x_out, wd):
    _st = contextlib.ExitStack()
    sb = lambda n, s, d: _st.enter_context(nc.sbuf_tensor(pfx + n, s, d))
    pst = lambda n, s, d: _st.enter_context(nc.psum_tensor(pfx + n, s, d))
    wrd = wd["WrT"]; atd = wd["ATall"]; bpd = wd["Bpk"]; ed = wd["Eall"]
    idb_d = wd["identb"]; idf_d = wd["identf"]

    x_in_t = x_in.rearrange("(t g p) f -> t p g f", g=4, p=128)
    x_out_t = x_out.rearrange("(t g p) f -> t p g f", g=4, p=128)
    x_cls = x_in.rearrange("(b s) f -> b s f", s=SEQ)[:, 0, :]

    xt = [sb(f"xt{q}", [128, 4 * D], F32) for q in range(2)]
    xo = sb("xo", [128, 4 * D], F32)
    zt = sb("zt", [128, 4 * D], BF16)
    zf = sb("zf", [128, 8 * TT], BF16)
    sq = sb("sq", [128, D], F32)
    xzb = sb("xzb", [128, D], F32)
    mid_sb = sb("mid_sb", [128, 4 * TT], BF16)
    wf = sb("wf", [128, 8 * TT], F32)
    ats = sb("ats", [128, 8 * 512], BF16)
    bps = sb("bps", [128, 4 * D], BF16)
    wrs = sb("wrs", [128, 8 * 8], BF16)
    ebs = sb("ebs", [8, 512], BF16)
    cls_x = sb("cls_x", [2, D], F32)
    clsz = sb("clsz", [2, D], BF16)
    cls_fm = sb("cls_fm", [128, 16], BF16)
    lg = sb("lg", [2, 8], F32)
    gsc = sb("gsc", [2, 16], F32)
    gs2 = sb("gs2", [2, 40], F32)
    gb = sb("gb", [2, 8], BF16)
    gts = sb("gts", [8, 2], BF16)
    g_all = sb("g_all", [128, 8], F32)
    idb = sb("idb", [128, 128], BF16)
    idf = sb("idf", [128, 128], F32)
    epsb = sb("epsb", [128, 1], F32)
    sums = sb("sums", [128, 4], F32)
    sumsq = sb("sumsq", [128, 4], F32)
    mu = sb("mu", [128, 4], F32)
    ex2 = sb("ex2", [128, 4], F32)
    musq = sb("musq", [128, 4], F32)
    var = sb("var", [128, 4], F32)
    stdv = sb("stdv", [128, 4], F32)
    istd = sb("istd", [128, 4], F32)
    half = sb("half", [128, 4], F32)
    aZ = sb("aZ", [128, 4], F32)
    bZmu = sb("bZmu", [128, 4], F32)

    psA = [pst(f"psA{q}", [128, TT], F32) for q in range(4)]
    psW = pst("psW", [128, D], F32)
    psT = [pst(f"psT{q}", [128, 1024], BF16) for q in range(2)]

    sm = _sems(nc, ["sC", "sst", "sln", "stp", "szf", "sMD", "sME", "sEO",
                    "sWf", "sYT", "sRes", "sVr", "sSd", "sLNz", "sTz", "sFM",
                    "sLG", "sGpre", "sLGe", "sTg", "sGT", "sGV", "sGA"])
    sm["sCW"] = S(nc, pfx + "sCW", persist=True)
    sx2 = [S(nc, f"sx{q}") for q in range(2)]
    VC = Chain(nc.vector, S(nc, "chV"))
    SC = Chain(nc.scalar, S(nc, "chS"))

    with nc.Block() as b0:
        @b0.sync
        def _(sp):
            sm["sC"].inc(sp.dma_start(idb[:], idb_d), 16)
            sm["sC"].inc(sp.dma_start(idf[:], idf_d), 16)
            sm["sC"].inc(sp.dma_start(cls_x[:], x_cls), 16)

        @b0.gpsimd
        def _(g):
            sm["sCW"].inc(g.dma_start(
                wrs[:].rearrange("p (a e) -> p a e", e=8),
                wrd.rearrange("(a p) e -> p a e", p=128)), 16)
            sm["sCW"].inc(g.dma_start(
                ats[:].rearrange("p (a m) -> p a m", m=512),
                atd.rearrange("(a p) m -> p a m", p=128)), 16)
            sm["sCW"].inc(g.dma_start(
                bps[:].rearrange("p (a m) -> p a m", m=D),
                bpd.rearrange("(a p) m -> p a m", p=128)), 16)
            sm["sCW"].inc(g.dma_start(ebs[:], ed), 16)

        @b0.vector
        def _(v_):
            v = VC
            v.op("memset", epsb[:], EPS)
            v.wait_ge(sm["sC"].h, 3 * 16)
            v.wait_ge(sm["sCW"].h, 4 * 16)
            v.op("tensor_tensor", sq[0:2, :], cls_x[:], cls_x[:], ALU.mult)
            v.op("reduce_sum", gsc[:, 0:1], sq[0:2, :], AX)
            v.op("reduce_sum", gsc[:, 1:2], cls_x[:], AX)
            v.op("tensor_scalar_mul", gsc[:, 2:3], gsc[:, 1:2], 1.0 / D)
            v.op("tensor_scalar_mul", gsc[:, 3:4], gsc[:, 0:1], 1.0 / D)
            v.op("tensor_tensor", gsc[:, 4:5], gsc[:, 2:3], gsc[:, 2:3], ALU.mult)
            v.op("tensor_tensor", gsc[:, 5:6], gsc[:, 3:4], gsc[:, 4:5],
                 ALU.subtract, inc=sm["sVr"])
            v.wait_ge(sm["sSd"].h, 1)
            v.op("reciprocal", gsc[:, 7:8], gsc[:, 6:7])
            v.op("tensor_scalar", clsz[:], cls_x[:], gsc[:, 2:3], gsc[:, 7:8],
                 ALU.subtract, ALU.mult, inc=sm["sLNz"])
            v.wait_ge(sm["sTz"].h, 1)
            v.op("tensor_copy", cls_fm[:], psT[0][:, 0:16], inc=sm["sFM"])
            v.wait_ge(sm["sLG"].h, 1)
            v.op("tensor_copy", lg[:], psA[0][0:2, 0:8])
            v.op("reduce_max", gsc[:, 8:9], lg[:], AX, negate=True)
            v.op("tensor_scalar", gs2[:, 0:8], lg[:], gsc[:, 8:9], None,
                 ALU.add)
            v.op("tensor_single_scalar", gs2[:, 8:16], gs2[:, 0:8], 0.0, ALU.is_ge)
            v.op("scalar_tensor_tensor", gs2[:, 16:24], gs2[:, 8:16], -1e30,
                 gs2[:, 0:8], ALU.mult, ALU.add)
            v.op("reduce_max", gsc[:, 9:10], gs2[:, 16:24], AX, negate=True)
            v.op("tensor_scalar_mul", gsc[:, 10:11], gsc[:, 9:10], -1.0)
            v.op("tensor_single_scalar", gs2[:, 24:32], gs2[:, 0:8],
                 gsc[:, 10:11], ALU.is_ge, inc=sm["sGpre"])
            v.wait_ge(sm["sSd"].h, 2)
            v.op("tensor_tensor", gs2[:, 32:40], gs2[:, 32:40], gs2[:, 24:32],
                 ALU.mult)
            v.op("reduce_sum", gsc[:, 11:12], gs2[:, 32:40], AX)
            v.op("reciprocal", gsc[:, 12:13], gsc[:, 11:12])
            v.op("tensor_scalar_mul", gb[:], gs2[:, 32:40], gsc[:, 12:13],
                 inc=sm["sLGe"])
            v.wait_ge(sm["sTg"].h, 1)
            v.op("tensor_copy", gts[:], psT[0][0:8, 16:18], inc=sm["sGT"])
            v.wait_ge(sm["sGV"].h, 4)
            v.op("tensor_copy", g_all[:], psA[1][:, 0:8], inc=sm["sGA"])

        @b0.scalar
        def _(s_):
            s = SC
            s.wait_ge(sm["sVr"].h, 1)
            s.op("activation", gsc[:, 6:7], gsc[:, 5:6], AF.Sqrt,
                 bias=epsb[0:2, 0:1], inc=sm["sSd"])
            s.wait_ge(sm["sGpre"].h, 1)
            s.op("activation", gs2[:, 32:40], gs2[:, 0:8], AF.Exp, inc=sm["sSd"])

        @b0.tensor
        def _(t):
            t.wait_ge(sm["sC"].h, 3 * 16)
            t.wait_ge(sm["sCW"].h, 4 * 16)
            t.wait_ge(sm["sLNz"].h, 1)
            for c in range(8):
                tr = t.transpose(psT[0][:, c * 2:(c + 1) * 2],
                                 clsz[:, c * 128:(c + 1) * 128], idb[0:2, 0:2])
            sm["sTz"].inc(tr)
            t.wait_ge(sm["sFM"].h, 1)
            for kc in range(8):
                mm = t.matmul(psA[0][0:2, 0:8], cls_fm[:, kc * 2:(kc + 1) * 2],
                              wrs[:, kc * 8:(kc + 1) * 8],
                              start=(kc == 0), stop=(kc == 7))
            sm["sLG"].inc(mm)
            t.wait_ge(sm["sLGe"].h, 1)
            tr = t.transpose(psT[0][0:8, 16:18], gb[:], idb[0:2, 0:2])
            sm["sTg"].inc(tr)
            t.wait_ge(sm["sGT"].h, 1)
            for p in range(4):
                mm = t.matmul(psA[1][:, p * 2:(p + 1) * 2],
                              ebs[:, p * 128:(p + 1) * 128], gts[:],
                              start=True, stop=True)
                sm["sGV"].inc(mm)

    with nc.Block() as b:
        for i in range(NTILES):
            bqi = i // 4

            @b.sync
            def _(sp, i=i):
                if i >= 2:
                    sp.wait_ge(sm["sRes"].h, 4 * (i - 1))
                sx2[i % 2].inc(sp.dma_start(
                    xt[i % 2][:].rearrange("p (g f) -> p g f", f=D), x_in_t[i]), 16)
                sp.wait_ge(sm["sRes"].h, 4 * (i + 1))
                sm["sst"].inc(sp.dma_start(
                    x_out_t[i], xo[:].rearrange("p (g f) -> p g f", f=D)), 16)

            @b.vector
            def _(v_, i=i):
                v = VC
                v.wait_ge(sx2[i % 2].h, 16 * (i // 2 + 1))
                for g in range(4):
                    xg = xt[i % 2][:, g * D:(g + 1) * D]
                    v.op("tensor_tensor", sq[:], xg, xg, ALU.mult)
                    v.op("reduce_sum", sumsq[:, g:g + 1], sq[:], AX)
                    v.op("reduce_sum", sums[:, g:g + 1], xg, AX)
                v.op("tensor_scalar_mul", mu[:], sums[:], 1.0 / D)
                v.op("tensor_scalar_mul", ex2[:], sumsq[:], 1.0 / D)
                v.op("tensor_tensor", musq[:], mu[:], mu[:], ALU.mult)
                v.op("tensor_tensor", var[:], ex2[:], musq[:], ALU.subtract,
                     inc=sm["sVr"])
                v.wait_ge(sm["sSd"].h, i + 3)
                v.op("reciprocal", istd[:], stdv[:])
                v.op("tensor_scalar_mul", half[:], istd[:], 0.5)
                v.op("tensor_scalar_add", aZ[:], half[:], 1.0)
                v.op("tensor_tensor", bZmu[:], mu[:], half[:], ALU.mult)
                v.wait_ge(sm["stp"].h, 8 * i)
                for g in range(4):
                    v.op("tensor_scalar", zt[:, g * D:(g + 1) * D],
                         xt[i % 2][:, g * D:(g + 1) * D],
                         mu[:, g:g + 1], istd[:, g:g + 1],
                         ALU.subtract, ALU.mult, inc=sm["sln"])
                v.wait_ge(sm["sMD"].h, 4 * i)
                for c in range(8):
                    v.wait_ge(sm["stp"].h, i * 8 + c + 1)
                    v.op("tensor_copy", zf[:, c * TT:(c + 1) * TT],
                         psT[c % 2][:, 0:TT], inc=sm["szf"])
                if i >= 1:
                    v.wait_ge(sm["sst"].h, 16 * i)
                for g in range(4):
                    v.wait_ge(sm["sYT"].h, i * 4 + g + 1)
                    v.op("tensor_scalar", xzb[:], xt[i % 2][:, g * D:(g + 1) * D],
                         aZ[:, g:g + 1], bZmu[:, g:g + 1], ALU.mult, ALU.subtract)
                    v.op("scalar_tensor_tensor", xo[:, g * D:(g + 1) * D],
                         psW[:, 0:D], 0.5, xzb[:], ALU.mult, ALU.add,
                         inc=sm["sRes"])

            @b.scalar
            def _(s_, i=i):
                s = SC
                s.wait_ge(sm["sVr"].h, i + 2)
                s.op("activation", stdv[:], var[:], AF.Sqrt, bias=epsb[:, 0:1],
                     inc=sm["sSd"])
                s.wait_ge(sm["sGA"].h, 1)
                for p in range(4):
                    s.wait_ge(sm["sMD"].h, i * 4 + p + 1)
                    s.op("activation", mid_sb[:, p * TT:(p + 1) * TT],
                         psA[p][:], AF.Copy,
                         scale=g_all[:, (p * 2 + (i // 4)):(p * 2 + (i // 4)) + 1],
                         inc=sm["sME"])
                if i >= 1:
                    s.wait_ge(sm["sYT"].h, 4 * i)
                for m in range(8):
                    s.wait_ge(sm["sEO"].h, i * 8 + m + 1)
                    s.op("activation", wf[:, m * TT:(m + 1) * TT],
                         psA[m % 2][:], AF.Copy, inc=sm["sWf"])

            @b.tensor
            def _(t, i=i):
                if i == 0:
                    t.wait_ge(sm["sGA"].h, 1)
                t.wait_ge(sm["sln"].h, 4 * (i + 1))
                for c in range(8):
                    CB = i * 8 + c
                    if CB >= 2:
                        t.wait_ge(sm["szf"].h, CB - 1)
                    for g in range(4):
                        tr = t.transpose(psT[c % 2][:, g * 128:(g + 1) * 128],
                                         zt[:, g * D + c * 128:g * D + c * 128 + 128],
                                         idb[:])
                    sm["stp"].inc(tr)
                t.wait_ge(sm["szf"].h, 8 * (i + 1))
                if i >= 1:
                    t.wait_ge(sm["sME"].h, 4 * i)
                for p in range(4):
                    for k in range(8):
                        mm = t.matmul(psA[p][:],
                                      ats[:, k * 512 + p * 128:k * 512 + p * 128 + 128],
                                      zf[:, k * TT:(k + 1) * TT],
                                      start=(k == 0), stop=(k == 7))
                    sm["sMD"].inc(mm)
                t.wait_ge(sm["sME"].h, 4 * (i + 1))
                for m in range(8):
                    EOg = i * 8 + m
                    if EOg >= 2:
                        t.wait_ge(sm["sWf"].h, EOg - 1)
                    for p in range(4):
                        mm = t.matmul(psA[m % 2][:],
                                      bps[:, p * D + m * 128:p * D + m * 128 + 128],
                                      mid_sb[:, p * TT:(p + 1) * TT],
                                      start=(p == 0), stop=(p == 3))
                    sm["sEO"].inc(mm)
                t.wait_ge(sm["sWf"].h, 8 * (i + 1))
                for g in range(4):
                    t.wait_ge(sm["sRes"].h, 4 * i + g)
                    for c in range(8):
                        tr = t.transpose(psW[:, c * 128:(c + 1) * 128],
                                         wf[:, c * TT + g * 128:c * TT + g * 128 + 128],
                                         idf[:])
                    sm["sYT"].inc(tr)

    with nc.Block() as bf:
        @bf.sync
        def _(sp):
            sp.wait_ge(sm["sst"].h, 16 * NTILES)

    _st.close()
    _emit_sem_reset(nc)


def build_l1ffn():
    nc = bass.Bass("TRN2", target_bir_lowering=False)
    x_in = nc.dram_tensor("x_in", [TPC, D], F32, kind="ExternalInput").ap()
    x_out = nc.dram_tensor("x_out", [TPC, D], F32, kind="ExternalOutput").ap()
    wd = _declare_io(nc, L1F_KEYS, _SHAPES)
    emit_l1ffn(nc, "m_", x_in, x_out, wd)
    return nc




L1A_KEYS = ["WQT", "WKT", "WVT", "WOT", "Ctab", "Stab", "Mtabb",
            "identb", "identf", "Pswap"]


def emit_l1attn(nc, pfx, x_in, x_out, wd):
    _st = contextlib.ExitStack()
    sb = lambda n, s, d: _st.enter_context(nc.sbuf_tensor(pfx + n, s, d))
    pst = lambda n, s, d: _st.enter_context(nc.psum_tensor(pfx + n, s, d))
    wqd = wd["WQT"]; wkd = wd["WKT"]; wvd = wd["WVT"]; wod = wd["WOT"]
    cthd = wd["Ctab"]; sthd = wd["Stab"]; mskd = wd["Mtabb"]
    idb_d = wd["identb"]; idf_d = wd["identf"]; psw_d = wd["Pswap"]

    x_in_q = x_in.rearrange("(q p) f -> q p f", p=128)
    x_out_q = x_out.rearrange("(q p) f -> q p f", p=128)
    r_w = {"K": wkd.rearrange("(a p) m -> p a m", p=128),
           "Q": wqd.rearrange("(a p) m -> p a m", p=128),
           "V": wvd.rearrange("(a p) m -> p a m", p=128)}
    r_wo = wod.rearrange("(a p) m -> p a m", p=128)

    Kf = sb("Kf", [128, 8 * SEQ], BF16)
    Qf = sb("Qf", [128, 8 * SEQ], BF16)
    VT = sb("VT", [128, 16 * D], BF16)
    wo_sb = sb("wo_sb", [128, 8 * D], BF16)
    idb = sb("idb", [128, 128], BF16)
    idf = sb("idf", [128, 128], F32)
    psw = sb("psw", [128, 128], BF16)
    Ct = sb("Ct", [128, SEQ], F32)
    St = sb("St", [128, SEQ], F32)
    epsb = sb("epsb", [128, 1], F32)
    q8b = sb("q8b", [128, 1], F32)
    stt1 = sb("stt1", [128, 8], F32)

    sm = _sems(nc, ["sC", "sln", "stp", "szf", "sPj", "sStg", "sSw",
                    "sDn", "sVr", "sSd", "sSmm", "sSe", "sNg", "sEx", "sPn",
                    "sPTb", "sPTe", "sAVa", "sAVe", "sWOc", "sYe", "sYTb",
                    "sResB", "sMtl", "sRH"])
    sm["sCW"] = S(nc, pfx + "sCW", persist=True)
    sxg = [S(nc, f"sxg{q}") for q in range(2)]
    sxq = [S(nc, f"sxq{q}") for q in range(2)]
    sstB = [S(nc, f"sstB{q}") for q in range(2)]
    swv = [S(nc, pfx + f"swv{q}", persist=True) for q in range(2)]
    VCH = Chain(nc.vector, S(nc, "chV"))
    SCH = Chain(nc.scalar, S(nc, "chS"))

    with nc.Block() as b0:
        @b0.sync
        def _(sp):
            sm["sC"].inc(sp.dma_start(idb[:], idb_d), 16)
            sm["sC"].inc(sp.dma_start(idf[:], idf_d), 16)
            sm["sC"].inc(sp.dma_start(psw[:], psw_d), 16)
            sm["sC"].inc(sp.dma_start(Ct[:], cthd), 16)
            sm["sC"].inc(sp.dma_start(St[:], sthd), 16)

        @b0.gpsimd
        def _(g):
            sm["sCW"].inc(g.dma_start(
                wo_sb[:].rearrange("p (a m) -> p a m", m=D), r_wo), 16)

        @b0.vector
        def _(v_):
            VCH.op("memset", epsb[:], EPS)
            VCH.op("memset", q8b[:], 0.125)

    for bq in range(2):
        with (
            nc.sbuf_tensor(f"{pfx}ztA{bq}", [128, 2 * D], BF16) as ztA,
            nc.sbuf_tensor(f"{pfx}zfA{bq}", [128, 8 * SEQ], BF16) as zf,
            nc.sbuf_tensor(f"{pfx}xtA{bq}", [128, 2 * D], F32) as xtA,
            nc.sbuf_tensor(f"{pfx}wbA{bq}", [128, 2 * 8 * 512], BF16) as wbA,
            nc.sbuf_tensor(f"{pfx}stg{bq}", [128, 2 * 512], BF16) as stg,
            nc.sbuf_tensor(f"{pfx}v1b{bq}", [128, 512], F32) as v1b,
            nc.sbuf_tensor(f"{pfx}v2b{bq}", [128, 512], F32) as v2b,
            nc.sbuf_tensor(f"{pfx}sqA{bq}", [128, D], F32) as sqA,
            nc.psum_tensor(f"{pfx}psA0{bq}", [128, 512], F32) as psA0,
            nc.psum_tensor(f"{pfx}psA1{bq}", [128, 512], F32) as psA1,
            nc.psum_tensor(f"{pfx}psB0{bq}", [128, 512], F32) as psB0,
            nc.psum_tensor(f"{pfx}psB1{bq}", [128, 512], F32) as psB1,
            nc.psum_tensor(f"{pfx}psTa{bq}", [128, 1024], BF16) as psTa,
            nc.psum_tensor(f"{pfx}psTb{bq}", [128, 1024], BF16) as psTb,
        ):
            psA2 = [psA0, psA1]
            psB2 = [psB0, psB1]
            psT2 = [psTa, psTb]

            def pg_info(pg):
                proj = pg // 32
                mc = (pg % 32) // 4
                nt = pg % 4
                mg = mc // 4
                return proj, mc, nt, mg

            with nc.Block() as b:
                @b.sync
                def _(sp, bq=bq):
                    for g in range(16):
                        gg = bq * 16 + g
                        if gg >= 2:
                            sp.wait_ge(sm["sln"].h, gg - 1)
                        sxg[g % 2].inc(sp.dma_start(
                            xtA[:, (g % 2) * D:(g % 2) * D + D], x_in_q[gg]), 16)

                @b.gpsimd
                def _(g, bq=bq):
                    for si in range(6):
                        SI = bq * 6 + si
                        proj = "KQV"[si // 2]
                        mg = si % 2
                        if SI >= 2:
                            g.wait_ge(sm["sPj"].h, (SI - 1) * 16)
                        swv[SI % 2].inc(g.dma_start(
                            wbA[:, (SI % 2) * 4096:(SI % 2) * 4096 + 4096]
                            .rearrange("p (a m) -> p a m", m=512),
                            r_w[proj][:, :, mg * 512:(mg + 1) * 512]), 16)

                @b.vector
                def _(v_, bq=bq):
                    v = VCH
                    for g in range(16):
                        gg = bq * 16 + g
                        v.wait_ge(sxg[g % 2].h, 16 * (gg // 2 + 1))
                        xg = xtA[:, (g % 2) * D:(g % 2) * D + D]
                        v.op("tensor_tensor", sqA[:], xg, xg, ALU.mult)
                        v.op("reduce_sum", stt1[:, 0:1], sqA[:], AX)
                        v.op("reduce_sum", stt1[:, 1:2], xg, AX)
                        v.op("tensor_scalar_mul", stt1[:, 2:3], stt1[:, 1:2], 1.0 / D)
                        v.op("tensor_scalar_mul", stt1[:, 3:4], stt1[:, 0:1], 1.0 / D)
                        v.op("tensor_tensor", stt1[:, 4:5], stt1[:, 2:3],
                             stt1[:, 2:3], ALU.mult)
                        v.op("tensor_tensor", stt1[:, 5:6], stt1[:, 3:4],
                             stt1[:, 4:5], ALU.subtract, inc=sm["sVr"])
                        v.wait_ge(sm["sSd"].h, gg + 1)
                        v.op("reciprocal", stt1[:, 7:8], stt1[:, 6:7])
                        v.op("tensor_scalar", ztA[:, (g % 2) * D:(g % 2) * D + D],
                             xg, stt1[:, 2:3], stt1[:, 7:8],
                             ALU.subtract, ALU.mult, inc=sm["sln"])
                        v.wait_ge(sm["stp"].h, gg + 1)
                        v.op("tensor_copy",
                             zf[:].rearrange("p (c t) -> p c t", t=SEQ)[:, :, g * 128:(g + 1) * 128],
                             psT2[g % 2][:, 0:1024].rearrange("p (c t) -> p c t", t=128),
                             inc=sm["szf"])
                    for pg in range(96):
                        G = bq * 96 + pg
                        proj, mc, nt, mg = pg_info(pg)
                        v.wait_ge(sm["sSw"].h, G + 1)
                        if proj < 2:
                            v.op("tensor_tensor", v1b[:],
                                 stg[:, (pg % 2) * 512:(pg % 2) * 512 + 512],
                                 Ct[:, nt * 512:(nt + 1) * 512], ALU.mult)
                            v.op("tensor_tensor", v2b[:], psB2[pg % 2][:],
                                 St[:, nt * 512:(nt + 1) * 512], ALU.mult)
                            obuf = Kf if proj == 0 else Qf
                            v.op("tensor_tensor",
                                 obuf[:, mc * SEQ + nt * 512:mc * SEQ + nt * 512 + 512],
                                 v1b[:], v2b[:], ALU.add, inc=sm["sDn"])
                        else:
                            v.op("tensor_copy",
                                 VT[:].rearrange("p (kt d) -> p kt d", d=D)
                                 [:, nt * 4:(nt + 1) * 4, mc * 128:(mc + 1) * 128],
                                 psT2[pg % 2][:, 0:512].rearrange("p (s t) -> p s t", t=128),
                                 inc=sm["sDn"])

                @b.scalar
                def _(s_, bq=bq):
                    s = SCH
                    for g in range(16):
                        gg = bq * 16 + g
                        s.wait_ge(sm["sVr"].h, gg + 1)
                        s.op("activation", stt1[:, 6:7], stt1[:, 5:6], AF.Sqrt,
                             bias=epsb[:, 0:1], inc=sm["sSd"])
                    for pg in range(96):
                        G = bq * 96 + pg
                        proj = pg // 32
                        s.wait_ge(sm["sPj"].h, G + 1)
                        if G >= 2:
                            s.wait_ge(sm["sDn"].h, G - 1)
                            s.wait_ge(sm["sSw"].h, G - 1)
                        if proj == 1:
                            s.op("activation", stg[:, (pg % 2) * 512:(pg % 2) * 512 + 512],
                                 psA2[pg % 2][:], AF.Copy, scale=q8b[:, 0:1],
                                 inc=sm["sStg"])
                        else:
                            s.op("activation", stg[:, (pg % 2) * 512:(pg % 2) * 512 + 512],
                                 psA2[pg % 2][:], AF.Copy, inc=sm["sStg"])

                @b.tensor
                def _(t, bq=bq):
                    if bq == 0:
                        t.wait_ge(sm["sC"].h, 5 * 16)
                        t.wait_ge(sm["sCW"].h, 16)

                    def emit_post(item):
                        G, pg = item
                        proj = pg // 32
                        t.wait_ge(sm["sStg"].h, G + 1)
                        if G >= 2:
                            t.wait_ge(sm["sDn"].h, G - 1)
                        if proj < 2:
                            mm = t.matmul(psB2[pg % 2][:], psw[:],
                                          stg[:, (pg % 2) * 512:(pg % 2) * 512 + 512],
                                          start=True, stop=True)
                            sm["sSw"].inc(mm)
                        else:
                            for sblk in range(4):
                                tr = t.transpose(
                                    psT2[pg % 2][:, sblk * 128:(sblk + 1) * 128],
                                    stg[:, (pg % 2) * 512 + sblk * 128:(pg % 2) * 512 + sblk * 128 + 128],
                                    idb[:])
                            sm["sSw"].inc(tr)

                    for g in range(16):
                        gg = bq * 16 + g
                        t.wait_ge(sm["sln"].h, gg + 1)
                        if gg >= 2:
                            t.wait_ge(sm["szf"].h, gg - 1)
                        for c in range(8):
                            tr = t.transpose(psT2[g % 2][:, c * 128:(c + 1) * 128],
                                             ztA[:, (g % 2) * D + c * 128:(g % 2) * D + c * 128 + 128],
                                             idb[:])
                        sm["stp"].inc(tr)
                    t.wait_ge(sm["szf"].h, 16 * (bq + 1))
                    pend = None
                    for pg in range(96):
                        G = bq * 96 + pg
                        proj, mc, nt, mg = pg_info(pg)
                        SI = bq * 6 + proj * 2 + mg
                        if pg % 16 == 0:
                            t.wait_ge(swv[SI % 2].h, 16 * (SI // 2 + 1))
                        if G >= 2:
                            t.wait_ge(sm["sStg"].h, G - 1)
                        mcl = mc % 4
                        wofs = (SI % 2) * 4096
                        for k in range(8):
                            mm = t.matmul(psA2[pg % 2][:],
                                          wbA[:, wofs + k * 512 + mcl * 128:wofs + k * 512 + mcl * 128 + 128],
                                          zf[:, k * SEQ + nt * 512:k * SEQ + nt * 512 + 512],
                                          start=(k == 0), stop=(k == 7))
                        sm["sPj"].inc(mm)
                        if pend is not None:
                            emit_post(pend)
                        pend = (G, pg)
                    emit_post(pend)

        with (
            nc.sbuf_tensor(f"{pfx}mskb{bq}", [128, 2048], BF16) as mskb,
            nc.sbuf_tensor(f"{pfx}Psb{bq}", [128, 2 * SEQ], BF16) as Psb,
            nc.sbuf_tensor(f"{pfx}PTsb{bq}", [128, 2 * SEQ], BF16) as PTsb,
            nc.sbuf_tensor(f"{pfx}xqb{bq}", [128, 2 * D], F32) as xqb,
            nc.sbuf_tensor(f"{pfx}xob{bq}", [128, 2 * D], F32) as xob,
            nc.sbuf_tensor(f"{pfx}avsb{bq}", [128, 2 * 8 * 128], BF16) as avsb,
            nc.sbuf_tensor(f"{pfx}zhb{bq}", [128, 8], F32) as zhb,
            nc.sbuf_tensor(f"{pfx}ngb{bq}", [128, 4], F32) as ngb,
            nc.psum_tensor(f"{pfx}psS0{bq}", [128, 512], F32) as psS0,
            nc.psum_tensor(f"{pfx}psS1{bq}", [128, 512], F32) as psS1,
            nc.psum_tensor(f"{pfx}psPTa{bq}", [128, 1024], BF16) as psPTa,
            nc.psum_tensor(f"{pfx}psPTb{bq}", [128, 1024], BF16) as psPTb,
            nc.psum_tensor(f"{pfx}psAVa{bq}", [128, 128], F32) as psAVa,
            nc.psum_tensor(f"{pfx}psAVb{bq}", [128, 128], F32) as psAVb,
            nc.psum_tensor(f"{pfx}psWO{bq}", [128, 1024], F32) as psWO,
        ):
            psS2 = [psS0, psS1]
            psPT2 = [psPTa, psPTb]
            psAV2 = [psAVa, psAVb]
            c0 = {"ch": sm["sSmm"].n, "h": sm["sPn"].n, "bat": sm["sPTb"].n,
                  "hp": sm["sAVe"].n, "j": sm["sYe"].n, "av": sm["sAVa"].n,
                  "res": sm["sResB"].n, "ytb": sm["sYTb"].n, "woc": sm["sWOc"].n,
                  "rh": sm["sRH"].n}
            jorder = []
            for k in range(8):
                jorder += [k, 15 - k]
            rank = {j: r for r, j in enumerate(jorder)}
            HL = []
            for k in range(8):
                for h in range(16):
                    HL.append((k, h))
                    HL.append((15 - k, h))

            def nch_of(j):
                return j // 4 + 1

            def nb8(j):
                return (j + 8) // 8

            chA = []; batA = []; oddA = []
            c_ch = c0["ch"]; c_bat = c0["bat"]; c_odd = 0
            for (j, h) in HL:
                chA.append(c_ch); batA.append(c_bat); oddA.append(c_odd)
                c_ch += nch_of(j); c_bat += nb8(j)
                if h % 2 == 1:
                    c_odd += 1
            av_done = {}
            c_odd = 0
            for idx, (j, h) in enumerate(HL):
                if h % 2 == 1:
                    c_odd += 1
                    if h == 15:
                        av_done[j] = c_odd

            with nc.Block() as b:
                @b.sync
                def _(sp, bq=bq):
                    sm["sMtl"].inc(sp.dma_start(mskb[:], mskd), 16)
                    for r, j in enumerate(jorder):
                        gj = bq * 16 + j
                        if r >= 2:
                            sp.wait_ge(sm["sResB"].h, c0["res"] + r - 1)
                        sxq[r % 2].inc(sp.dma_start(
                            xqb[:, (r % 2) * D:(r % 2) * D + D], x_in_q[gj]), 16)
                        sp.wait_ge(sm["sResB"].h, c0["res"] + r + 1)
                        sstB[r % 2].inc(sp.dma_start(
                            x_out_q[gj], xob[:, (r % 2) * D:(r % 2) * D + D]), 16)
                    sp.wait_ge(sstB[0].h, 16 * 8 * (bq + 1))
                    sp.wait_ge(sstB[1].h, 16 * 8 * (bq + 1))

                @b.vector
                def _(v_, bq=bq, c0=c0):
                    v = VCH

                    def pt_evac(idx):
                        j, h = HL[idx]
                        par = idx % 2
                        if idx >= 2:
                            v.wait_ge(sm["sAVa"].h, c0["av"] + idx - 1)
                        bat = batA[idx]
                        for bi in range(nb8(j)):
                            bat += 1
                            wl = 128 * min(8, j + 1 - bi * 8)
                            v.wait_ge(sm["sPTb"].h, bat)
                            v.op("tensor_copy",
                                 PTsb[:, par * SEQ + bi * 1024:par * SEQ + bi * 1024 + wl],
                                 psPT2[par][:, 0:wl], inc=sm["sPTe"])

                    def zdmat(idx):
                        j, h = HL[idx]
                        par = idx % 2
                        nch = nch_of(j)
                        wS = 512 * nch
                        v.wait_ge(sm["sSe"].h, chA[idx] + nch)
                        if nch > 1:
                            v.op("reduce_sum", ngb[:, par:par + 1],
                                 zhb[:, par * 4:par * 4 + nch], AX)
                            v.op("reciprocal", ngb[:, 2 + par:3 + par],
                                 ngb[:, par:par + 1])
                        else:
                            v.op("reciprocal", ngb[:, 2 + par:3 + par],
                                 zhb[:, par * 4:par * 4 + 1])
                        wlast = 512 * (nch - 1) + 128 * (j % 4 + 1)
                        v.op("tensor_scalar_mul",
                             Psb[:, par * SEQ:par * SEQ + wlast],
                             Psb[:, par * SEQ:par * SEQ + wlast],
                             ngb[:, 2 + par:3 + par], inc=sm["sPn"])

                    def resid(j2):
                        r = rank[j2]
                        v.wait_ge(sm["sWOc"].h, c0["woc"] + 2 * (r + 1))
                        if r >= 2:
                            v.wait_ge(sstB[r % 2].h, 16 * 8 * bq + 16 * (r // 2))
                        v.wait_ge(sxq[r % 2].h, 16 * 8 * bq + 16 * (r // 2 + 1))
                        v.op("tensor_tensor", xob[:, (r % 2) * D:(r % 2) * D + D],
                             psWO[:, 0:D], xqb[:, (r % 2) * D:(r % 2) * D + D],
                             ALU.add, inc=sm["sResB"])

                    for idx in range(len(HL)):
                        if idx >= 2:
                            pt_evac(idx - 2)
                        zdmat(idx)
                        if idx >= 3 and HL[idx - 3][1] == 15:
                            resid(HL[idx - 3][0])
                    n = len(HL)
                    pt_evac(n - 2)
                    pt_evac(n - 1)
                    for idx in (n - 3, n - 2, n - 1):
                        if HL[idx][1] == 15:
                            resid(HL[idx][0])

                @b.scalar
                def _(s_, bq=bq, c0=c0):
                    s = SCH

                    def exp_h(idx):
                        j, h = HL[idx]
                        par = idx % 2
                        nch = nch_of(j)
                        if idx >= 2:
                            s.wait_ge(sm["sPn"].h, c0["h"] + idx - 1)
                            s.wait_ge(sm["sPTb"].h, batA[idx - 1])
                        ch = chA[idx]
                        for kk in range(nch):
                            bank = ch % 2
                            ch += 1
                            wd = 128 * (j % 4 + 1) if kk == nch - 1 else 512
                            s.wait_ge(sm["sSmm"].h, ch)
                            s.op("activation",
                                 Psb[:, par * SEQ + kk * 512:par * SEQ + kk * 512 + wd],
                                 psS2[bank][:, 0:wd], AF.Exp,
                                 accum_out=zhb[:, par * 4 + kk:par * 4 + kk + 1],
                                 inc=sm["sSe"])

                    def post(idx):
                        j, h = HL[idx]
                        r = rank[j]
                        if h % 2 == 1:
                            hp = h // 2
                            s.wait_ge(sm["sAVa"].h, c0["av"] + idx + 1)
                            if r >= 2 and hp == 0:
                                s.wait_ge(sm["sWOc"].h, c0["woc"] + 2 * (r - 1))
                            ofs = (r % 2) * 1024 + hp * 128
                            s.op("activation", avsb[:, ofs:ofs + 128],
                                 psAV2[idx % 2][:], AF.Copy, inc=sm["sAVe"])


                    for idx in range(len(HL)):
                        exp_h(idx)
                        if idx >= 3:
                            post(idx - 3)
                    n = len(HL)
                    post(n - 3)
                    post(n - 2)
                    post(n - 1)

                @b.tensor
                def _(t, bq=bq, c0=c0):
                    t.wait_ge(sm["sMtl"].h, 16 * (bq + 1))

                    def s_mms(idx):
                        j, h = HL[idx]
                        qsl = Qf[64 * (h % 2):64 * (h % 2) + 64,
                                 (h // 2) * SEQ + j * 128:(h // 2) * SEQ + j * 128 + 128]
                        ch = chA[idx]
                        for kk in range(nch_of(j)):
                            bank = ch % 2
                            ch += 1
                            if ch > 2:
                                t.wait_ge(sm["sSe"].h, ch - 2)
                            diag = (kk == j // 4)
                            wd = 128 * (j % 4 + 1) if diag else 512
                            mm = t.matmul(psS2[bank][:, 0:wd], qsl,
                                          Kf[64 * (h % 2):64 * (h % 2) + 64,
                                             (h // 2) * SEQ + kk * 512:(h // 2) * SEQ + kk * 512 + wd],
                                          start=True, stop=not diag)
                            if diag:
                                mm = t.matmul(psS2[bank][:, 0:wd], idb[:],
                                              mskb[:, (j % 4) * 512:(j % 4) * 512 + wd],
                                              start=False, stop=True)
                            sm["sSmm"].inc(mm)

                    def pt_part(idx):
                        j, h = HL[idx]
                        par = idx % 2
                        t.wait_ge(sm["sPn"].h, c0["h"] + idx + 1)
                        bat = batA[idx]
                        for bi in range(nb8(j)):
                            bat += 1
                            if bi > 0:
                                t.wait_ge(sm["sPTe"].h, bat - 1)
                            elif idx >= 2:
                                t.wait_ge(sm["sPTe"].h, batA[idx - 1])
                            for s8 in range(8):
                                kt = bi * 8 + s8
                                if kt > j:
                                    break
                                tr = t.transpose(psPT2[par][:, s8 * 128:(s8 + 1) * 128],
                                                 Psb[:, par * SEQ + kt * 128:par * SEQ + kt * 128 + 128],
                                                 idb[:])
                            sm["sPTb"].inc(tr)

                    def av_part(idx):
                        j, h = HL[idx]
                        par = idx % 2
                        hp, hs = h // 2, h % 2
                        t.wait_ge(sm["sPTe"].h, batA[idx] + nb8(j))
                        if hs == 0:
                            t.wait_ge(sm["sAVe"].h, c0["hp"] + oddA[idx])
                        for kt in range(j + 1):
                            mm = t.matmul(psAV2[idx % 2][64 * hs:64 * hs + 64, :],
                                          VT[:, kt * D + hp * 128 + hs * 64:kt * D + hp * 128 + hs * 64 + 64],
                                          PTsb[:, par * SEQ + kt * 128:par * SEQ + kt * 128 + 128],
                                          start=(kt == 0), stop=(kt == j),
                                          tile_position=(0, 64 * hs))
                        sm["sAVa"].inc(mm)

                    def wo_yt(j):
                        r = rank[j]
                        t.wait_ge(sm["sAVe"].h, c0["hp"] + av_done[j])
                        if r >= 1:
                            t.wait_ge(sm["sResB"].h, c0["res"] + r)
                        for m in range(2):
                            for hp in range(8):
                                mm = t.matmul(psWO[:, m * 512:(m + 1) * 512],
                                              avsb[:, (r % 2) * 1024 + hp * 128:(r % 2) * 1024 + hp * 128 + 128],
                                              wo_sb[:, hp * D + m * 512:hp * D + m * 512 + 512],
                                              start=(hp == 0), stop=(hp == 7))
                            sm["sWOc"].inc(mm)

                    for idx in range(len(HL)):
                        if idx >= 2:
                            pt_part(idx - 2)
                        if idx >= 3:
                            av_part(idx - 3)
                        s_mms(idx)
                        if idx >= 3 and HL[idx - 3][1] == 15:
                            wo_yt(HL[idx - 3][0])
                    n = len(HL)
                    pt_part(n - 2)
                    av_part(n - 3)
                    pt_part(n - 1)
                    av_part(n - 2)
                    wo_yt(HL[n - 2][0])
                    av_part(n - 1)
                    wo_yt(HL[n - 1][0])

    with nc.Block() as bf:
        @bf.sync
        def _(sp):
            sp.wait_ge(sstB[0].h, 16 * 16)
            sp.wait_ge(sstB[1].h, 16 * 16)

    _st.close()
    _emit_sem_reset(nc)


def build_l1attn():
    nc = bass.Bass("TRN2", target_bir_lowering=False)
    x_in = nc.dram_tensor("x_in", [TPC, D], F32, kind="ExternalInput").ap()
    x_out = nc.dram_tensor("x_out", [TPC, D], F32, kind="ExternalOutput").ap()
    wd = _declare_io(nc, L1A_KEYS, _SHAPES)
    emit_l1attn(nc, "t_", x_in, x_out, wd)
    return nc


FUSE_PARTS = [("ffn0", "f1_"), ("l0attn", "a1_"), ("ffn0", "f2_"),
              ("l1ffn", "m1_"), ("l1attn", "t1_"), ("l1ffn", "m2_")]
_EMITTERS = {"ffn0": None, "l0attn": None, "l1ffn": None, "l1attn": None}


def build_fused(parts=None):
    if parts is None:
        parts = FUSE_PARTS
    emitters = {"ffn0": emit_ffn0, "l0attn": emit_l0attn,
                "l1ffn": emit_l1ffn, "l1attn": emit_l1attn}
    nc = bass.Bass("TRN2", target_bir_lowering=False)
    x_in = nc.dram_tensor("x_in", [TPC, D], F32, kind="ExternalInput").ap()
    x_out = nc.dram_tensor("x_out", [TPC, D], F32, kind="ExternalOutput").ap()
    wd = _declare_io(nc, sorted(set(FFN0_KEYS + L0_KEYS + L1F_KEYS + L1A_KEYS)),
                     _SHAPES)
    wscr = {k: nc.dram_tensor(k + "_b16", _SHAPES[k][0], BF16).ap()
            for k in ("w1T", "w2T", "h1T", "h2T", "w3T")}
    cur = x_in
    seen_ffn = False
    for i, (name, pfx) in enumerate(parts):
        dst = (x_out if i == len(parts) - 1
               else nc.dram_tensor(f"xs{i}", [TPC, D], F32).ap())
        if name == "ffn0":
            emitters[name](nc, pfx, cur, dst, wd, wscr=wscr, first=not seen_ffn)
            seen_ffn = True
        else:
            emitters[name](nc, pfx, cur, dst, wd)
        cur = dst
    return nc



def _stage_common(inp):
    st = {}
    f32 = np.float32

    def tr(a):
        return np.ascontiguousarray(np.asarray(a, f32).T)

    st["w1T"] = tr(inp["L0_ffn_w1_w"])
    st["w2T"] = tr(inp["L0_ffn_w2_w"])
    st["h1T"] = tr(inp["L0_ffn_h1_w"])
    st["h2T"] = tr(inp["L0_ffn_h2_w"])
    st["w3T"] = tr(inp["L0_ffn_w3_w"])
    for nm, key, nchunk in (("s1c", "L0_ffn_w1_s", 32), ("s2c", "L0_ffn_w2_s", 32),
                            ("sh1c", "L0_ffn_h1_s", 32), ("sh2c", "L0_ffn_h2_s", 32),
                            ("s3c", "L0_ffn_w3_s", 8)):
        st[nm] = np.ascontiguousarray(
            np.asarray(inp[key], f32).reshape(nchunk, 128).T)
    st["identb"] = np.eye(128, dtype=BF)
    st["identf"] = np.eye(128, dtype=f32)
    st["WthT"] = tr(inp["L0_attn_W_theta"])
    st["WphT"] = tr(inp["L0_attn_W_phi"])
    st["WgT"] = tr(inp["L0_attn_W_g"])
    A = np.asarray(inp["L1_ffn_A"], f32)
    Bw = np.asarray(inp["L1_ffn_Bw"], f32)
    st["WrT"] = tr(inp["L1_ffn_Wr"])
    st["ATall"] = np.ascontiguousarray(
        np.concatenate([np.concatenate([A[2 * p].T, A[2 * p + 1].T], axis=1)
                        for p in range(4)], axis=1))
    st["Bpk"] = np.ascontiguousarray(
        np.concatenate([np.concatenate([Bw[2 * p].T, Bw[2 * p + 1].T], axis=0)
                        for p in range(4)], axis=0))
    E = np.zeros((8, 512), np.float32)
    for p in range(4):
        for r in range(128):
            E[2 * p + r // 64, p * 128 + r] = 1.0
    st["Eall"] = E
    st["WQT"] = tr(inp["L1_attn_W_Q"])
    st["WKT"] = tr(inp["L1_attn_W_K"])
    st["WVT"] = tr(inp["L1_attn_W_V"])
    st["WOT"] = tr(inp["L1_attn_W_O"])
    kk = np.arange(32, dtype=f32)
    invf = np.power(np.float32(10000.0), (np.float32(-2.0) * kk / np.float32(64.0)),
                    dtype=f32)
    pos = np.arange(SEQ, dtype=f32)
    ang = pos[None, :] * invf[:, None]
    cosv = np.cos(ang).astype(f32)
    sinv = np.sin(ang).astype(f32)
    C = np.empty((128, SEQ), f32)
    Sg = np.empty((128, SEQ), f32)
    for r in range(128):
        d = r % 64
        k = d // 2
        C[r] = cosv[k]
        Sg[r] = -sinv[k] if d % 2 == 0 else sinv[k]
    st["Ctab"] = C
    st["Stab"] = Sg
    M = np.zeros((128, 2048), f32)
    r_ = np.arange(128)[:, None]
    for jj in range(4):
        u = np.arange(512)[None, :]
        M[:, jj * 512:(jj + 1) * 512] = np.where(u <= 128 * jj + r_, 0.0, -1e30)
    st["Mtab"] = M
    st["Mtabb"] = M.astype(BF)
    P = np.zeros((128, 128), f32)
    for i in range(128):
        P[i, i ^ 1] = 1.0
    st["Pswap"] = P.astype(BF)
    return st


def _run(nc, in_maps, out_key):
    res = run_bass_kernel_spmd(nc, in_maps, core_ids=list(range(NCORES)))
    return [np.asarray(res.results[c][out_key]) for c in range(NCORES)]


_CACHE = {}


def _get(name, builder):
    if name not in _CACHE:
        _CACHE[name] = builder()
    return _CACHE[name]


_KEYS = {
    "ffn0": ["w1T", "w2T", "h1T", "h2T", "w3T", "s1c", "s2c", "sh1c", "sh2c",
             "s3c", "identb", "identf"],
    "l0attn": ["WthT", "WphT", "WgT", "identb"],
    "l1ffn": ["WrT", "ATall", "Bpk", "Eall", "identb", "identf"],
    "l1attn": ["WQT", "WKT", "WVT", "WOT", "Ctab", "Stab", "Mtabb",
               "identb", "identf", "Pswap"],
}
_BUILDERS = {"ffn0": build_ffn0, "l0attn": build_l0attn,
             "l1ffn": build_l1ffn, "l1attn": build_l1attn}


def kernel(**inp):
    st = _stage_common(inp)
    x = np.asarray(inp["L0_attn_x"], np.float32).reshape(NCORES, TPC, D)
    shards = [np.ascontiguousarray(x[c]) for c in range(NCORES)]

    nc = _get("fused", build_fused)
    allk = sorted(set(_KEYS["ffn0"] + _KEYS["l0attn"] + _KEYS["l1ffn"] +
                      _KEYS["l1attn"]))
    maps = [dict({k: st[k] for k in allk}, x_in=shards[c])
            for c in range(NCORES)]
    shards = _run(nc, maps, "x_out")
    out = np.stack(shards).reshape(16, SEQ, D).astype(np.float32)
    return out
